# version 07
# baseline (speedup 1.0000x reference)
import numpy as np
from contextlib import ExitStack

import concourse.bass as bass
import concourse.mybir as mybir
from concourse.bass_utils import run_bass_kernel_spmd
from bass_rust import ReduceOp

F32 = mybir.dt.float32
BF16 = mybir.dt.bfloat16
I8 = mybir.dt.int8
AF = mybir.ActivationFunctionType
ALU = mybir.AluOpType
AX = mybir.AxisListType

B = 2
S = 4096
D = 2048
NTOK = 1024
DL0 = 512
FF = 8192
ROPE_THETA = 10000.0

C0_A = np.float32(1.0 / np.sqrt(np.float64(128.0)))
C0_B = np.float32((1.0 / np.sqrt(np.float64(128.0))) / np.float64(C0_A))
CH2048 = np.float32(1.0 / np.sqrt(np.float64(2048.0)))
CH8192 = np.float32(1.0 / np.sqrt(np.float64(8192.0)))
INV254 = np.float32(np.float64(1.0) / np.float64(254.0))
INV2_24 = np.float32(2.0 ** -24)

ENGS = ("te", "ve", "se", "ge", "sy")


class Ctx:

    def __init__(self, nc):
        self.nc = nc
        self.q = {e: [] for e in ENGS}
        self.sems = {}
        self.val = {}
        self.wm = {e: {} for e in ENGS}
        self.bufw = {}
        self.bufr = {}

    def add_sem(self, es, name):
        self.sems[name] = es.enter_context(self.nc.semaphore(name))
        self.val[name] = 0

    def wait(self, e, sname, v):
        if v is None or v <= 0:
            return
        if self.wm[e].get(sname, 0) >= v:
            return
        self.wm[e][sname] = v
        s = self.sems[sname]
        self.q[e].append(lambda eng, s=s, v=v: eng.wait_ge(s, v))

    def waits(self, e, tickets):
        for t in tickets:
            if t is not None:
                self.wait(e, t[0], t[1])

    def op(self, e, fn, r=(), w=(), wait=(), inc=None, amt=1):
        if hasattr(self, "psum_ids") and self.psum_ids:
            rr = tuple(b for b in r if id(b) not in self.psum_ids)
            w = tuple(w) + tuple(b for b in r if id(b) in self.psum_ids)
            r = rr
        tickets = list(wait)
        for buf in r:
            tickets.append(self.bufw.get(id(buf)))
        for buf in w:
            tickets.append(self.bufw.get(id(buf)))
            for sname, v in self.bufr.get(id(buf), {}).items():
                tickets.append((sname, v))
        self.waits(e, tickets)
        tk = None
        if inc is not None:
            self.val[inc] += amt
            s = self.sems[inc]
            if e == "te":
                def run(eng, fn=fn, s=s, amt=amt):
                    fn(eng)
                    eng.maybe_drain_then_inc((s, amt), fusable=True)
                self.q[e].append(run)
            else:
                self.q[e].append(lambda eng, fn=fn, s=s, amt=amt: fn(eng).then_inc(s, amt))
            tk = (inc, self.val[inc])
        else:
            self.q[e].append(lambda eng, fn=fn: fn(eng))
        if tk is not None:
            for buf in r:
                d = self.bufr.setdefault(id(buf), {})
                if d.get(tk[0], 0) < tk[1]:
                    d[tk[0]] = tk[1]
            for buf in w:
                self.bufw[id(buf)] = tk
                self.bufr[id(buf)] = {}
        return tk

    def dma(self, qe, out_ap, in_ap, r=(), w=(), wait=(), accum=False, slow=False):
        kw = {"accum_op": ALU.add} if accum else {}
        if slow:
            kw["allow_slow_non_contiguous"] = True
        chs = self.dchan[qe]
        ch = chs[self.drr[qe] % len(chs)]
        self.drr[qe] += 1
        self.wait(qe, ch, self.val[ch])
        return self.op(
            qe,
            lambda eng, o=out_ap, i=in_ap, kw=kw: eng.dma_start(out=o, in_=i, **kw),
            r=r, w=w, wait=wait, inc=ch, amt=16,
        )

    def barrier(self):
        for e, sname in (("te", "te"), ("ve", "ve"), ("se", "se"), ("ge", "ge")):
            self.op(e, lambda eng: eng.drain(), inc=sname)
        targets = [(s, self.val[s]) for s in self.val]
        for e in ENGS:
            for s, v in targets:
                self.wait(e, s, v)

    def realize(self, block):
        def mk(ops):
            def run(eng):
                for f in ops:
                    f(eng)
            return run
        block.sync(mk(self.q["sy"]))
        block.tensor(mk(self.q["te"]))
        block.vector(mk(self.q["ve"]))
        block.scalar(mk(self.q["se"]))
        block.gpsimd(mk(self.q["ge"]))


def _host_rope(cid):
    cp = cid % 4
    pos = 4.0 * np.arange(NTOK, dtype=np.float64) + cp
    freq = np.float64(ROPE_THETA) ** (-(np.arange(32, dtype=np.float64) * (2.0 / 64.0)))
    ph = pos[:, None] * freq[None, :]
    cos8 = np.tile(np.cos(ph), (1, 8)).astype(np.float32)
    sin8 = np.tile(np.sin(ph), (1, 8)).astype(np.float32)
    return np.ascontiguousarray(cos8), np.ascontiguousarray(sin8)


def _host_mask(cid):
    cp = cid % 4
    i = np.arange(128)[:, None]
    f = np.arange(512)[None, :]
    u = f // 128
    p = f % 128
    kpos = 128 * u + 4 * (p % 32) + p // 32
    qpos = 4 * i + cp
    return np.ascontiguousarray(
        np.where(kpos <= qpos, np.float64(0.0), np.float64(-1e9)).astype(np.float32))


def _host_hadamard():
    h = np.array([[1.0]])
    while h.shape[0] < 128:
        h = np.block([[h, h], [h, -h]])
    return np.ascontiguousarray(h.astype(np.float32))


def _os_early():
    import os
    return os.environ.get("KDUMP")


def build_program(ph=99):
    import os
    kn_heads = int(os.environ.get("KN_HEADS", "16"))
    kn_qb = int(os.environ.get("KN_QB", "8"))
    kn_part = int(os.environ.get("KN_PART", "3"))
    nc = bass.Bass("TRN2", target_bir_lowering=False)

    def EI(name, shape, dt=F32):
        return nc.dram_tensor(name, list(shape), dt, kind="ExternalInput").ap()

    x_own = EI("x_own", [NTOK, D])
    W_q = EI("W_q", [D, D])
    W_dkv = EI("W_dkv", [DL0, D])
    W_upk = EI("W_upk", [16, 128, DL0])
    W_upv = EI("W_upv", [16, 128, DL0])
    W_o = EI("W_o", [D, D])
    Wg0 = EI("Wg0", [FF, D])
    Wu0 = EI("Wu0", [FF, D])
    Wd0 = EI("Wd0", [D, FF])
    wq1 = EI("wq1", [D, D])
    wk1 = EI("wk1", [D, DL0])
    wv1 = EI("wv1", [D, DL0])
    wo1 = EI("wo1", [D, D])
    wg1 = EI("wg1", [D, FF], I8)
    wu1 = EI("wu1", [D, FF], I8)
    wd1 = EI("wd1", [FF, D], I8)
    sg1 = EI("sg1", [FF])
    su1 = EI("su1", [FF])
    sd1 = EI("sd1", [D])
    c_ident = EI("c_ident", [128, 128])
    c_identb = EI("c_identb", [128, 128], BF16)
    c_h128 = EI("c_h128", [128, 128])
    c_mask = EI("c_mask", [128, 512])
    c_cos = EI("c_cos", [NTOK, 256])
    c_sin = EI("c_sin", [NTOK, 256])
    c_ones = EI("c_ones", [1, 128])

    y_out = nc.dram_tensor("y_out", [NTOK, D], F32, kind="ExternalOutput").ap()

    def DT(name, shape, dt=F32):
        return nc.dram_tensor(name, list(shape), dt).ap()

    xres = DT("xres", [NTOK, D])
    qt0 = DT("qt0", [16, 2, 128, 512])
    ot0 = DT("ot0", [16, 8, 128, 128])
    ckv_own = DT("ckv_own", [NTOK, DL0])
    ckv_full = DT("ckv_full", [S, DL0])
    kv_own = DT("kv_own", [NTOK, 2 * DL0])
    kv_full = DT("kv_full", [S, 2 * DL0])
    qt1 = DT("qt1", [16, 8, 128, 128])
    o1b = DT("o1b", [16, 8, 128, 128])
    wgt_b = DT("wgt_b", [64, 128, 16, 128], BF16)
    wut_b = DT("wut_b", [64, 128, 16, 128], BF16)
    wdt_b = DT("wdt_b", [16, 128, 64, 128], BF16)

    esg = ExitStack()
    ctx = Ctx(nc)
    for sname in ("te", "ve", "se", "ge"):
        ctx.add_sem(esg, sname)
    ctx.dchan = {"sy": [], "se": [], "ge": []}
    ctx.drr = {"sy": 0, "se": 0, "ge": 0}
    for qe, nch in (("sy", 4), ("se", 4), ("ge", 2)):
        for i in range(nch):
            nm = f"d{qe}{i}"
            ctx.add_sem(esg, nm)
            ctx.dchan[qe].append(nm)

    def SBg(name, shape, dt=F32):
        return esg.enter_context(nc.sbuf_tensor(name, list(shape), dt))

    ident = SBg("ident", [128, 128])
    maskt = SBg("maskt", [128, 512])
    ones_r = SBg("ones_r", [1, 128])
    eps_t = SBg("eps_t", [128, 1])
    thr_p = [SBg(f"thrp{i}", [128, 1]) for i in range(3)]
    thr_n = [SBg(f"thrn{i}", [128, 1]) for i in range(3)]
    scw = SBg("scw", [128, 4])
    ln_s6 = SBg("ln_s6", [128, 24])
    ln_mv = SBg("ln_mv", [128, 2])
    ln_std = SBg("ln_std", [128, 1])
    ln_rstd = SBg("ln_rstd", [128, 1])
    st_a = SBg("st_a", [128, 4])
    st_b = SBg("st_b", [128, 4])
    st_c = SBg("st_c", [128, 1])
    st_d = SBg("st_d", [128, 1])
    mx_t = SBg("mx_t", [128, 32])
    acc_t = SBg("acc_t", [128, 32])
    bias_t = SBg("bias_t", [128, 1])
    rowt = SBg("rowt", [1, 128])
    rowt2 = SBg("rowt2", [1, 256])
    wsum = SBg("wsum", [128, 224])
    wstat = [SBg(f"wstat{i}", [128, D]) for i in range(2)]

    pb = [esg.enter_context(nc.psum_tensor(f"pb{i}", [128, 512], F32)) for i in range(7)]
    pbT = esg.enter_context(nc.psum_tensor("pbT", [128, 1024], BF16))
    PB = list(pb) + [pbT]
    ctx.psum_ids = {id(b) for b in PB}

    def mmg(bi, out_ap, lhsT_ap, rhs_ap, start, stop, r=(), wait=()):
        return ctx.op("te", lambda eng, o=out_ap, l=lhsT_ap, r_=rhs_ap, st=start, sp=stop:
                      eng.matmul(out=o, lhsT=l, rhs=r_, start=st, stop=sp),
                      r=tuple(r), w=(PB[bi],), wait=wait, inc="te")

    def tr(bi, off, in_ap, r=(), wait=(), idn=None, rows=128):
        ap = pbT[:, off:off + 128] if bi == 7 else pb[bi][:rows, off:off + 128]
        return ctx.op("te", lambda eng, o=ap, i=in_ap, d=(idn if idn is not None else ident):
                      eng.transpose(out=o, in_=i, identity=d[:]),
                      r=tuple(r), w=(PB[bi],), wait=wait, inc="te")

    def ve(fn, r=(), w=(), wait=()):
        return ctx.op("ve", fn, r=r, w=w, wait=wait, inc="ve")

    def se(fn, r=(), w=(), wait=()):
        return ctx.op("se", fn, r=r, w=w, wait=wait, inc="se")

    def ge(fn, r=(), w=(), wait=()):
        return ctx.op("ge", fn, r=r, w=w, wait=wait, inc="ge")

    def emit_ln(htok, xtok, extra_wait=()):
        for qq in range(4):
            ve(lambda eng, qq=qq: eng.bn_stats(out=ln_s6[:, qq * 6:(qq + 1) * 6],
                                               in_=xtok[:, qq * 512:(qq + 1) * 512]),
               r=(xtok,), w=(ln_s6,), wait=extra_wait)
        ve(lambda eng: eng.bn_aggr(out=ln_mv[:], in_=ln_s6[:]), r=(ln_s6,), w=(ln_mv,))
        se(lambda eng: eng.activation(out=ln_std[:], in_=ln_mv[:, 1:2], func=AF.Sqrt,
                                      bias=eps_t[:]), r=(ln_mv, eps_t), w=(ln_std,))
        ve(lambda eng: eng.reciprocal(out=ln_rstd[:], in_=ln_std[:]), r=(ln_std,),
           w=(ln_rstd,))
        return ve(lambda eng: eng.tensor_scalar(out=htok[:], in0=xtok[:],
                                                scalar1=ln_mv[:, 0:1], scalar2=ln_rstd[:],
                                                op0=ALU.subtract, op1=ALU.mult),
                  r=(xtok, ln_mv, ln_rstd), w=(htok,))


    def _finish(stacks, dump=None):
        if dump is not None:
            dv = dump.rearrange("o n p t -> (o n) p t")
            nb_, wid = dv.shape[0], dv.shape[2]
            per = 2048 // wid
            for i in range(nb_):
                ctx.dma("se", y_out[128 * (i // per):128 * (i // per) + 128,
                                    wid * (i % per):wid * (i % per) + wid], dv[i],
                        r=(dump,))
        else:
            for i in range(4):
                ctx.dma("se", y_out[i * 256:(i + 1) * 256, :],
                        xres[i * 256:(i + 1) * 256, :], r=(xres,))
        for e in ENGS:
            for sname in ctx.val:
                ctx.wait(e, sname, ctx.val[sname])
        for st in stacks:
            st.close()
        with nc.Block() as block:
            ctx.realize(block)
        esg.close()
        return nc

    ctx.dma("sy", ident[:], c_ident, w=(ident,))
    ctx.dma("sy", maskt[:], c_mask, w=(maskt,))
    ctx.dma("sy", ones_r[:], c_ones, w=(ones_r,))
    ve(lambda eng: eng.memset(eps_t[:], 1e-5), w=(eps_t,))
    ve(lambda eng: eng.memset(mx_t[:], -1e30), w=(mx_t,))
    ve(lambda eng: eng.memset(acc_t[:], 0.0), w=(acc_t,))
    for i in range(4):
        ctx.dma("se", xres[i * 256:(i + 1) * 256, :], x_own[i * 256:(i + 1) * 256, :],
                w=(xres,))

    es1 = ExitStack()

    def SB1(name, shape, dt=F32):
        return es1.enter_context(nc.sbuf_tensor(name, list(shape), dt))

    x_tok = SB1("x_tok1", [128, D])
    h_tok = SB1("h_tok1", [128, D])
    hT = SB1("hT1", [128, 16 * 1024])
    wq_buf = [SB1(f"wq_buf{i}", [128, D]) for i in range(2)]
    wqT = SB1("wqT", [128, D])
    q_sb = SB1("q_sb1", [128, 512])
    wdT = SB1("wdT", [128, 16 * 512])
    ckv_sb = SB1("ckv_sb", [128, 512])

    for tb in range(8):
        ctx.dma("sy", x_tok[:], x_own[tb * 128:(tb + 1) * 128, :], w=(x_tok,))
        emit_ln(h_tok, x_tok)
        for ic in range(16):
            bi = ic % 2
            tr(bi, 0, h_tok[:, ic * 128:(ic + 1) * 128], r=(h_tok, ident))
            c = ic * 1024 + tb * 128
            ve(lambda eng, b=pb[bi][:, 0:128], c=c:
               eng.tensor_copy(out=hT[:, c:c + 128], in_=b), r=(PB[bi],), w=(hT,))

    col = 0
    wst_cols = {}
    for wi, Wsrc in enumerate((Wg0, Wu0, Wd0)):
        rows, cols = (FF, D) if wi < 2 else (D, FF)
        wst_cols[wi] = (col, col + (rows // 128) * (cols // D))
        for rb in range(rows // 128):
            for cs in range(cols // D):
                bi = col % 2
                ctx.dma("se", wstat[bi][:],
                        Wsrc[rb * 128:(rb + 1) * 128, cs * D:(cs + 1) * D], w=(wstat[bi],))
                ve(lambda eng, b=wstat[bi], c=col:
                   eng.tensor_reduce(out=wsum[:, c:c + 1], in_=b[:], axis=AX.X, op=ALU.add,
                                     apply_absolute_value=True), r=(wstat[bi],), w=(wsum,))
                col += 1
    for wi in range(3):
        c0, c1 = wst_cols[wi]
        ve(lambda eng, c0=c0, c1=c1, wi=wi:
           eng.tensor_reduce(out=st_a[:, wi:wi + 1], in_=wsum[:, c0:c1], axis=AX.X,
                             op=ALU.add), r=(wsum,), w=(st_a,))
        tr(6, 0, st_a[:, wi:wi + 1], r=(st_a, ident), rows=1)
        ve(lambda eng: eng.tensor_copy(out=rowt[:], in_=pb[6][:1, 0:128]), r=(PB[6],),
           w=(rowt,))
        ve(lambda eng: eng.reduce_sum(out=st_c[:1, :], in_=rowt[:], axis=AX.X),
           r=(rowt,), w=(st_c,))
        mmg(6, pb[6][:, 0:1], ones_r[:], st_c[:1, :], start=True, stop=True,
            r=(ones_r, st_c))
        ve(lambda eng, wi=wi: eng.tensor_copy(out=st_b[:, wi:wi + 1], in_=pb[6][:, 0:1]),
           r=(PB[6],), w=(st_b,))
        ve(lambda eng, wi=wi: eng.tensor_scalar(
            out=scw[:, wi:wi + 1], in0=st_b[:, wi:wi + 1], scalar1=float(INV2_24),
            scalar2=float(np.float32(1e-5)), op0=ALU.mult, op1=ALU.max), r=(st_b,), w=(scw,))
        ve(lambda eng, wi=wi: eng.tensor_scalar(
            out=thr_p[wi][:], in0=scw[:, wi:wi + 1], scalar1=float(np.float32(0.5)), scalar2=None,
            op0=ALU.mult), r=(scw,), w=(thr_p[wi],))
        ve(lambda eng, wi=wi: eng.tensor_scalar(
            out=thr_n[wi][:], in0=scw[:, wi:wi + 1], scalar1=float(np.float32(-0.5)), scalar2=None,
            op0=ALU.mult), r=(scw,), w=(thr_n[wi],))

    for oc in range(16):
        wb = wq_buf[oc % 2]
        ctx.dma("sy", wb[:], W_q[oc * 128:(oc + 1) * 128, :], w=(wb,))
        for ic in range(16):
            bi = ic % 2
            tr(bi, 0, wb[:, ic * 128:(ic + 1) * 128], r=(wb, ident))
            ve(lambda eng, b=pb[bi][:, 0:128], c=ic * 128:
               eng.tensor_copy(out=wqT[:, c:c + 128], in_=b), r=(PB[bi],), w=(wqT,))
        for nb in range(2):
            for ic in range(16):
                mmg(2, pb[2][:], wqT[:, ic * 128:(ic + 1) * 128],
                    hT[:, ic * 1024 + nb * 512: ic * 1024 + nb * 512 + 512],
                    start=(ic == 0), stop=(ic == 15), r=(wqT, hT))
            ve(lambda eng: eng.tensor_scalar(out=q_sb[:], in0=pb[2][:], scalar1=float(C0_A),
                                             scalar2=None, op0=ALU.mult),
               r=(PB[2],), w=(q_sb,))
            ctx.dma("se", qt0[oc, nb], q_sb[:], r=(q_sb,), w=(qt0,))

    for lc in range(4):
        ctx.dma("sy", wq_buf[0][:], W_dkv[lc * 128:(lc + 1) * 128, :], w=(wq_buf[0],))
        for ic in range(16):
            bi = ic % 2
            tr(bi, 0, wq_buf[0][:, ic * 128:(ic + 1) * 128], r=(wq_buf[0], ident))
            ve(lambda eng, b=pb[bi][:, 0:128], c=ic * 512 + lc * 128:
               eng.tensor_copy(out=wdT[:, c:c + 128], in_=b), r=(PB[bi],), w=(wdT,))
    for tb in range(8):
        for ic in range(16):
            mmg(3, pb[3][:], hT[:, ic * 1024 + tb * 128: ic * 1024 + tb * 128 + 128],
                wdT[:, ic * 512:(ic + 1) * 512], start=(ic == 0), stop=(ic == 15),
                r=(hT, wdT))
        ve(lambda eng: eng.tensor_copy(out=ckv_sb[:], in_=pb[3][:]), r=(PB[3],),
           w=(ckv_sb,))
        ctx.dma("se", ckv_own[tb * 128:(tb + 1) * 128, :], ckv_sb[:], r=(ckv_sb,),
                w=(ckv_own,))

    ctx.barrier()
    if ph <= 1:
        import os
        return _finish([es1], dump=qt0 if os.environ.get("KDUMP") == "qt0" else None)
    es1.close()

    es3 = ExitStack()

    def SB3(name, shape, dt=F32):
        return es3.enter_context(nc.sbuf_tensor(name, list(shape), dt))

    ckvT = SB3("ckvT", [128, 4 * 4096])
    kT_h = SB3("kT_h", [128, 4096])
    v_h = SB3("v_h", [128, 4096])
    qh = SB3("qh", [128, 1024])
    s_sb = SB3("s_sb", [128, 4096])
    p_sb = SB3("p_sb", [128, 4096])
    pT_sb = SB3("pT_sb", [128, 4096])
    o_sb = SB3("o_sb", [128, 128])
    wup_buf = SB3("wup_buf", [128, 2 * DL0])
    bc_sb = SB3("bc_sb", [128, 128])
    wupkT = SB3("wupkT", [128, 512])
    wupvT = SB3("wupvT", [128, 512])
    ckv_tile = SB3("ckv_tile", [128, 512])

    ge(lambda eng: eng.collective_compute(
        "AllGather", ALU.bypass, replica_groups=[[0, 1, 2, 3], [4, 5, 6, 7]],
        ins=[ckv_own.opt()], outs=[ckv_full.opt()]), r=(ckv_own,), w=(ckv_full,))

    for u in range(32):
        ctx.dma("ge", ckv_tile[:], ckv_full[u * 128:(u + 1) * 128, :],
                r=(ckv_full,), w=(ckv_tile,))
        for lc in range(4):
            bi = lc % 2
            tr(bi, 0, ckv_tile[:, lc * 128:(lc + 1) * 128], r=(ckv_tile, ident))
            ve(lambda eng, b=pb[bi][:, 0:128], c=lc * 4096 + u * 128:
               eng.tensor_copy(out=ckvT[:, c:c + 128], in_=b), r=(PB[bi],), w=(ckvT,))

    if ph <= 2:
        ctx.barrier()
        import os
        if os.environ.get("KDUMP") == "ckv":
            dv = ckv_full.rearrange("(a p) f -> a p f", p=128)
            for i in range(32):
                ctx.dma("se", y_out[128 * (i // 4):128 * (i // 4) + 128,
                                    512 * (i % 4):512 * (i % 4) + 512], dv[i],
                        r=(ckv_full,))
            for e in ENGS:
                for sname in ctx.val:
                    ctx.wait(e, sname, ctx.val[sname])
            es3.close()
            with nc.Block() as block:
                ctx.realize(block)
            esg.close()
            return nc
        return _finish([es3])

    if _os_early() == "ckvT2":
        ctx.barrier()
        for i in range(8):
            ctx.dma("se", y_out[i * 128:(i + 1) * 128, :],
                    ckvT[:, i * 2048:(i + 1) * 2048], r=(ckvT,))
        for e in ENGS:
            for sname in ctx.val:
                ctx.wait(e, sname, ctx.val[sname])
        es3.close()
        with nc.Block() as block:
            ctx.realize(block)
        esg.close()
        return nc

    for h in range(kn_heads):
        ctx.dma("sy", wup_buf[:, 0:512], W_upk[h], w=(wup_buf,))
        ctx.dma("sy", wup_buf[:, 512:1024], W_upv[h], w=(wup_buf,))
        for lc in range(4):
            tr(0, 0, wup_buf[:, lc * 128:(lc + 1) * 128], r=(wup_buf, ident))
            ve(lambda eng, b=pb[0][:, 0:128], c=lc * 128:
               eng.tensor_copy(out=wupkT[:, c:c + 128], in_=b), r=(PB[0],), w=(wupkT,))
            tr(1, 0, wup_buf[:, 512 + lc * 128: 512 + (lc + 1) * 128], r=(wup_buf, ident))
            ve(lambda eng, b=pb[1][:, 0:128], c=lc * 128:
               eng.tensor_copy(out=wupvT[:, c:c + 128], in_=b), r=(PB[1],), w=(wupvT,))
        for kb in range(8):
            for lc in range(4):
                mmg(2, pb[2][:], wupkT[:, lc * 128:(lc + 1) * 128],
                    ckvT[:, lc * 4096 + kb * 512: lc * 4096 + kb * 512 + 512],
                    start=(lc == 0), stop=(lc == 3), r=(wupkT, ckvT))
            ve(lambda eng, c=kb * 512: eng.tensor_copy(out=kT_h[:, c:c + 512], in_=pb[2][:]),
               r=(PB[2],), w=(kT_h,))
        for u in range(32):
            for lc in range(4):
                mmg(3, pb[3][:, 0:128], ckvT[:, lc * 4096 + u * 128: lc * 4096 + u * 128 + 128],
                    wupvT[:, lc * 128:(lc + 1) * 128], start=(lc == 0), stop=(lc == 3),
                    r=(ckvT, wupvT))
            ve(lambda eng, c=u * 128: eng.tensor_copy(out=v_h[:, c:c + 128],
                                                      in_=pb[3][:, 0:128]),
               r=(PB[3],), w=(v_h,))
        ctx.dma("se", qh[:, 0:512], qt0[h, 0], r=(qt0,), w=(qh,))
        ctx.dma("se", qh[:, 512:1024], qt0[h, 1], r=(qt0,), w=(qh,))

        for qb in range(kn_qb):
            for kb in range(8):
                bi = 4 + (kb % 2)
                mmg(bi, pb[bi][:], qh[:, qb * 128:(qb + 1) * 128],
                    kT_h[:, kb * 512:(kb + 1) * 512], start=True, stop=True, r=(qh, kT_h))
                ve(lambda eng, b=pb[bi], c=kb * 512:
                   eng.tensor_copy(out=s_sb[:, c:c + 512], in_=b[:]),
                   r=(PB[bi],), w=(s_sb,))
                ve(lambda eng, c=kb * 512, k=kb:
                   eng.reduce_max(out=mx_t[:, k:k + 1], in_=s_sb[:, c:c + 512], axis=AX.X),
                   r=(s_sb,), w=(mx_t,))
            ve(lambda eng: eng.reduce_max(out=st_c[:], in_=mx_t[:, 0:32], axis=AX.X),
               r=(mx_t,), w=(st_c,))
            ve(lambda eng: eng.tensor_scalar(out=bias_t[:], in0=st_c[:],
                                             scalar1=float(-C0_B), scalar2=None,
                                             op0=ALU.mult), r=(st_c,), w=(bias_t,))
            for kb in range(8):
                se(lambda eng, c=kb * 512, k=kb: eng.activation(
                    out=p_sb[:, c:c + 512], in_=s_sb[:, c:c + 512], func=AF.Exp,
                    bias=bias_t[:], scale=float(C0_B), accum_out=acc_t[:, k:k + 1]),
                    r=(s_sb, bias_t), w=(p_sb, acc_t))
            for u in range(32):
                off = (u % 4) * 128
                tr(2, off, p_sb[:, u * 128:(u + 1) * 128], r=(p_sb, ident))
                ve(lambda eng, o=off, c=u * 128: eng.tensor_copy(
                    out=pT_sb[:, c:c + 128], in_=pb[2][:, o:o + 128]),
                   r=(PB[2],), w=(pT_sb,))
            for u in range(32):
                mmg(6, pb[6][:, 0:128], v_h[:, u * 128:(u + 1) * 128],
                    pT_sb[:, u * 128:(u + 1) * 128], start=(u == 0), stop=(u == 31),
                    r=(v_h, pT_sb))
            ve(lambda eng: eng.reduce_sum(out=st_c[:], in_=acc_t[:, 0:32], axis=AX.X),
               r=(acc_t,), w=(st_c,))
            ve(lambda eng: eng.reciprocal(out=st_d[:], in_=st_c[:]), r=(st_c,), w=(st_d,))
            tr(3, 0, st_d[:], r=(st_d, ident), rows=1)
            ve(lambda eng: eng.tensor_copy(out=rowt[:], in_=pb[3][:1, 0:128]),
               r=(PB[3],), w=(rowt,))
            mmg(3, pb[3][:, 128:256], ones_r[:], rowt[:], start=True, stop=True,
                r=(ones_r, rowt))
            ve(lambda eng: eng.tensor_copy(out=bc_sb[:], in_=pb[3][:, 128:256]),
               r=(PB[3],), w=(bc_sb,))
            ve(lambda eng: eng.tensor_tensor(out=o_sb[:], in0=pb[6][:, 0:128],
                                             in1=bc_sb[:], op=ALU.mult),
               r=(PB[6], bc_sb), w=(o_sb,))
            ctx.dma("se", ot0[h, qb], o_sb[:], r=(o_sb,), w=(ot0,))
            ctx.barrier()

    import os as _os
    if _os.environ.get("KDUMP") == "ckvT":
        ctx.barrier()
        for i in range(8):
            ctx.dma("se", y_out[i * 128:(i + 1) * 128, :],
                    ckvT[:, i * 2048:(i + 1) * 2048], r=(ckvT,))
        for e in ENGS:
            for sname in ctx.val:
                ctx.wait(e, sname, ctx.val[sname])
        es3.close()
        with nc.Block() as block:
            ctx.realize(block)
        esg.close()
        return nc
    if _os.environ.get("KDUMP") == "kv":
        for a_ in range(8):
            ctx.dma("se", ckv_own[a_ * 128:(a_ + 1) * 128, :],
                    kT_h[:, a_ * 512:(a_ + 1) * 512], r=(kT_h,), w=(ckv_own,))
        for a_ in range(4):
            ctx.dma("se", kv_own[a_ * 128:(a_ + 1) * 128, :],
                    v_h[:, a_ * 1024:(a_ + 1) * 1024], r=(v_h,), w=(kv_own,))
        ctx.barrier()
        ctx.dma("se", y_out[0:256, :], ckv_own.rearrange("(a b) f -> a (b f)", b=4),
                r=(ckv_own,))
        ctx.dma("se", y_out[256:512, :], kv_own[0:512, :].rearrange("(a b) f -> a (b f)", b=2),
                r=(kv_own,))
        for e in ENGS:
            for sname in ctx.val:
                ctx.wait(e, sname, ctx.val[sname])
        es3.close()
        with nc.Block() as block:
            ctx.realize(block)
        esg.close()
        return nc
    ctx.barrier()
    if ph <= 3:
        import os
        return _finish([es3], dump=ot0 if os.environ.get("KDUMP") == "ot0" else None)
    es3.close()

    es3b = ExitStack()

    def SB3b(name, shape, dt=F32):
        return es3b.enter_context(nc.sbuf_tensor(name, list(shape), dt))

    woT = SB3b("woT", [128, 16 * 512])
    xr_sb = SB3b("xr_sb", [128, 512])
    oT_buf = SB3b("oT_buf", [128, 2048])
    wo_out = SB3b("wo_out", [128, 512])
    wo_buf = [SB3b(f"wo_buf{i}", [128, D]) for i in range(2)]

    for og in range(4):
        for ocr in range(4):
            wb = wo_buf[ocr % 2]
            ctx.dma("sy", wb[:], W_o[og * 512 + ocr * 128: og * 512 + (ocr + 1) * 128, :],
                    w=(wb,))
            for ic in range(16):
                bi = ic % 2
                tr(bi, 0, wb[:, ic * 128:(ic + 1) * 128], r=(wb, ident))
                ve(lambda eng, b=pb[bi][:, 0:128], c=ic * 512 + ocr * 128:
                   eng.tensor_copy(out=woT[:, c:c + 128], in_=b), r=(PB[bi],), w=(woT,))
        for tb in range(8):
            for ic in range(16):
                ctx.dma("se", oT_buf[:, ic * 128:(ic + 1) * 128], ot0[ic, tb],
                        r=(ot0,), w=(oT_buf,))
            bi = 2 + (tb % 2)
            for ic in range(16):
                mmg(bi, pb[bi][:], oT_buf[:, ic * 128:(ic + 1) * 128],
                    woT[:, ic * 512:(ic + 1) * 512], start=(ic == 0), stop=(ic == 15),
                    r=(oT_buf, woT))
            ctx.dma("se", xr_sb[:], xres[tb * 128:(tb + 1) * 128, og * 512:(og + 1) * 512],
                    r=(xres,), w=(xr_sb,))
            ve(lambda eng, b=pb[bi]: eng.tensor_tensor(out=wo_out[:], in0=b[:],
                                                       in1=xr_sb[:], op=ALU.add),
               r=(PB[bi], xr_sb), w=(wo_out,))
            ctx.dma("se", xres[tb * 128:(tb + 1) * 128, og * 512:(og + 1) * 512],
                    wo_out[:], r=(wo_out,), w=(xres,))

    ctx.barrier()
    if ph <= 4:
        return _finish([es3b])
    es3b.close()

    es4a = ExitStack()

    def SB4a(name, shape, dt=F32):
        return es4a.enter_context(nc.sbuf_tensor(name, list(shape), dt))

    identb = SB4a("identb", [128, 128], BF16)
    m_buf = SB4a("m_buf", [128, D], BF16)
    sg_a = SB4a("sg_a", [128, D])
    sg_b = SB4a("sg_b", [128, D])
    mT_sb = SB4a("mT_sb", [128, D], BF16)
    ctx.dma("sy", identb[:], c_identb, w=(identb,))

    for wi, (Wsrc, Wdst, rows, cols) in enumerate(
            ((Wg0, wgt_b, FF, D), (Wu0, wut_b, FF, D), (Wd0, wdt_b, D, FF))):
        for rb in range(rows // 128):
            for cs in range(cols // D):
                bi = (rb * (cols // D) + cs) % 2
                ctx.dma("sy", wstat[bi][:],
                        Wsrc[rb * 128:(rb + 1) * 128, cs * D:(cs + 1) * D], w=(wstat[bi],))
                se(lambda eng, b=wstat[bi], w=wi: eng.activation(
                    out=sg_a[:], in_=b[:], func=AF.Sign, bias=thr_n[w][:]),
                    r=(wstat[bi], thr_n[wi]), w=(sg_a,))
                se(lambda eng, b=wstat[bi], w=wi: eng.activation(
                    out=sg_b[:], in_=b[:], func=AF.Sign, bias=thr_p[w][:]),
                    r=(wstat[bi], thr_p[wi]), w=(sg_b,))
                ve(lambda eng: eng.tensor_tensor(out=m_buf[:], in0=sg_a[:], in1=sg_b[:],
                                                 op=ALU.add), r=(sg_a, sg_b), w=(m_buf,))
                for ic in range(16):
                    off = (ic % 8) * 128
                    tr(7, off, m_buf[:, ic * 128:(ic + 1) * 128], r=(m_buf, identb),
                       idn=identb)
                    ve(lambda eng, o=off, c=ic * 128:
                       eng.tensor_copy(out=mT_sb[:, c:c + 128], in_=pbT[:, o:o + 128]),
                       r=(PB[7],), w=(mT_sb,))
                    ctx.dma("se", Wdst[rb, :, cs * 16 + ic, :],
                            mT_sb[:, ic * 128:(ic + 1) * 128], r=(mT_sb,),
                            w=(Wdst,))

    ctx.barrier()
    if ph <= 5:
        return _finish([es4a])
    es4a.close()

    es4b = ExitStack()

    def SB4b(name, shape, dt=F32):
        return es4b.enter_context(nc.sbuf_tensor(name, list(shape), dt))

    h128 = SB4b("h128", [128, 128])
    x_tok4 = SB4b("x_tok4", [128, D])
    h_tok4 = SB4b("h_tok4", [128, D])
    hT2 = SB4b("hT2", [128, 16 * 256])
    k1 = SB4b("k1", [128, 16 * 256], BF16)
    t_sb = SB4b("t_sb", [128, 64 * 256])
    k2 = SB4b("k2", [128, 64 * 256], BF16)
    cmx = SB4b("cmx", [128, 256])
    cmb = SB4b("cmb", [128, 256])
    sbc = SB4b("sbc", [128, 256])
    comb = SB4b("comb", [128, 256])
    vq = SB4b("vq", [128, 256])
    avx = SB4b("avx", [128, 256])
    rrx = SB4b("rrx", [128, 256])
    sgn = SB4b("sgn", [128, 256])
    ggx = SB4b("ggx", [128, 256])
    wg_sb = SB4b("wg_sb", [128, 16 * 128], BF16)
    wu_sb = SB4b("wu_sb", [128, 16 * 128], BF16)
    wd_sb = SB4b("wd_sb", [128, 64 * 128], BF16)
    ot_stage = SB4b("ot_stage", [128, 128])
    ctx.dma("sy", h128[:], c_h128, w=(h128,))

    def emit_quant(src_sb, nchunk, ch_norm, k_dst):
        ve(lambda eng: eng.scalar_tensor_tensor(out=cmx[:], in0=src_sb[:, 0:256],
                                                scalar=-1.0, in1=src_sb[:, 0:256],
                                                op0=ALU.mult, op1=ALU.max),
           r=(src_sb,), w=(cmx,))
        for icx in range(1, nchunk):
            c = icx * 256
            ve(lambda eng, c=c: eng.scalar_tensor_tensor(
                out=avx[:], in0=src_sb[:, c:c + 256], scalar=-1.0,
                in1=src_sb[:, c:c + 256], op0=ALU.mult, op1=ALU.max),
                r=(src_sb,), w=(avx,))
            ve(lambda eng: eng.tensor_tensor(out=cmx[:], in0=cmx[:], in1=avx[:],
                                             op=ALU.max), r=(cmx, avx), w=(cmx,))
        for half in range(2):
            tr(6, 0, cmx[:, half * 128:(half + 1) * 128], r=(cmx, ident))
            ve(lambda eng: eng.reduce_max(out=st_c[:], in_=pb[6][:, 0:128], axis=AX.X),
               r=(PB[6],), w=(st_c,))
            tr(6, 128, st_c[:], r=(st_c, ident), rows=1)
            ve(lambda eng, half=half: eng.tensor_copy(out=rowt2[:, half * 128:(half + 1) * 128],
                                                      in_=pb[6][:1, 128:256]),
               r=(PB[6],), w=(rowt2,))
        mmg(6, pb[6][:, 0:256], ones_r[:], rowt2[:], start=True, stop=True,
            r=(ones_r, rowt2))
        ve(lambda eng: eng.tensor_copy(out=cmb[:], in_=pb[6][:, 0:256]), r=(PB[6],),
           w=(cmb,))
        ve(lambda eng: eng.tensor_scalar(out=cmx[:], in0=cmb[:], scalar1=float(ch_norm),
                                         scalar2=float(np.float32(1e-5)), op0=ALU.mult,
                                         op1=ALU.max), r=(cmb,), w=(cmx,))
        ve(lambda eng: eng.reciprocal(out=cmb[:], in_=cmx[:]), r=(cmx,), w=(cmb,))
        ve(lambda eng: eng.tensor_scalar(out=sbc[:], in0=cmb[:], scalar1=float(ch_norm),
                                         scalar2=float(np.float32(127.0)), op0=ALU.mult,
                                         op1=ALU.mult), r=(cmb,), w=(sbc,))
        ve(lambda eng: eng.tensor_scalar(out=comb[:], in0=cmx[:], scalar1=float(INV254),
                                         scalar2=None, op0=ALU.mult), r=(cmx,), w=(comb,))
        for icx in range(nchunk):
            c = icx * 256
            ve(lambda eng, c=c: eng.tensor_tensor(out=vq[:], in0=src_sb[:, c:c + 256],
                                                  in1=sbc[:], op=ALU.mult),
               r=(src_sb, sbc), w=(vq,))
            se(lambda eng: eng.activation(out=sgn[:], in_=vq[:], func=AF.Sign),
               r=(vq,), w=(sgn,))
            ve(lambda eng: eng.scalar_tensor_tensor(out=avx[:], in0=vq[:], scalar=-1.0,
                                                    in1=vq[:], op0=ALU.mult, op1=ALU.max),
               r=(vq,), w=(avx,))
            ve(lambda eng: eng.tensor_scalar(out=avx[:], in0=avx[:], scalar1=0.5,
                                             scalar2=None, op0=ALU.add), r=(avx,), w=(avx,))
            ve(lambda eng: eng.tensor_copy(out=rrx[:], in_=avx[:]), r=(avx,), w=(rrx,))
            for bb in (64.0, 32.0, 16.0, 8.0, 4.0, 2.0, 1.0):
                ve(lambda eng, bb=bb: eng.tensor_scalar(out=ggx[:], in0=rrx[:], scalar1=bb,
                                                        scalar2=None, op0=ALU.is_ge),
                   r=(rrx,), w=(ggx,))
                ve(lambda eng, bb=bb: eng.scalar_tensor_tensor(out=rrx[:], in0=ggx[:],
                                                               scalar=-bb, in1=rrx[:],
                                                               op0=ALU.mult, op1=ALU.add),
                   r=(ggx, rrx), w=(rrx,))
            ve(lambda eng: eng.tensor_tensor(out=avx[:], in0=avx[:], in1=rrx[:],
                                             op=ALU.subtract), r=(avx, rrx), w=(avx,))
            ve(lambda eng, c=c: eng.tensor_tensor(out=k_dst[:, c:c + 256], in0=avx[:],
                                                  in1=sgn[:], op=ALU.mult),
               r=(avx, sgn), w=(k_dst,))

    for ttile in range(4):
        for tbb in range(2):
            tbix = ttile * 2 + tbb
            ctx.dma("sy", x_tok4[:], xres[tbix * 128:(tbix + 1) * 128, :],
                    r=(xres,), w=(x_tok4,))
            emit_ln(h_tok4, x_tok4)
            for ic in range(16):
                bi = ic % 2
                tr(bi, 0, h_tok4[:, ic * 128:(ic + 1) * 128], r=(h_tok4, ident))
                ve(lambda eng, b=pb[bi][:, 0:128], c=ic * 256 + tbb * 128:
                   eng.tensor_copy(out=hT2[:, c:c + 128], in_=b), r=(PB[bi],), w=(hT2,))
        for stg in (1, 2, 4, 8):
            for icx in range(16):
                if icx & stg:
                    continue
                a = icx * 256
                b_ = (icx ^ stg) * 256
                ve(lambda eng, a=a, b_=b_: eng.tensor_tensor(
                    out=hT2[:, a:a + 256], in0=hT2[:, a:a + 256], in1=hT2[:, b_:b_ + 256],
                    op=ALU.add), r=(hT2,), w=(hT2,))
                ve(lambda eng, a=a, b_=b_: eng.scalar_tensor_tensor(
                    out=hT2[:, b_:b_ + 256], in0=hT2[:, b_:b_ + 256],
                    scalar=float(np.float32(-2.0)), in1=hT2[:, a:a + 256],
                    op0=ALU.mult, op1=ALU.add), r=(hT2,), w=(hT2,))
        for icx in range(16):
            bi = icx % 2
            mmg(bi, pb[bi][:, 0:256], h128[:], hT2[:, icx * 256:(icx + 1) * 256],
                start=True, stop=True, r=(h128, hT2))
            ve(lambda eng, b=pb[bi], c=icx * 256:
               eng.tensor_copy(out=hT2[:, c:c + 256], in_=b[:, 0:256]),
               r=(PB[bi],), w=(hT2,))
        emit_quant(hT2, 16, CH2048, k1)
        for oc in range(64):
            ctx.dma("sy", wg_sb[:], wgt_b[oc], w=(wg_sb,))
            ctx.dma("sy", wu_sb[:], wut_b[oc], w=(wu_sb,))
            for ic in range(16):
                mmg(2, pb[2][:, 0:256], wg_sb[:, ic * 128:(ic + 1) * 128],
                    k1[:, ic * 256:(ic + 1) * 256], start=(ic == 0), stop=(ic == 15),
                    r=(wg_sb, k1))
                mmg(3, pb[3][:, 0:256], wu_sb[:, ic * 128:(ic + 1) * 128],
                    k1[:, ic * 256:(ic + 1) * 256], start=(ic == 0), stop=(ic == 15),
                    r=(wu_sb, k1))
            ve(lambda eng: eng.scalar_tensor_tensor(
                out=vq[:], in0=pb[2][:, 0:256], scalar=scw[:, 0:1], in1=comb[:],
                op0=ALU.mult, op1=ALU.mult), r=(PB[2], scw, comb), w=(vq,))
            se(lambda eng: eng.activation(out=avx[:], in_=vq[:], func=AF.Silu),
               r=(vq,), w=(avx,))
            ve(lambda eng: eng.scalar_tensor_tensor(
                out=rrx[:], in0=pb[3][:, 0:256], scalar=scw[:, 1:2], in1=comb[:],
                op0=ALU.mult, op1=ALU.mult), r=(PB[3], scw, comb), w=(rrx,))
            ve(lambda eng, c=oc * 256: eng.tensor_tensor(
                out=t_sb[:, c:c + 256], in0=avx[:], in1=rrx[:], op=ALU.mult),
                r=(avx, rrx), w=(t_sb,))
        for stg in (1, 2, 4, 8, 16, 32):
            for icx in range(64):
                if icx & stg:
                    continue
                a = icx * 256
                b_ = (icx ^ stg) * 256
                ve(lambda eng, a=a, b_=b_: eng.tensor_tensor(
                    out=t_sb[:, a:a + 256], in0=t_sb[:, a:a + 256], in1=t_sb[:, b_:b_ + 256],
                    op=ALU.add), r=(t_sb,), w=(t_sb,))
                ve(lambda eng, a=a, b_=b_: eng.scalar_tensor_tensor(
                    out=t_sb[:, b_:b_ + 256], in0=t_sb[:, b_:b_ + 256],
                    scalar=float(np.float32(-2.0)), in1=t_sb[:, a:a + 256],
                    op0=ALU.mult, op1=ALU.add), r=(t_sb,), w=(t_sb,))
        for icx in range(64):
            bi = icx % 2
            mmg(bi, pb[bi][:, 0:256], h128[:], t_sb[:, icx * 256:(icx + 1) * 256],
                start=True, stop=True, r=(h128, t_sb))
            ve(lambda eng, b=pb[bi], c=icx * 256:
               eng.tensor_copy(out=t_sb[:, c:c + 256], in_=b[:, 0:256]),
               r=(PB[bi],), w=(t_sb,))
        emit_quant(t_sb, 64, CH8192, k2)
        for tbb in range(2):
            tbix = ttile * 2 + tbb
            ctx.dma("sy", hT2[:, tbb * 2048:(tbb + 1) * 2048],
                    xres[tbix * 128:(tbix + 1) * 128, :], r=(xres,), w=(hT2,))
        for ocp in range(16):
            ctx.dma("sy", wd_sb[:], wdt_b[ocp], w=(wd_sb,))
            for icp in range(64):
                mmg(4, pb[4][:, 0:256], wd_sb[:, icp * 128:(icp + 1) * 128],
                    k2[:, icp * 256:(icp + 1) * 256], start=(icp == 0), stop=(icp == 63),
                    r=(wd_sb, k2))
            ve(lambda eng: eng.scalar_tensor_tensor(
                out=vq[:], in0=pb[4][:, 0:256], scalar=scw[:, 2:3], in1=comb[:],
                op0=ALU.mult, op1=ALU.mult), r=(PB[4], scw, comb), w=(vq,))
            for tbb in range(2):
                tr(5, 0, vq[:, tbb * 128:(tbb + 1) * 128], r=(vq, ident))
                ve(lambda eng, tbb=tbb, c=ocp * 128: eng.tensor_tensor(
                    out=hT2[:, tbb * 2048 + c: tbb * 2048 + c + 128], in0=pb[5][:, 0:128],
                    in1=hT2[:, tbb * 2048 + c: tbb * 2048 + c + 128], op=ALU.add),
                   r=(PB[5], hT2), w=(hT2,))

        for tbb in range(2):
            tbix = ttile * 2 + tbb
            ctx.dma("se", xres[tbix * 128:(tbix + 1) * 128, :],
                    hT2[:, tbb * 2048:(tbb + 1) * 2048], r=(hT2,), w=(xres,))

    ctx.barrier()
    if ph <= 6:
        return _finish([es4b])
    es4b.close()

    es5a = ExitStack()

    def SB5a(name, shape, dt=F32):
        return es5a.enter_context(nc.sbuf_tensor(name, list(shape), dt))

    x_tok5 = SB5a("x_tok5", [128, D])
    h_tok5 = SB5a("h_tok5", [128, D])
    hT5 = SB5a("hT5", [128, 16 * 1024])
    wkv_sb = SB5a("wkv_sb", [128, 16 * 512])
    cs_cos = SB5a("cs_cos", [128, 8 * 256])
    cs_sin = SB5a("cs_sin", [128, 8 * 256])
    ro_a = SB5a("ro_a", [128, 256])
    ro_b = SB5a("ro_b", [128, 256])
    ro_out = SB5a("ro_out", [128, 512])
    qT_stage = SB5a("qT_stage", [128, 128])

    for tb in range(8):
        ctx.dma("sy", cs_cos[:, tb * 256:(tb + 1) * 256], c_cos[tb * 128:(tb + 1) * 128, :],
                w=(cs_cos,))
        ctx.dma("sy", cs_sin[:, tb * 256:(tb + 1) * 256], c_sin[tb * 128:(tb + 1) * 128, :],
                w=(cs_sin,))

    for tb in range(8):
        ctx.dma("sy", x_tok5[:], xres[tb * 128:(tb + 1) * 128, :], r=(xres,), w=(x_tok5,))
        emit_ln(h_tok5, x_tok5)
        for ic in range(16):
            bi = ic % 2
            tr(bi, 0, h_tok5[:, ic * 128:(ic + 1) * 128], r=(h_tok5, ident))
            ve(lambda eng, b=pb[bi][:, 0:128], c=ic * 1024 + tb * 128:
               eng.tensor_copy(out=hT5[:, c:c + 128], in_=b), r=(PB[bi],), w=(hT5,))

    def emit_rope(bank, tb, out_buf):
        c0 = tb * 256
        ve(lambda eng, b=pb[bank]: eng.tensor_tensor(
            out=ro_a[:], in0=b[:, 0::2], in1=cs_cos[:, c0:c0 + 256], op=ALU.mult),
            r=(PB[bank], cs_cos), w=(ro_a,))
        ve(lambda eng, b=pb[bank]: eng.tensor_tensor(
            out=ro_b[:], in0=b[:, 1::2], in1=cs_sin[:, c0:c0 + 256], op=ALU.mult),
            r=(PB[bank], cs_sin), w=(ro_b,))
        ve(lambda eng: eng.tensor_tensor(
            out=out_buf[:, 0::2], in0=ro_a[:], in1=ro_b[:], op=ALU.subtract),
            r=(ro_a, ro_b), w=(out_buf,))
        ve(lambda eng, b=pb[bank]: eng.tensor_tensor(
            out=ro_a[:], in0=b[:, 0::2], in1=cs_sin[:, c0:c0 + 256], op=ALU.mult),
            r=(PB[bank], cs_sin), w=(ro_a,))
        ve(lambda eng, b=pb[bank]: eng.tensor_tensor(
            out=ro_b[:], in0=b[:, 1::2], in1=cs_cos[:, c0:c0 + 256], op=ALU.mult),
            r=(PB[bank], cs_cos), w=(ro_b,))
        return ve(lambda eng: eng.tensor_tensor(
            out=out_buf[:, 1::2], in0=ro_a[:], in1=ro_b[:], op=ALU.add),
            r=(ro_a, ro_b), w=(out_buf,))

    for ic in range(16):
        ctx.dma("sy", wkv_sb[:, ic * 512:(ic + 1) * 512], wk1[ic * 128:(ic + 1) * 128, :],
                w=(wkv_sb,))
    for tb in range(8):
        for ic in range(16):
            mmg(2, pb[2][:], hT5[:, ic * 1024 + tb * 128: ic * 1024 + tb * 128 + 128],
                wkv_sb[:, ic * 512:(ic + 1) * 512], start=(ic == 0), stop=(ic == 15),
                r=(hT5, wkv_sb))
        emit_rope(2, tb, ro_out)
        ctx.dma("se", kv_own[tb * 128:(tb + 1) * 128, 0:512], ro_out[:], r=(ro_out,),
                w=(kv_own,))
    for ic in range(16):
        ctx.dma("sy", wkv_sb[:, ic * 512:(ic + 1) * 512], wv1[ic * 128:(ic + 1) * 128, :],
                w=(wkv_sb,))
    for tb in range(8):
        for ic in range(16):
            mmg(2, pb[2][:], hT5[:, ic * 1024 + tb * 128: ic * 1024 + tb * 128 + 128],
                wkv_sb[:, ic * 512:(ic + 1) * 512], start=(ic == 0), stop=(ic == 15),
                r=(hT5, wkv_sb))
        ve(lambda eng: eng.tensor_copy(out=ro_out[:], in_=pb[2][:]), r=(PB[2],),
           w=(ro_out,))
        ctx.dma("se", kv_own[tb * 128:(tb + 1) * 128, 512:1024], ro_out[:], r=(ro_out,),
                w=(kv_own,))
    ge(lambda eng: eng.collective_compute(
        "AllGather", ALU.bypass, replica_groups=[[0, 1, 2, 3], [4, 5, 6, 7]],
        ins=[kv_own.opt()], outs=[kv_full.opt()]), r=(kv_own,), w=(kv_full,))

    for og in range(4):
        for ic in range(16):
            ctx.dma("sy", wkv_sb[:, ic * 512:(ic + 1) * 512],
                    wq1[ic * 128:(ic + 1) * 128, og * 512:(og + 1) * 512],
                    w=(wkv_sb,))
        for tb in range(8):
            for ic in range(16):
                mmg(2, pb[2][:], hT5[:, ic * 1024 + tb * 128: ic * 1024 + tb * 128 + 128],
                    wkv_sb[:, ic * 512:(ic + 1) * 512], start=(ic == 0), stop=(ic == 15),
                    r=(hT5, wkv_sb))
            emit_rope(2, tb, ro_out)
            for j in range(4):
                tr(3, 0, ro_out[:, j * 128:(j + 1) * 128], r=(ro_out, ident))
                ve(lambda eng: eng.tensor_copy(out=qT_stage[:], in_=pb[3][:, 0:128]),
                   r=(PB[3],), w=(qT_stage,))
                ctx.dma("se", qt1[og * 4 + j, tb], qT_stage[:], r=(qT_stage,),
                        w=(qt1,))

    ctx.barrier()
    if ph <= 7:
        return _finish([es5a])
    es5a.close()

    es5b = ExitStack()

    def SB5b(name, shape, dt=F32):
        return es5b.enter_context(nc.sbuf_tensor(name, list(shape), dt))

    kT_sb = SB5b("kT_sb", [128, 4 * 4096])
    v_sb = SB5b("v_sb", [128, 32 * 512])
    q_sb5 = SB5b("q_sb5", [128, 1024])
    s_a = SB5b("s_a", [128, 4096])
    s_b = SB5b("s_b", [128, 4096])
    p5_sb = SB5b("p5_sb", [128, 512])
    pT5 = SB5b("pT5", [128, 256])
    o_sb5 = SB5b("o_sb5", [128, 128])
    bc_sb5 = SB5b("bc_sb5", [128, 128])
    k_tile = SB5b("k_tile", [128, 512])

    kvr = kv_full.rearrange("(r u s) f -> u r s f", r=4, s=32)
    for u in range(32):
        ctx.dma("ge", k_tile[:], kvr[u, :, :, 0:512], r=(kv_full,), w=(k_tile,))
        ctx.dma("ge", v_sb[:, u * 512:(u + 1) * 512], kvr[u, :, :, 512:1024],
                r=(kv_full,), w=(v_sb,))
        for fc in range(4):
            bi = fc % 2
            tr(bi, 0, k_tile[:, fc * 128:(fc + 1) * 128], r=(k_tile, ident))
            ve(lambda eng, b=pb[bi][:, 0:128], c=fc * 4096 + u * 128:
               eng.tensor_copy(out=kT_sb[:, c:c + 128], in_=b), r=(PB[bi],),
               w=(kT_sb,))

    pairs = [h for a in range(4) for h in (8 * a, 8 * a + 1, 8 * a + 2, 8 * a + 3)]
    for h in pairs:
        j = h // 4
        fc = j // 2
        oc_a, sub_a = h // 2, 64 * (h % 2)
        hb = h + 4
        oc_b, sub_b = hb // 2, 64 * (hb % 2)
        for tb in range(8):
            ctx.dma("se", q_sb5[0:64, tb * 128:(tb + 1) * 128],
                    qt1[oc_a, tb, sub_a:sub_a + 64, :], r=(qt1,), w=(q_sb5,))
            ctx.dma("se", q_sb5[64:128, tb * 128:(tb + 1) * 128],
                    qt1[oc_b, tb, sub_b:sub_b + 64, :], r=(qt1,),
                    w=(q_sb5,))
        for qt in range(8):
            nch = qt + 1
            for u5 in range(nch):
                mmg(4, pb[4][:], q_sb5[0:64, qt * 128:(qt + 1) * 128],
                    kT_sb[0:64, fc * 4096 + u5 * 512: fc * 4096 + u5 * 512 + 512],
                    start=True, stop=True, r=(q_sb5, kT_sb))
                mmg(5, pb[5][:], q_sb5[64:128, qt * 128:(qt + 1) * 128],
                    kT_sb[64:128, fc * 4096 + u5 * 512: fc * 4096 + u5 * 512 + 512],
                    start=True, stop=True, r=(q_sb5, kT_sb))
                if u5 == qt:
                    ve(lambda eng, c=u5 * 512: eng.tensor_tensor(
                        out=s_a[:, c:c + 512], in0=pb[4][:], in1=maskt[:], op=ALU.add),
                        r=(PB[4], maskt), w=(s_a,))
                    ve(lambda eng, c=u5 * 512: eng.tensor_tensor(
                        out=s_b[:, c:c + 512], in0=pb[5][:], in1=maskt[:], op=ALU.add),
                        r=(PB[5], maskt), w=(s_b,))
                else:
                    se(lambda eng, c=u5 * 512: eng.copy(out=s_a[:, c:c + 512],
                                                          in_=pb[4][:]),
                       r=(PB[4],), w=(s_a,))
                    se(lambda eng, c=u5 * 512: eng.copy(out=s_b[:, c:c + 512],
                                                          in_=pb[5][:]),
                       r=(PB[5],), w=(s_b,))
            for s_x, trb, avb, vj, oc_x, sub_x in ((s_a, 2, 0, j, oc_a, sub_a),
                                                   (s_b, 3, 1, j + 1, oc_b, sub_b)):
                ve(lambda eng: eng.memset(mx_t[:], -1e30), w=(mx_t,))
                ve(lambda eng: eng.memset(acc_t[:], 0.0), w=(acc_t,))
                for u5 in range(nch):
                    ve(lambda eng, c=u5 * 512, k=u5, s_x=s_x: eng.reduce_max(
                        out=mx_t[:, k:k + 1], in_=s_x[:, c:c + 512], axis=AX.X),
                        r=(s_x,), w=(mx_t,))
                ve(lambda eng, n=nch: eng.reduce_max(out=st_c[:], in_=mx_t[:, 0:n],
                                                     axis=AX.X), r=(mx_t,), w=(st_c,))
                ve(lambda eng: eng.tensor_scalar(
                    out=bias_t[:], in0=st_c[:], scalar1=float(np.float32(-0.125)), scalar2=None,
                    op0=ALU.mult), r=(st_c,), w=(bias_t,))
                for u5 in range(nch):
                    se(lambda eng, c=u5 * 512, k=u5, s_x=s_x: eng.activation(
                        out=p5_sb[:], in_=s_x[:, c:c + 512], func=AF.Exp, bias=bias_t[:],
                        scale=float(np.float32(0.125)), accum_out=acc_t[:, k:k + 1]),
                        r=(s_x, bias_t), w=(p5_sb, acc_t))
                    for jj in range(4):
                        u = u5 * 4 + jj
                        off = (jj % 2) * 128
                        tr(trb, off, p5_sb[:, jj * 128:(jj + 1) * 128], r=(p5_sb, ident))
                        ve(lambda eng, o=off, trb=trb: eng.tensor_copy(
                            out=pT5[:, o:o + 128], in_=pb[trb][:, o:o + 128]),
                            r=(PB[trb],), w=(pT5,))
                        mmg(avb, pb[avb][sub_x:sub_x + 64, 0:128],
                            v_sb[:, u * 512 + 64 * vj: u * 512 + 64 * vj + 64],
                            pT5[:, off:off + 128], start=(u == 0), stop=(u == 4 * nch - 1),
                            r=(v_sb, pT5))
                ve(lambda eng, n=nch: eng.reduce_sum(out=st_c[:], in_=acc_t[:, 0:n],
                                                     axis=AX.X), r=(acc_t,), w=(st_c,))
                ve(lambda eng: eng.reciprocal(out=st_d[:], in_=st_c[:]), r=(st_c,),
                   w=(st_d,))
                tr(6, 0, st_d[:], r=(st_d, ident), rows=1)
                ve(lambda eng: eng.tensor_copy(out=rowt[:], in_=pb[6][:1, 0:128]),
                   r=(PB[6],), w=(rowt,))
                mmg(6, pb[6][:, 128:256], ones_r[:], rowt[:], start=True, stop=True,
                    r=(ones_r, rowt))
                ve(lambda eng: eng.tensor_copy(out=bc_sb5[:], in_=pb[6][:, 128:256]),
                   r=(PB[6],), w=(bc_sb5,))
                ve(lambda eng, avb=avb, sub_x=sub_x: eng.tensor_tensor(
                    out=o_sb5[sub_x:sub_x + 64, :], in0=pb[avb][sub_x:sub_x + 64, 0:128],
                    in1=bc_sb5[sub_x:sub_x + 64, :], op=ALU.mult),
                    r=(PB[avb], bc_sb5), w=(o_sb5,))
                ctx.dma("se", o1b[oc_x, qt, sub_x:sub_x + 64, :], o_sb5[sub_x:sub_x + 64, :],
                        r=(o_sb5,), w=(o1b,))

    ctx.barrier()
    if ph <= 8:
        return _finish([es5b])
    es5b.close()

    es5c = ExitStack()

    def SB5c(name, shape, dt=F32):
        return es5c.enter_context(nc.sbuf_tensor(name, list(shape), dt))

    wo_sb = SB5c("wo_sb", [128, 16 * 512])
    xr_sb5 = SB5c("xr_sb5", [128, 512])
    oT_buf5 = SB5c("oT_buf5", [128, 2048])
    wo_out5 = SB5c("wo_out5", [128, 512])

    for og in range(4):
        for ic in range(16):
            ctx.dma("sy", wo_sb[:, ic * 512:(ic + 1) * 512],
                    wo1[ic * 128:(ic + 1) * 128, og * 512:(og + 1) * 512],
                    w=(wo_sb,))
        for tb in range(8):
            for ic in range(16):
                ctx.dma("se", oT_buf5[:, ic * 128:(ic + 1) * 128], o1b[ic, tb],
                        r=(o1b,), w=(oT_buf5,))
            bi = 2 + (tb % 2)
            for ic in range(16):
                mmg(bi, pb[bi][:], oT_buf5[:, ic * 128:(ic + 1) * 128],
                    wo_sb[:, ic * 512:(ic + 1) * 512], start=(ic == 0), stop=(ic == 15),
                    r=(oT_buf5, wo_sb))
            ctx.dma("se", xr_sb5[:], xres[tb * 128:(tb + 1) * 128, og * 512:(og + 1) * 512],
                    r=(xres,), w=(xr_sb5,))
            ve(lambda eng, b=pb[bi]: eng.tensor_tensor(out=wo_out5[:], in0=b[:],
                                                       in1=xr_sb5[:], op=ALU.add),
               r=(PB[bi], xr_sb5), w=(wo_out5,))
            ctx.dma("se", xres[tb * 128:(tb + 1) * 128, og * 512:(og + 1) * 512],
                    wo_out5[:], r=(wo_out5,), w=(xres,))

    ctx.barrier()
    if ph <= 9:
        return _finish([es5c])
    es5c.close()

    es6 = ExitStack()

    def SB6(name, shape, dt=F32):
        return es6.enter_context(nc.sbuf_tensor(name, list(shape), dt))

    x_tok6 = SB6("x_tok6", [128, D])
    h_tok6 = SB6("h_tok6", [128, D])
    x3hi = SB6("x3hi", [128, 16 * 256], BF16)
    x3lo = SB6("x3lo", [128, 16 * 256], BF16)
    fhi = SB6("fhi", [128, 64 * 256], BF16)
    flo = SB6("flo", [128, 64 * 256], BF16)
    wgb = SB6("wgb", [128, 16 * 512], BF16)
    wub = SB6("wub", [128, 16 * 512], BF16)
    w8buf = [SB6(f"w8buf{i}", [128, 512], I8) for i in range(2)]
    wdb = [SB6(f"wdb{i}", [128, 512], BF16) for i in range(2)]
    sdb = SB6("sdb", [128, 2048])
    sd_row = SB6("sd_row", [1, 2048])
    sg_sb = SB6("sg_sb", [128, 64])
    su_sb = SB6("su_sb", [128, 64])
    vq6 = SB6("vq6", [128, 512])
    av6 = SB6("av6", [128, 512])
    rr6 = SB6("rr6", [128, 512])
    out6 = SB6("out6", [128, 512])

    ctx.dma("sy", sg_sb[:], sg1.rearrange("(o p) -> p o", p=128), w=(sg_sb,), slow=True)
    ctx.dma("sy", su_sb[:], su1.rearrange("(o p) -> p o", p=128), w=(su_sb,), slow=True)
    ctx.dma("sy", sd_row[:], sd1.rearrange("(a d) -> a d", a=1), w=(sd_row,))
    for jj in range(4):
        mmg(6, pb[6][:], ones_r[:], sd_row[:, jj * 512:(jj + 1) * 512],
            start=True, stop=True, r=(ones_r, sd_row))
        ve(lambda eng, c=jj * 512: eng.tensor_copy(out=sdb[:, c:c + 512], in_=pb[6][:]),
           r=(PB[6],), w=(sdb,))

    for q4 in range(4):
        for tbb in range(2):
            tbix = q4 * 2 + tbb
            ctx.dma("sy", x_tok6[:], xres[tbix * 128:(tbix + 1) * 128, :],
                    r=(xres,), w=(x_tok6,))
            emit_ln(h_tok6, x_tok6)
            for ic in range(16):
                bi = ic % 2
                tr(bi, 0, h_tok6[:, ic * 128:(ic + 1) * 128], r=(h_tok6, ident))
                c = ic * 256 + tbb * 128
                ve(lambda eng, b=pb[bi][:, 0:128], c=c:
                   eng.tensor_copy(out=x3hi[:, c:c + 128], in_=b), r=(PB[bi],), w=(x3hi,))
                ve(lambda eng, b=pb[bi][:, 0:128], c=c:
                   eng.tensor_tensor(out=x3lo[:, c:c + 128], in0=b, in1=x3hi[:, c:c + 128],
                                     op=ALU.subtract), r=(PB[bi], x3hi), w=(x3lo,))
        for og in range(16):
            for ic in range(16):
                bi8 = ic % 2
                ctx.dma("sy", w8buf[bi8][:],
                        wg1[ic * 128:(ic + 1) * 128, og * 512:(og + 1) * 512],
                        w=(w8buf[bi8],))
                ve(lambda eng, b=w8buf[bi8], c=ic * 512:
                   eng.tensor_copy(out=wgb[:, c:c + 512], in_=b[:]),
                   r=(w8buf[bi8],), w=(wgb,))
                ctx.dma("sy", w8buf[bi8][:],
                        wu1[ic * 128:(ic + 1) * 128, og * 512:(og + 1) * 512],
                        w=(w8buf[bi8],))
                ve(lambda eng, b=w8buf[bi8], c=ic * 512:
                   eng.tensor_copy(out=wub[:, c:c + 512], in_=b[:]),
                   r=(w8buf[bi8],), w=(wub,))
            for oc4 in range(4):
                oc = og * 4 + oc4
                for ic in range(16):
                    for hl, xsrc in ((0, x3hi), (1, x3lo)):
                        mmg(2, pb[2][:, 0:256],
                            wgb[:, ic * 512 + oc4 * 128: ic * 512 + oc4 * 128 + 128],
                            xsrc[:, ic * 256:(ic + 1) * 256],
                            start=(ic == 0 and hl == 0), stop=(ic == 15 and hl == 1),
                            r=(wgb, xsrc))
                        mmg(3, pb[3][:, 0:256],
                            wub[:, ic * 512 + oc4 * 128: ic * 512 + oc4 * 128 + 128],
                            xsrc[:, ic * 256:(ic + 1) * 256],
                            start=(ic == 0 and hl == 0), stop=(ic == 15 and hl == 1),
                            r=(wub, xsrc))
                ve(lambda eng, oc=oc: eng.tensor_scalar(
                    out=vq6[:, 0:256], in0=pb[2][:, 0:256], scalar1=sg_sb[:, oc:oc + 1],
                    scalar2=None, op0=ALU.mult), r=(PB[2], sg_sb), w=(vq6,))
                se(lambda eng: eng.activation(out=av6[:, 0:256], in_=vq6[:, 0:256],
                                              func=AF.Silu), r=(vq6,), w=(av6,))
                ve(lambda eng, oc=oc: eng.tensor_scalar(
                    out=rr6[:, 0:256], in0=pb[3][:, 0:256], scalar1=su_sb[:, oc:oc + 1],
                    scalar2=None, op0=ALU.mult), r=(PB[3], su_sb), w=(rr6,))
                ve(lambda eng: eng.tensor_tensor(out=vq6[:, 256:512], in0=av6[:, 0:256],
                                                 in1=rr6[:, 0:256], op=ALU.mult),
                   r=(av6, rr6), w=(vq6,))
                ve(lambda eng, c=oc * 256: eng.tensor_copy(out=fhi[:, c:c + 256],
                                                           in_=vq6[:, 256:512]),
                   r=(vq6,), w=(fhi,))
                ve(lambda eng, c=oc * 256: eng.tensor_tensor(
                    out=flo[:, c:c + 256], in0=vq6[:, 256:512], in1=fhi[:, c:c + 256],
                    op=ALU.subtract), r=(vq6, fhi), w=(flo,))
        for ogd in range(4):
            for icp in range(64):
                bi8 = icp % 2
                ctx.dma("sy", w8buf[bi8][:],
                        wd1[icp * 128:(icp + 1) * 128, ogd * 512:(ogd + 1) * 512],
                        w=(w8buf[bi8],))
                ve(lambda eng, b=w8buf[bi8], d=wdb[bi8]: eng.tensor_copy(out=d[:], in_=b[:]),
                   r=(w8buf[bi8],), w=(wdb[bi8],))
                for tb in range(2):
                    for hl, fsrc in ((0, fhi), (1, flo)):
                        mmg(4 + tb, pb[4 + tb][:],
                            fsrc[:, icp * 256 + tb * 128: icp * 256 + tb * 128 + 128],
                            wdb[bi8][:], start=(icp == 0 and hl == 0),
                            stop=(icp == 63 and hl == 1), r=(fsrc, wdb[bi8]))
            for tb in range(2):
                tbix = q4 * 2 + tb
                ve(lambda eng, b=pb[4 + tb], c=ogd * 512:
                   eng.tensor_tensor(out=vq6[:], in0=b[:], in1=sdb[:, c:c + 512],
                                     op=ALU.mult), r=(PB[4 + tb], sdb), w=(vq6,))
                ctx.dma("sy", av6[:], xres[tbix * 128:(tbix + 1) * 128,
                                           ogd * 512:(ogd + 1) * 512],
                        r=(xres,), w=(av6,))
                ve(lambda eng: eng.tensor_tensor(out=out6[:], in0=vq6[:], in1=av6[:],
                                                 op=ALU.add), r=(vq6, av6), w=(out6,))
                ctx.dma("se", y_out[tbix * 128:(tbix + 1) * 128,
                                    ogd * 512:(ogd + 1) * 512], out6[:], r=(out6,))

    for e in ENGS:
        for sname in ctx.val:
            ctx.wait(e, sname, ctx.val[sname])

    es6.close()
    with nc.Block() as block:
        ctx.realize(block)
    esg.close()
    return nc


_PROG_CACHE = {}


def _get_prog():
    import os
    ph = int(os.environ.get("KPHASE", "99"))
    key = ("nc", ph, os.environ.get("KN_HEADS"), os.environ.get("KN_QB"),
           os.environ.get("KN_PART"))
    if key not in _PROG_CACHE:
        _PROG_CACHE[key] = build_program(ph)
    return _PROG_CACHE[key]


def kernel(L0_attn_x, L0_attn_W_q, L0_attn_W_down_kv, L0_attn_W_up_k, L0_attn_W_up_v,
           L0_attn_W_o, L0_ffn_w_gate, L0_ffn_w_up, L0_ffn_w_down,
           L1_attn_wq, L1_attn_wk, L1_attn_wv, L1_attn_wo,
           L1_ffn_w_gate, L1_ffn_w_up, L1_ffn_w_down,
           L1_ffn_s_gate, L1_ffn_s_up, L1_ffn_s_down):
    import ml_dtypes
    nc = _get_prog()
    x = np.ascontiguousarray(np.asarray(L0_attn_x, dtype=np.float32))
    shared = {
        "W_q": np.ascontiguousarray(np.asarray(L0_attn_W_q, dtype=np.float32)),
        "W_dkv": np.ascontiguousarray(np.asarray(L0_attn_W_down_kv, dtype=np.float32)),
        "W_upk": np.ascontiguousarray(np.asarray(L0_attn_W_up_k, dtype=np.float32)),
        "W_upv": np.ascontiguousarray(np.asarray(L0_attn_W_up_v, dtype=np.float32)),
        "W_o": np.ascontiguousarray(np.asarray(L0_attn_W_o, dtype=np.float32)),
        "Wg0": np.ascontiguousarray(np.asarray(L0_ffn_w_gate, dtype=np.float32)),
        "Wu0": np.ascontiguousarray(np.asarray(L0_ffn_w_up, dtype=np.float32)),
        "Wd0": np.ascontiguousarray(np.asarray(L0_ffn_w_down, dtype=np.float32)),
        "wq1": np.ascontiguousarray(np.asarray(L1_attn_wq, dtype=np.float32)),
        "wk1": np.ascontiguousarray(np.asarray(L1_attn_wk, dtype=np.float32)),
        "wv1": np.ascontiguousarray(np.asarray(L1_attn_wv, dtype=np.float32)),
        "wo1": np.ascontiguousarray(np.asarray(L1_attn_wo, dtype=np.float32)),
        "wg1": np.ascontiguousarray(np.asarray(L1_ffn_w_gate, dtype=np.int8)),
        "wu1": np.ascontiguousarray(np.asarray(L1_ffn_w_up, dtype=np.int8)),
        "wd1": np.ascontiguousarray(np.asarray(L1_ffn_w_down, dtype=np.int8)),
        "sg1": np.ascontiguousarray(np.asarray(L1_ffn_s_gate, dtype=np.float32)),
        "su1": np.ascontiguousarray(np.asarray(L1_ffn_s_up, dtype=np.float32)),
        "sd1": np.ascontiguousarray(np.asarray(L1_ffn_s_down, dtype=np.float32)),
        "c_ident": np.eye(128, dtype=np.float32),
        "c_identb": np.eye(128, dtype=ml_dtypes.bfloat16),
        "c_h128": _host_hadamard(),
        "c_ones": np.ones((1, 128), dtype=np.float32),
    }
    in_maps = []
    for c in range(8):
        b, cp = c // 4, c % 4
        cos8, sin8 = _host_rope(c)
        m = dict(shared)
        m["x_own"] = np.ascontiguousarray(x[b, cp::4, :])
        m["c_mask"] = _host_mask(c)
        m["c_cos"] = cos8
        m["c_sin"] = sin8
        in_maps.append(m)
    res = run_bass_kernel_spmd(nc, in_maps, core_ids=list(range(8)))
    out = np.empty((B, S, D), dtype=np.float32)
    for c in range(8):
        b, cp = c // 4, c % 4
        out[b, cp::4, :] = res.results[c]["y_out"]
    return out


# version 08
# speedup vs baseline: 1.1692x; 1.1692x over previous
import numpy as np
from contextlib import ExitStack

import concourse.bass as bass
import concourse.mybir as mybir
from concourse.bass_utils import run_bass_kernel_spmd
from bass_rust import ReduceOp

F32 = mybir.dt.float32
BF16 = mybir.dt.bfloat16
I8 = mybir.dt.int8
AF = mybir.ActivationFunctionType
ALU = mybir.AluOpType
AX = mybir.AxisListType

B = 2
S = 4096
D = 2048
NTOK = 1024
DL0 = 512
FF = 8192
ROPE_THETA = 10000.0

C0_A = np.float32(1.0 / np.sqrt(np.float64(128.0)))
C0_B = np.float32((1.0 / np.sqrt(np.float64(128.0))) / np.float64(C0_A))
CH2048 = np.float32(1.0 / np.sqrt(np.float64(2048.0)))
CH8192 = np.float32(1.0 / np.sqrt(np.float64(8192.0)))
INV254 = np.float32(np.float64(1.0) / np.float64(254.0))
INV2_24 = np.float32(2.0 ** -24)

ENGS = ("te", "ve", "se", "ge", "sy")


class Ctx:

    def __init__(self, nc):
        self.nc = nc
        self.q = {e: [] for e in ENGS}
        self.sems = {}
        self.val = {}
        self.wm = {e: {} for e in ENGS}
        self.bufw = {}
        self.bufr = {}

    def add_sem(self, es, name):
        self.sems[name] = es.enter_context(self.nc.semaphore(name))
        self.val[name] = 0

    def wait(self, e, sname, v):
        if v is None or v <= 0:
            return
        if self.wm[e].get(sname, 0) >= v:
            return
        self.wm[e][sname] = v
        s = self.sems[sname]
        self.q[e].append(lambda eng, s=s, v=v: eng.wait_ge(s, v))

    def waits(self, e, tickets):
        for t in tickets:
            if t is not None:
                self.wait(e, t[0], t[1])

    def op(self, e, fn, r=(), w=(), wait=(), inc=None, amt=1, dr=True):
        if hasattr(self, "psum_ids") and self.psum_ids:
            rr = tuple(b for b in r if id(b) not in self.psum_ids)
            w = tuple(w) + tuple(b for b in r if id(b) in self.psum_ids)
            r = rr
        tickets = list(wait)
        for buf in r:
            tickets.append(self.bufw.get(id(buf)))
        for buf in w:
            tickets.append(self.bufw.get(id(buf)))
            for sname, v in self.bufr.get(id(buf), {}).items():
                tickets.append((sname, v))
        self.waits(e, tickets)
        tk = None
        if inc is not None:
            self.val[inc] += amt
            s = self.sems[inc]
            if e == "te" and dr:
                def run(eng, fn=fn, s=s, amt=amt):
                    fn(eng)
                    eng.maybe_drain_then_inc((s, amt), fusable=True)
                self.q[e].append(run)
            else:
                self.q[e].append(lambda eng, fn=fn, s=s, amt=amt: fn(eng).then_inc(s, amt))
            tk = (inc, self.val[inc])
        else:
            self.q[e].append(lambda eng, fn=fn: fn(eng))
        if tk is not None:
            for buf in r:
                d = self.bufr.setdefault(id(buf), {})
                if d.get(tk[0], 0) < tk[1]:
                    d[tk[0]] = tk[1]
            for buf in w:
                self.bufw[id(buf)] = tk
                self.bufr[id(buf)] = {}
        return tk

    def dma(self, qe, out_ap, in_ap, r=(), w=(), wait=(), accum=False, slow=False):
        kw = {"accum_op": ALU.add} if accum else {}
        if slow:
            kw["allow_slow_non_contiguous"] = True
        chs = self.dchan[qe]
        ch = chs[self.drr[qe] % len(chs)]
        self.drr[qe] += 1
        self.wait(qe, ch, self.val[ch])
        return self.op(
            qe,
            lambda eng, o=out_ap, i=in_ap, kw=kw: eng.dma_start(out=o, in_=i, **kw),
            r=r, w=w, wait=wait, inc=ch, amt=16,
        )

    def barrier(self):
        for e, sname in (("te", "te"), ("ve", "ve"), ("se", "se"), ("ge", "ge")):
            self.op(e, lambda eng: eng.drain(), inc=sname)
        targets = [(s, self.val[s]) for s in self.val]
        for e in ENGS:
            for s, v in targets:
                self.wait(e, s, v)

    def realize(self, block):
        def mk(ops):
            def run(eng):
                for f in ops:
                    f(eng)
            return run
        block.sync(mk(self.q["sy"]))
        block.tensor(mk(self.q["te"]))
        block.vector(mk(self.q["ve"]))
        block.scalar(mk(self.q["se"]))
        block.gpsimd(mk(self.q["ge"]))


def _host_rope(cid):
    cp = cid % 4
    pos = 4.0 * np.arange(NTOK, dtype=np.float64) + cp
    freq = np.float64(ROPE_THETA) ** (-(np.arange(32, dtype=np.float64) * (2.0 / 64.0)))
    ph = pos[:, None] * freq[None, :]
    cos8 = np.tile(np.cos(ph), (1, 8)).astype(np.float32)
    sin8 = np.tile(np.sin(ph), (1, 8)).astype(np.float32)
    return np.ascontiguousarray(cos8), np.ascontiguousarray(sin8)


def _host_mask(cid):
    cp = cid % 4
    i = np.arange(128)[:, None]
    f = np.arange(512)[None, :]
    u = f // 128
    p = f % 128
    kpos = 128 * u + 4 * (p % 32) + p // 32
    qpos = 4 * i + cp
    return np.ascontiguousarray(
        np.where(kpos <= qpos, np.float64(0.0), np.float64(-1e9)).astype(np.float32))


def _host_hadamard():
    h = np.array([[1.0]])
    while h.shape[0] < 128:
        h = np.block([[h, h], [h, -h]])
    return np.ascontiguousarray(h.astype(np.float32))


def _os_early():
    import os
    return os.environ.get("KDUMP")


def build_program(ph=99):
    import os
    kn_heads = int(os.environ.get("KN_HEADS", "16"))
    kn_qb = int(os.environ.get("KN_QB", "8"))
    kn_part = int(os.environ.get("KN_PART", "3"))
    nc = bass.Bass("TRN2", target_bir_lowering=False)

    def EI(name, shape, dt=F32):
        return nc.dram_tensor(name, list(shape), dt, kind="ExternalInput").ap()

    x_own = EI("x_own", [NTOK, D])
    W_q = EI("W_q", [D, D])
    W_dkv = EI("W_dkv", [DL0, D])
    W_upk = EI("W_upk", [16, 128, DL0])
    W_upv = EI("W_upv", [16, 128, DL0])
    W_o = EI("W_o", [D, D])
    Wg0 = EI("Wg0", [FF, D])
    Wu0 = EI("Wu0", [FF, D])
    Wd0 = EI("Wd0", [D, FF])
    wq1 = EI("wq1", [D, D])
    wk1 = EI("wk1", [D, DL0])
    wv1 = EI("wv1", [D, DL0])
    wo1 = EI("wo1", [D, D])
    wg1 = EI("wg1", [D, FF], I8)
    wu1 = EI("wu1", [D, FF], I8)
    wd1 = EI("wd1", [FF, D], I8)
    sg1 = EI("sg1", [FF])
    su1 = EI("su1", [FF])
    sd1 = EI("sd1", [D])
    c_ident = EI("c_ident", [128, 128])
    c_identb = EI("c_identb", [128, 128], BF16)
    c_h128 = EI("c_h128", [128, 128])
    c_mask = EI("c_mask", [128, 512])
    c_cos = EI("c_cos", [NTOK, 256])
    c_sin = EI("c_sin", [NTOK, 256])
    c_ones = EI("c_ones", [1, 128])

    y_out = nc.dram_tensor("y_out", [NTOK, D], F32, kind="ExternalOutput").ap()

    def DT(name, shape, dt=F32):
        return nc.dram_tensor(name, list(shape), dt).ap()

    xres = DT("xres", [NTOK, D])
    qt0 = DT("qt0", [16, 2, 128, 512])
    ot0 = DT("ot0", [16, 8, 128, 128])
    ckv_own = DT("ckv_own", [NTOK, DL0])
    ckv_full = DT("ckv_full", [S, DL0])
    kv_own = DT("kv_own", [NTOK, 2 * DL0])
    kv_full = DT("kv_full", [S, 2 * DL0])
    qt1 = DT("qt1", [16, 8, 128, 128])
    o1b = DT("o1b", [16, 8, 128, 128])
    wgt_b = DT("wgt_b", [64, 128, 16, 128], BF16)
    wut_b = DT("wut_b", [64, 128, 16, 128], BF16)
    wdt_b = DT("wdt_b", [16, 128, 64, 128], BF16)

    esg = ExitStack()
    ctx = Ctx(nc)
    for sname in ("te", "ve", "se", "ge"):
        ctx.add_sem(esg, sname)
    ctx.dchan = {"sy": [], "se": [], "ge": []}
    ctx.drr = {"sy": 0, "se": 0, "ge": 0}
    for qe, nch in (("sy", 4), ("se", 4), ("ge", 2)):
        for i in range(nch):
            nm = f"d{qe}{i}"
            ctx.add_sem(esg, nm)
            ctx.dchan[qe].append(nm)

    def SBg(name, shape, dt=F32):
        return esg.enter_context(nc.sbuf_tensor(name, list(shape), dt))

    ident = SBg("ident", [128, 128])
    maskt = SBg("maskt", [128, 512])
    ones_r = SBg("ones_r", [1, 128])
    eps_t = SBg("eps_t", [128, 1])
    thr_p = [SBg(f"thrp{i}", [128, 1]) for i in range(3)]
    thr_n = [SBg(f"thrn{i}", [128, 1]) for i in range(3)]
    scw = SBg("scw", [128, 4])
    ln_s6 = SBg("ln_s6", [128, 24])
    ln_mv = SBg("ln_mv", [128, 2])
    ln_std = SBg("ln_std", [128, 1])
    ln_rstd = SBg("ln_rstd", [128, 1])
    st_a = SBg("st_a", [128, 4])
    st_b = SBg("st_b", [128, 4])
    st_c = SBg("st_c", [128, 1])
    st_d = SBg("st_d", [128, 1])
    mx_t = SBg("mx_t", [128, 32])
    acc_t = SBg("acc_t", [128, 32])
    bias_t = SBg("bias_t", [128, 1])
    rowt = SBg("rowt", [1, 128])
    rowt2 = SBg("rowt2", [1, 256])
    wsum = SBg("wsum", [128, 224])
    wstat = [SBg(f"wstat{i}", [128, D]) for i in range(2)]

    pb = [esg.enter_context(nc.psum_tensor(f"pb{i}", [128, 512], F32)) for i in range(7)]
    pbT = esg.enter_context(nc.psum_tensor("pbT", [128, 1024], BF16))
    PB = list(pb) + [pbT]
    ctx.psum_ids = {id(b) for b in PB}

    def mmg(bi, out_ap, lhsT_ap, rhs_ap, start, stop, r=(), wait=()):
        return ctx.op("te", lambda eng, o=out_ap, l=lhsT_ap, r_=rhs_ap, st=start, sp=stop:
                      eng.matmul(out=o, lhsT=l, rhs=r_, start=st, stop=sp),
                      r=tuple(r), w=(PB[bi],), wait=wait, inc="te", dr=stop)

    def tr(bi, off, in_ap, r=(), wait=(), idn=None, rows=128):
        ap = pbT[:, off:off + 128] if bi == 7 else pb[bi][:rows, off:off + 128]
        return ctx.op("te", lambda eng, o=ap, i=in_ap, d=(idn if idn is not None else ident):
                      eng.transpose(out=o, in_=i, identity=d[:]),
                      r=tuple(r), w=(PB[bi],), wait=wait, inc="te")

    def ve(fn, r=(), w=(), wait=()):
        return ctx.op("ve", fn, r=r, w=w, wait=wait, inc="ve")

    def se(fn, r=(), w=(), wait=()):
        return ctx.op("se", fn, r=r, w=w, wait=wait, inc="se")

    def ge(fn, r=(), w=(), wait=()):
        return ctx.op("ge", fn, r=r, w=w, wait=wait, inc="ge")

    def emit_ln(htok, xtok, extra_wait=()):
        for qq in range(4):
            ve(lambda eng, qq=qq: eng.bn_stats(out=ln_s6[:, qq * 6:(qq + 1) * 6],
                                               in_=xtok[:, qq * 512:(qq + 1) * 512]),
               r=(xtok,), w=(ln_s6,), wait=extra_wait)
        ve(lambda eng: eng.bn_aggr(out=ln_mv[:], in_=ln_s6[:]), r=(ln_s6,), w=(ln_mv,))
        se(lambda eng: eng.activation(out=ln_std[:], in_=ln_mv[:, 1:2], func=AF.Sqrt,
                                      bias=eps_t[:]), r=(ln_mv, eps_t), w=(ln_std,))
        ve(lambda eng: eng.reciprocal(out=ln_rstd[:], in_=ln_std[:]), r=(ln_std,),
           w=(ln_rstd,))
        return ve(lambda eng: eng.tensor_scalar(out=htok[:], in0=xtok[:],
                                                scalar1=ln_mv[:, 0:1], scalar2=ln_rstd[:],
                                                op0=ALU.subtract, op1=ALU.mult),
                  r=(xtok, ln_mv, ln_rstd), w=(htok,))


    def _finish(stacks, dump=None):
        if dump is not None:
            dv = dump.rearrange("o n p t -> (o n) p t")
            nb_, wid = dv.shape[0], dv.shape[2]
            per = 2048 // wid
            for i in range(nb_):
                ctx.dma("se", y_out[128 * (i // per):128 * (i // per) + 128,
                                    wid * (i % per):wid * (i % per) + wid], dv[i],
                        r=(dump,))
        else:
            for i in range(4):
                ctx.dma("se", y_out[i * 256:(i + 1) * 256, :],
                        xres[i * 256:(i + 1) * 256, :], r=(xres,))
        for e in ENGS:
            for sname in ctx.val:
                ctx.wait(e, sname, ctx.val[sname])
        for st in stacks:
            st.close()
        with nc.Block() as block:
            ctx.realize(block)
        esg.close()
        return nc

    ctx.dma("sy", ident[:], c_ident, w=(ident,))
    ctx.dma("sy", maskt[:], c_mask, w=(maskt,))
    ctx.dma("sy", ones_r[:], c_ones, w=(ones_r,))
    ve(lambda eng: eng.memset(eps_t[:], 1e-5), w=(eps_t,))
    ve(lambda eng: eng.memset(mx_t[:], -1e30), w=(mx_t,))
    ve(lambda eng: eng.memset(acc_t[:], 0.0), w=(acc_t,))
    for i in range(4):
        ctx.dma("se", xres[i * 256:(i + 1) * 256, :], x_own[i * 256:(i + 1) * 256, :],
                w=(xres,))

    es1 = ExitStack()

    def SB1(name, shape, dt=F32):
        return es1.enter_context(nc.sbuf_tensor(name, list(shape), dt))

    x_tok = SB1("x_tok1", [128, D])
    h_tok = SB1("h_tok1", [128, D])
    hT = SB1("hT1", [128, 16 * 1024])
    wq_buf = [SB1(f"wq_buf{i}", [128, D]) for i in range(2)]
    wqT = SB1("wqT", [128, D])
    q_sb = SB1("q_sb1", [128, 512])
    wdT = SB1("wdT", [128, 16 * 512])
    ckv_sb = SB1("ckv_sb", [128, 512])

    for tb in range(8):
        ctx.dma("sy", x_tok[:], x_own[tb * 128:(tb + 1) * 128, :], w=(x_tok,))
        emit_ln(h_tok, x_tok)
        for ic in range(16):
            bi = ic % 2
            tr(bi, 0, h_tok[:, ic * 128:(ic + 1) * 128], r=(h_tok, ident))
            c = ic * 1024 + tb * 128
            ve(lambda eng, b=pb[bi][:, 0:128], c=c:
               eng.tensor_copy(out=hT[:, c:c + 128], in_=b), r=(PB[bi],), w=(hT,))

    col = 0
    wst_cols = {}
    for wi, Wsrc in enumerate((Wg0, Wu0, Wd0)):
        rows, cols = (FF, D) if wi < 2 else (D, FF)
        wst_cols[wi] = (col, col + (rows // 128) * (cols // D))
        for rb in range(rows // 128):
            for cs in range(cols // D):
                bi = col % 2
                ctx.dma("se", wstat[bi][:],
                        Wsrc[rb * 128:(rb + 1) * 128, cs * D:(cs + 1) * D], w=(wstat[bi],))
                ve(lambda eng, b=wstat[bi], c=col:
                   eng.tensor_reduce(out=wsum[:, c:c + 1], in_=b[:], axis=AX.X, op=ALU.add,
                                     apply_absolute_value=True), r=(wstat[bi],), w=(wsum,))
                col += 1
    for wi in range(3):
        c0, c1 = wst_cols[wi]
        ve(lambda eng, c0=c0, c1=c1, wi=wi:
           eng.tensor_reduce(out=st_a[:, wi:wi + 1], in_=wsum[:, c0:c1], axis=AX.X,
                             op=ALU.add), r=(wsum,), w=(st_a,))
        tr(6, 0, st_a[:, wi:wi + 1], r=(st_a, ident), rows=1)
        ve(lambda eng: eng.tensor_copy(out=rowt[:], in_=pb[6][:1, 0:128]), r=(PB[6],),
           w=(rowt,))
        ve(lambda eng: eng.reduce_sum(out=st_c[:1, :], in_=rowt[:], axis=AX.X),
           r=(rowt,), w=(st_c,))
        mmg(6, pb[6][:, 0:1], ones_r[:], st_c[:1, :], start=True, stop=True,
            r=(ones_r, st_c))
        ve(lambda eng, wi=wi: eng.tensor_copy(out=st_b[:, wi:wi + 1], in_=pb[6][:, 0:1]),
           r=(PB[6],), w=(st_b,))
        ve(lambda eng, wi=wi: eng.tensor_scalar(
            out=scw[:, wi:wi + 1], in0=st_b[:, wi:wi + 1], scalar1=float(INV2_24),
            scalar2=float(np.float32(1e-5)), op0=ALU.mult, op1=ALU.max), r=(st_b,), w=(scw,))
        ve(lambda eng, wi=wi: eng.tensor_scalar(
            out=thr_p[wi][:], in0=scw[:, wi:wi + 1], scalar1=float(np.float32(0.5)), scalar2=None,
            op0=ALU.mult), r=(scw,), w=(thr_p[wi],))
        ve(lambda eng, wi=wi: eng.tensor_scalar(
            out=thr_n[wi][:], in0=scw[:, wi:wi + 1], scalar1=float(np.float32(-0.5)), scalar2=None,
            op0=ALU.mult), r=(scw,), w=(thr_n[wi],))

    for oc in range(16):
        wb = wq_buf[oc % 2]
        ctx.dma("sy", wb[:], W_q[oc * 128:(oc + 1) * 128, :], w=(wb,))
        for ic in range(16):
            bi = ic % 2
            tr(bi, 0, wb[:, ic * 128:(ic + 1) * 128], r=(wb, ident))
            ve(lambda eng, b=pb[bi][:, 0:128], c=ic * 128:
               eng.tensor_copy(out=wqT[:, c:c + 128], in_=b), r=(PB[bi],), w=(wqT,))
        for nb in range(2):
            for ic in range(16):
                mmg(2, pb[2][:], wqT[:, ic * 128:(ic + 1) * 128],
                    hT[:, ic * 1024 + nb * 512: ic * 1024 + nb * 512 + 512],
                    start=(ic == 0), stop=(ic == 15), r=(wqT, hT))
            ve(lambda eng: eng.tensor_scalar(out=q_sb[:], in0=pb[2][:], scalar1=float(C0_A),
                                             scalar2=None, op0=ALU.mult),
               r=(PB[2],), w=(q_sb,))
            ctx.dma("se", qt0[oc, nb], q_sb[:], r=(q_sb,), w=(qt0,))

    for lc in range(4):
        ctx.dma("sy", wq_buf[0][:], W_dkv[lc * 128:(lc + 1) * 128, :], w=(wq_buf[0],))
        for ic in range(16):
            bi = ic % 2
            tr(bi, 0, wq_buf[0][:, ic * 128:(ic + 1) * 128], r=(wq_buf[0], ident))
            ve(lambda eng, b=pb[bi][:, 0:128], c=ic * 512 + lc * 128:
               eng.tensor_copy(out=wdT[:, c:c + 128], in_=b), r=(PB[bi],), w=(wdT,))
    for tb in range(8):
        for ic in range(16):
            mmg(3, pb[3][:], hT[:, ic * 1024 + tb * 128: ic * 1024 + tb * 128 + 128],
                wdT[:, ic * 512:(ic + 1) * 512], start=(ic == 0), stop=(ic == 15),
                r=(hT, wdT))
        ve(lambda eng: eng.tensor_copy(out=ckv_sb[:], in_=pb[3][:]), r=(PB[3],),
           w=(ckv_sb,))
        ctx.dma("se", ckv_own[tb * 128:(tb + 1) * 128, :], ckv_sb[:], r=(ckv_sb,),
                w=(ckv_own,))

    ctx.barrier()
    if ph <= 1:
        import os
        return _finish([es1], dump=qt0 if os.environ.get("KDUMP") == "qt0" else None)
    es1.close()

    es3 = ExitStack()

    def SB3(name, shape, dt=F32):
        return es3.enter_context(nc.sbuf_tensor(name, list(shape), dt))

    ckvT = SB3("ckvT", [128, 4 * 4096])
    kT_h = SB3("kT_h", [128, 4096])
    v_h = SB3("v_h", [128, 4096])
    qh = SB3("qh", [128, 1024])
    s_sb = SB3("s_sb", [128, 4096])
    p_sb = SB3("p_sb", [128, 4096])
    pT_sb = SB3("pT_sb", [128, 4096])
    o_sb = SB3("o_sb", [128, 128])
    wup_buf = SB3("wup_buf", [128, 2 * DL0])
    bc_sb = SB3("bc_sb", [128, 128])
    wupkT = SB3("wupkT", [128, 512])
    wupvT = SB3("wupvT", [128, 512])
    ckv_tile = SB3("ckv_tile", [128, 512])

    ge(lambda eng: eng.collective_compute(
        "AllGather", ALU.bypass, replica_groups=[[0, 1, 2, 3], [4, 5, 6, 7]],
        ins=[ckv_own.opt()], outs=[ckv_full.opt()]), r=(ckv_own,), w=(ckv_full,))

    for u in range(32):
        ctx.dma("ge", ckv_tile[:], ckv_full[u * 128:(u + 1) * 128, :],
                r=(ckv_full,), w=(ckv_tile,))
        for lc in range(4):
            bi = lc % 2
            tr(bi, 0, ckv_tile[:, lc * 128:(lc + 1) * 128], r=(ckv_tile, ident))
            ve(lambda eng, b=pb[bi][:, 0:128], c=lc * 4096 + u * 128:
               eng.tensor_copy(out=ckvT[:, c:c + 128], in_=b), r=(PB[bi],), w=(ckvT,))

    if ph <= 2:
        ctx.barrier()
        import os
        if os.environ.get("KDUMP") == "ckv":
            dv = ckv_full.rearrange("(a p) f -> a p f", p=128)
            for i in range(32):
                ctx.dma("se", y_out[128 * (i // 4):128 * (i // 4) + 128,
                                    512 * (i % 4):512 * (i % 4) + 512], dv[i],
                        r=(ckv_full,))
            for e in ENGS:
                for sname in ctx.val:
                    ctx.wait(e, sname, ctx.val[sname])
            es3.close()
            with nc.Block() as block:
                ctx.realize(block)
            esg.close()
            return nc
        return _finish([es3])

    if _os_early() == "ckvT2":
        ctx.barrier()
        for i in range(8):
            ctx.dma("se", y_out[i * 128:(i + 1) * 128, :],
                    ckvT[:, i * 2048:(i + 1) * 2048], r=(ckvT,))
        for e in ENGS:
            for sname in ctx.val:
                ctx.wait(e, sname, ctx.val[sname])
        es3.close()
        with nc.Block() as block:
            ctx.realize(block)
        esg.close()
        return nc

    for h in range(kn_heads):
        ctx.dma("sy", wup_buf[:, 0:512], W_upk[h], w=(wup_buf,))
        ctx.dma("sy", wup_buf[:, 512:1024], W_upv[h], w=(wup_buf,))
        for lc in range(4):
            tr(0, 0, wup_buf[:, lc * 128:(lc + 1) * 128], r=(wup_buf, ident))
            ve(lambda eng, b=pb[0][:, 0:128], c=lc * 128:
               eng.tensor_copy(out=wupkT[:, c:c + 128], in_=b), r=(PB[0],), w=(wupkT,))
            tr(1, 0, wup_buf[:, 512 + lc * 128: 512 + (lc + 1) * 128], r=(wup_buf, ident))
            ve(lambda eng, b=pb[1][:, 0:128], c=lc * 128:
               eng.tensor_copy(out=wupvT[:, c:c + 128], in_=b), r=(PB[1],), w=(wupvT,))
        for kb in range(8):
            for lc in range(4):
                mmg(2, pb[2][:], wupkT[:, lc * 128:(lc + 1) * 128],
                    ckvT[:, lc * 4096 + kb * 512: lc * 4096 + kb * 512 + 512],
                    start=(lc == 0), stop=(lc == 3), r=(wupkT, ckvT))
            ve(lambda eng, c=kb * 512: eng.tensor_copy(out=kT_h[:, c:c + 512], in_=pb[2][:]),
               r=(PB[2],), w=(kT_h,))
        for u in range(32):
            for lc in range(4):
                mmg(3, pb[3][:, 0:128], ckvT[:, lc * 4096 + u * 128: lc * 4096 + u * 128 + 128],
                    wupvT[:, lc * 128:(lc + 1) * 128], start=(lc == 0), stop=(lc == 3),
                    r=(ckvT, wupvT))
            ve(lambda eng, c=u * 128: eng.tensor_copy(out=v_h[:, c:c + 128],
                                                      in_=pb[3][:, 0:128]),
               r=(PB[3],), w=(v_h,))
        ctx.dma("se", qh[:, 0:512], qt0[h, 0], r=(qt0,), w=(qh,))
        ctx.dma("se", qh[:, 512:1024], qt0[h, 1], r=(qt0,), w=(qh,))

        for qb in range(kn_qb):
            for kb in range(8):
                bi = 4 + (kb % 2)
                mmg(bi, pb[bi][:], qh[:, qb * 128:(qb + 1) * 128],
                    kT_h[:, kb * 512:(kb + 1) * 512], start=True, stop=True, r=(qh, kT_h))
                ve(lambda eng, b=pb[bi], c=kb * 512:
                   eng.tensor_copy(out=s_sb[:, c:c + 512], in_=b[:]),
                   r=(PB[bi],), w=(s_sb,))
                ve(lambda eng, c=kb * 512, k=kb:
                   eng.reduce_max(out=mx_t[:, k:k + 1], in_=s_sb[:, c:c + 512], axis=AX.X),
                   r=(s_sb,), w=(mx_t,))
            ve(lambda eng: eng.reduce_max(out=st_c[:], in_=mx_t[:, 0:32], axis=AX.X),
               r=(mx_t,), w=(st_c,))
            ve(lambda eng: eng.tensor_scalar(out=bias_t[:], in0=st_c[:],
                                             scalar1=float(-C0_B), scalar2=None,
                                             op0=ALU.mult), r=(st_c,), w=(bias_t,))
            for kb in range(8):
                se(lambda eng, c=kb * 512, k=kb: eng.activation(
                    out=p_sb[:, c:c + 512], in_=s_sb[:, c:c + 512], func=AF.Exp,
                    bias=bias_t[:], scale=float(C0_B), accum_out=acc_t[:, k:k + 1]),
                    r=(s_sb, bias_t), w=(p_sb, acc_t))
            for u in range(32):
                off = (u % 4) * 128
                tr(2, off, p_sb[:, u * 128:(u + 1) * 128], r=(p_sb, ident))
                ve(lambda eng, o=off, c=u * 128: eng.tensor_copy(
                    out=pT_sb[:, c:c + 128], in_=pb[2][:, o:o + 128]),
                   r=(PB[2],), w=(pT_sb,))
            for u in range(32):
                mmg(6, pb[6][:, 0:128], v_h[:, u * 128:(u + 1) * 128],
                    pT_sb[:, u * 128:(u + 1) * 128], start=(u == 0), stop=(u == 31),
                    r=(v_h, pT_sb))
            ve(lambda eng: eng.reduce_sum(out=st_c[:], in_=acc_t[:, 0:32], axis=AX.X),
               r=(acc_t,), w=(st_c,))
            ve(lambda eng: eng.reciprocal(out=st_d[:], in_=st_c[:]), r=(st_c,), w=(st_d,))
            tr(3, 0, st_d[:], r=(st_d, ident), rows=1)
            ve(lambda eng: eng.tensor_copy(out=rowt[:], in_=pb[3][:1, 0:128]),
               r=(PB[3],), w=(rowt,))
            mmg(3, pb[3][:, 128:256], ones_r[:], rowt[:], start=True, stop=True,
                r=(ones_r, rowt))
            ve(lambda eng: eng.tensor_copy(out=bc_sb[:], in_=pb[3][:, 128:256]),
               r=(PB[3],), w=(bc_sb,))
            ve(lambda eng: eng.tensor_tensor(out=o_sb[:], in0=pb[6][:, 0:128],
                                             in1=bc_sb[:], op=ALU.mult),
               r=(PB[6], bc_sb), w=(o_sb,))
            ctx.dma("se", ot0[h, qb], o_sb[:], r=(o_sb,), w=(ot0,))

    import os as _os
    if _os.environ.get("KDUMP") == "ckvT":
        ctx.barrier()
        for i in range(8):
            ctx.dma("se", y_out[i * 128:(i + 1) * 128, :],
                    ckvT[:, i * 2048:(i + 1) * 2048], r=(ckvT,))
        for e in ENGS:
            for sname in ctx.val:
                ctx.wait(e, sname, ctx.val[sname])
        es3.close()
        with nc.Block() as block:
            ctx.realize(block)
        esg.close()
        return nc
    if _os.environ.get("KDUMP") == "kv":
        for a_ in range(8):
            ctx.dma("se", ckv_own[a_ * 128:(a_ + 1) * 128, :],
                    kT_h[:, a_ * 512:(a_ + 1) * 512], r=(kT_h,), w=(ckv_own,))
        for a_ in range(4):
            ctx.dma("se", kv_own[a_ * 128:(a_ + 1) * 128, :],
                    v_h[:, a_ * 1024:(a_ + 1) * 1024], r=(v_h,), w=(kv_own,))
        ctx.barrier()
        ctx.dma("se", y_out[0:256, :], ckv_own.rearrange("(a b) f -> a (b f)", b=4),
                r=(ckv_own,))
        ctx.dma("se", y_out[256:512, :], kv_own[0:512, :].rearrange("(a b) f -> a (b f)", b=2),
                r=(kv_own,))
        for e in ENGS:
            for sname in ctx.val:
                ctx.wait(e, sname, ctx.val[sname])
        es3.close()
        with nc.Block() as block:
            ctx.realize(block)
        esg.close()
        return nc
    ctx.barrier()
    if ph <= 3:
        import os
        return _finish([es3], dump=ot0 if os.environ.get("KDUMP") == "ot0" else None)
    es3.close()

    es3b = ExitStack()

    def SB3b(name, shape, dt=F32):
        return es3b.enter_context(nc.sbuf_tensor(name, list(shape), dt))

    woT = SB3b("woT", [128, 16 * 512])
    xr_sb = SB3b("xr_sb", [128, 512])
    oT_buf = SB3b("oT_buf", [128, 2048])
    wo_out = SB3b("wo_out", [128, 512])
    wo_buf = [SB3b(f"wo_buf{i}", [128, D]) for i in range(2)]

    for og in range(4):
        for ocr in range(4):
            wb = wo_buf[ocr % 2]
            ctx.dma("sy", wb[:], W_o[og * 512 + ocr * 128: og * 512 + (ocr + 1) * 128, :],
                    w=(wb,))
            for ic in range(16):
                bi = ic % 2
                tr(bi, 0, wb[:, ic * 128:(ic + 1) * 128], r=(wb, ident))
                ve(lambda eng, b=pb[bi][:, 0:128], c=ic * 512 + ocr * 128:
                   eng.tensor_copy(out=woT[:, c:c + 128], in_=b), r=(PB[bi],), w=(woT,))
        for tb in range(8):
            for ic in range(16):
                ctx.dma("se", oT_buf[:, ic * 128:(ic + 1) * 128], ot0[ic, tb],
                        r=(ot0,), w=(oT_buf,))
            bi = 2 + (tb % 2)
            for ic in range(16):
                mmg(bi, pb[bi][:], oT_buf[:, ic * 128:(ic + 1) * 128],
                    woT[:, ic * 512:(ic + 1) * 512], start=(ic == 0), stop=(ic == 15),
                    r=(oT_buf, woT))
            ctx.dma("se", xr_sb[:], xres[tb * 128:(tb + 1) * 128, og * 512:(og + 1) * 512],
                    r=(xres,), w=(xr_sb,))
            ve(lambda eng, b=pb[bi]: eng.tensor_tensor(out=wo_out[:], in0=b[:],
                                                       in1=xr_sb[:], op=ALU.add),
               r=(PB[bi], xr_sb), w=(wo_out,))
            ctx.dma("se", xres[tb * 128:(tb + 1) * 128, og * 512:(og + 1) * 512],
                    wo_out[:], r=(wo_out,), w=(xres,))

    ctx.barrier()
    if ph <= 4:
        return _finish([es3b])
    es3b.close()

    es4a = ExitStack()

    def SB4a(name, shape, dt=F32):
        return es4a.enter_context(nc.sbuf_tensor(name, list(shape), dt))

    identb = SB4a("identb", [128, 128], BF16)
    m_buf = SB4a("m_buf", [128, D], BF16)
    sg_a = SB4a("sg_a", [128, D])
    sg_b = SB4a("sg_b", [128, D])
    mT_sb = SB4a("mT_sb", [128, D], BF16)
    ctx.dma("sy", identb[:], c_identb, w=(identb,))

    for wi, (Wsrc, Wdst, rows, cols) in enumerate(
            ((Wg0, wgt_b, FF, D), (Wu0, wut_b, FF, D), (Wd0, wdt_b, D, FF))):
        for rb in range(rows // 128):
            for cs in range(cols // D):
                bi = (rb * (cols // D) + cs) % 2
                ctx.dma("sy", wstat[bi][:],
                        Wsrc[rb * 128:(rb + 1) * 128, cs * D:(cs + 1) * D], w=(wstat[bi],))
                se(lambda eng, b=wstat[bi], w=wi: eng.activation(
                    out=sg_a[:], in_=b[:], func=AF.Sign, bias=thr_n[w][:]),
                    r=(wstat[bi], thr_n[wi]), w=(sg_a,))
                se(lambda eng, b=wstat[bi], w=wi: eng.activation(
                    out=sg_b[:], in_=b[:], func=AF.Sign, bias=thr_p[w][:]),
                    r=(wstat[bi], thr_p[wi]), w=(sg_b,))
                ve(lambda eng: eng.tensor_tensor(out=m_buf[:], in0=sg_a[:], in1=sg_b[:],
                                                 op=ALU.add), r=(sg_a, sg_b), w=(m_buf,))
                for ic in range(16):
                    off = (ic % 8) * 128
                    tr(7, off, m_buf[:, ic * 128:(ic + 1) * 128], r=(m_buf, identb),
                       idn=identb)
                    ve(lambda eng, o=off, c=ic * 128:
                       eng.tensor_copy(out=mT_sb[:, c:c + 128], in_=pbT[:, o:o + 128]),
                       r=(PB[7],), w=(mT_sb,))
                    ctx.dma("se", Wdst[rb, :, cs * 16 + ic, :],
                            mT_sb[:, ic * 128:(ic + 1) * 128], r=(mT_sb,),
                            w=(Wdst,))

    ctx.barrier()
    if ph <= 5:
        return _finish([es4a])
    es4a.close()

    es4b = ExitStack()

    def SB4b(name, shape, dt=F32):
        return es4b.enter_context(nc.sbuf_tensor(name, list(shape), dt))

    h128 = SB4b("h128", [128, 128])
    x_tok4 = SB4b("x_tok4", [128, D])
    h_tok4 = SB4b("h_tok4", [128, D])
    hT2 = SB4b("hT2", [128, 16 * 256])
    k1 = SB4b("k1", [128, 16 * 256], BF16)
    t_sb = SB4b("t_sb", [128, 64 * 256])
    k2 = SB4b("k2", [128, 64 * 256], BF16)
    cmx = SB4b("cmx", [128, 256])
    cmb = SB4b("cmb", [128, 256])
    sbc = SB4b("sbc", [128, 256])
    comb = SB4b("comb", [128, 256])
    vq = SB4b("vq", [128, 256])
    avx = SB4b("avx", [128, 256])
    rrx = SB4b("rrx", [128, 256])
    sgn = SB4b("sgn", [128, 256])
    ggx = SB4b("ggx", [128, 256])
    wg_sb = SB4b("wg_sb", [128, 16 * 128], BF16)
    wu_sb = SB4b("wu_sb", [128, 16 * 128], BF16)
    wd_sb = SB4b("wd_sb", [128, 64 * 128], BF16)
    ot_stage = SB4b("ot_stage", [128, 128])
    ctx.dma("sy", h128[:], c_h128, w=(h128,))

    def emit_quant(src_sb, nchunk, ch_norm, k_dst):
        ve(lambda eng: eng.scalar_tensor_tensor(out=cmx[:], in0=src_sb[:, 0:256],
                                                scalar=-1.0, in1=src_sb[:, 0:256],
                                                op0=ALU.mult, op1=ALU.max),
           r=(src_sb,), w=(cmx,))
        for icx in range(1, nchunk):
            c = icx * 256
            ve(lambda eng, c=c: eng.scalar_tensor_tensor(
                out=avx[:], in0=src_sb[:, c:c + 256], scalar=-1.0,
                in1=src_sb[:, c:c + 256], op0=ALU.mult, op1=ALU.max),
                r=(src_sb,), w=(avx,))
            ve(lambda eng: eng.tensor_tensor(out=cmx[:], in0=cmx[:], in1=avx[:],
                                             op=ALU.max), r=(cmx, avx), w=(cmx,))
        for half in range(2):
            tr(6, 0, cmx[:, half * 128:(half + 1) * 128], r=(cmx, ident))
            ve(lambda eng: eng.reduce_max(out=st_c[:], in_=pb[6][:, 0:128], axis=AX.X),
               r=(PB[6],), w=(st_c,))
            tr(6, 128, st_c[:], r=(st_c, ident), rows=1)
            ve(lambda eng, half=half: eng.tensor_copy(out=rowt2[:, half * 128:(half + 1) * 128],
                                                      in_=pb[6][:1, 128:256]),
               r=(PB[6],), w=(rowt2,))
        mmg(6, pb[6][:, 0:256], ones_r[:], rowt2[:], start=True, stop=True,
            r=(ones_r, rowt2))
        ve(lambda eng: eng.tensor_copy(out=cmb[:], in_=pb[6][:, 0:256]), r=(PB[6],),
           w=(cmb,))
        ve(lambda eng: eng.tensor_scalar(out=cmx[:], in0=cmb[:], scalar1=float(ch_norm),
                                         scalar2=float(np.float32(1e-5)), op0=ALU.mult,
                                         op1=ALU.max), r=(cmb,), w=(cmx,))
        ve(lambda eng: eng.reciprocal(out=cmb[:], in_=cmx[:]), r=(cmx,), w=(cmb,))
        ve(lambda eng: eng.tensor_scalar(out=sbc[:], in0=cmb[:], scalar1=float(ch_norm),
                                         scalar2=float(np.float32(127.0)), op0=ALU.mult,
                                         op1=ALU.mult), r=(cmb,), w=(sbc,))
        ve(lambda eng: eng.tensor_scalar(out=comb[:], in0=cmx[:], scalar1=float(INV254),
                                         scalar2=None, op0=ALU.mult), r=(cmx,), w=(comb,))
        for icx in range(nchunk):
            c = icx * 256
            ve(lambda eng, c=c: eng.tensor_tensor(out=vq[:], in0=src_sb[:, c:c + 256],
                                                  in1=sbc[:], op=ALU.mult),
               r=(src_sb, sbc), w=(vq,))
            se(lambda eng: eng.activation(out=sgn[:], in_=vq[:], func=AF.Sign),
               r=(vq,), w=(sgn,))
            ve(lambda eng: eng.scalar_tensor_tensor(out=avx[:], in0=vq[:], scalar=-1.0,
                                                    in1=vq[:], op0=ALU.mult, op1=ALU.max),
               r=(vq,), w=(avx,))
            ve(lambda eng: eng.tensor_scalar(out=avx[:], in0=avx[:], scalar1=0.5,
                                             scalar2=None, op0=ALU.add), r=(avx,), w=(avx,))
            ve(lambda eng: eng.tensor_copy(out=rrx[:], in_=avx[:]), r=(avx,), w=(rrx,))
            for bb in (64.0, 32.0, 16.0, 8.0, 4.0, 2.0, 1.0):
                ve(lambda eng, bb=bb: eng.tensor_scalar(out=ggx[:], in0=rrx[:], scalar1=bb,
                                                        scalar2=None, op0=ALU.is_ge),
                   r=(rrx,), w=(ggx,))
                ve(lambda eng, bb=bb: eng.scalar_tensor_tensor(out=rrx[:], in0=ggx[:],
                                                               scalar=-bb, in1=rrx[:],
                                                               op0=ALU.mult, op1=ALU.add),
                   r=(ggx, rrx), w=(rrx,))
            ve(lambda eng: eng.tensor_tensor(out=avx[:], in0=avx[:], in1=rrx[:],
                                             op=ALU.subtract), r=(avx, rrx), w=(avx,))
            ve(lambda eng, c=c: eng.tensor_tensor(out=k_dst[:, c:c + 256], in0=avx[:],
                                                  in1=sgn[:], op=ALU.mult),
               r=(avx, sgn), w=(k_dst,))

    for ttile in range(4):
        for tbb in range(2):
            tbix = ttile * 2 + tbb
            ctx.dma("sy", x_tok4[:], xres[tbix * 128:(tbix + 1) * 128, :],
                    r=(xres,), w=(x_tok4,))
            emit_ln(h_tok4, x_tok4)
            for ic in range(16):
                bi = ic % 2
                tr(bi, 0, h_tok4[:, ic * 128:(ic + 1) * 128], r=(h_tok4, ident))
                ve(lambda eng, b=pb[bi][:, 0:128], c=ic * 256 + tbb * 128:
                   eng.tensor_copy(out=hT2[:, c:c + 128], in_=b), r=(PB[bi],), w=(hT2,))
        for stg in (1, 2, 4, 8):
            for icx in range(16):
                if icx & stg:
                    continue
                a = icx * 256
                b_ = (icx ^ stg) * 256
                ve(lambda eng, a=a, b_=b_: eng.tensor_tensor(
                    out=hT2[:, a:a + 256], in0=hT2[:, a:a + 256], in1=hT2[:, b_:b_ + 256],
                    op=ALU.add), r=(hT2,), w=(hT2,))
                ve(lambda eng, a=a, b_=b_: eng.scalar_tensor_tensor(
                    out=hT2[:, b_:b_ + 256], in0=hT2[:, b_:b_ + 256],
                    scalar=float(np.float32(-2.0)), in1=hT2[:, a:a + 256],
                    op0=ALU.mult, op1=ALU.add), r=(hT2,), w=(hT2,))
        for icx in range(16):
            bi = icx % 2
            mmg(bi, pb[bi][:, 0:256], h128[:], hT2[:, icx * 256:(icx + 1) * 256],
                start=True, stop=True, r=(h128, hT2))
            ve(lambda eng, b=pb[bi], c=icx * 256:
               eng.tensor_copy(out=hT2[:, c:c + 256], in_=b[:, 0:256]),
               r=(PB[bi],), w=(hT2,))
        emit_quant(hT2, 16, CH2048, k1)
        for oc in range(64):
            ctx.dma("sy", wg_sb[:], wgt_b[oc], w=(wg_sb,))
            ctx.dma("sy", wu_sb[:], wut_b[oc], w=(wu_sb,))
            for ic in range(16):
                mmg(2, pb[2][:, 0:256], wg_sb[:, ic * 128:(ic + 1) * 128],
                    k1[:, ic * 256:(ic + 1) * 256], start=(ic == 0), stop=(ic == 15),
                    r=(wg_sb, k1))
                mmg(3, pb[3][:, 0:256], wu_sb[:, ic * 128:(ic + 1) * 128],
                    k1[:, ic * 256:(ic + 1) * 256], start=(ic == 0), stop=(ic == 15),
                    r=(wu_sb, k1))
            ve(lambda eng: eng.scalar_tensor_tensor(
                out=vq[:], in0=pb[2][:, 0:256], scalar=scw[:, 0:1], in1=comb[:],
                op0=ALU.mult, op1=ALU.mult), r=(PB[2], scw, comb), w=(vq,))
            se(lambda eng: eng.activation(out=avx[:], in_=vq[:], func=AF.Silu),
               r=(vq,), w=(avx,))
            ve(lambda eng: eng.scalar_tensor_tensor(
                out=rrx[:], in0=pb[3][:, 0:256], scalar=scw[:, 1:2], in1=comb[:],
                op0=ALU.mult, op1=ALU.mult), r=(PB[3], scw, comb), w=(rrx,))
            ve(lambda eng, c=oc * 256: eng.tensor_tensor(
                out=t_sb[:, c:c + 256], in0=avx[:], in1=rrx[:], op=ALU.mult),
                r=(avx, rrx), w=(t_sb,))
        for stg in (1, 2, 4, 8, 16, 32):
            for icx in range(64):
                if icx & stg:
                    continue
                a = icx * 256
                b_ = (icx ^ stg) * 256
                ve(lambda eng, a=a, b_=b_: eng.tensor_tensor(
                    out=t_sb[:, a:a + 256], in0=t_sb[:, a:a + 256], in1=t_sb[:, b_:b_ + 256],
                    op=ALU.add), r=(t_sb,), w=(t_sb,))
                ve(lambda eng, a=a, b_=b_: eng.scalar_tensor_tensor(
                    out=t_sb[:, b_:b_ + 256], in0=t_sb[:, b_:b_ + 256],
                    scalar=float(np.float32(-2.0)), in1=t_sb[:, a:a + 256],
                    op0=ALU.mult, op1=ALU.add), r=(t_sb,), w=(t_sb,))
        for icx in range(64):
            bi = icx % 2
            mmg(bi, pb[bi][:, 0:256], h128[:], t_sb[:, icx * 256:(icx + 1) * 256],
                start=True, stop=True, r=(h128, t_sb))
            ve(lambda eng, b=pb[bi], c=icx * 256:
               eng.tensor_copy(out=t_sb[:, c:c + 256], in_=b[:, 0:256]),
               r=(PB[bi],), w=(t_sb,))
        emit_quant(t_sb, 64, CH8192, k2)
        for tbb in range(2):
            tbix = ttile * 2 + tbb
            ctx.dma("sy", hT2[:, tbb * 2048:(tbb + 1) * 2048],
                    xres[tbix * 128:(tbix + 1) * 128, :], r=(xres,), w=(hT2,))
        for ocp in range(16):
            ctx.dma("sy", wd_sb[:], wdt_b[ocp], w=(wd_sb,))
            for icp in range(64):
                mmg(4, pb[4][:, 0:256], wd_sb[:, icp * 128:(icp + 1) * 128],
                    k2[:, icp * 256:(icp + 1) * 256], start=(icp == 0), stop=(icp == 63),
                    r=(wd_sb, k2))
            ve(lambda eng: eng.scalar_tensor_tensor(
                out=vq[:], in0=pb[4][:, 0:256], scalar=scw[:, 2:3], in1=comb[:],
                op0=ALU.mult, op1=ALU.mult), r=(PB[4], scw, comb), w=(vq,))
            for tbb in range(2):
                tr(5, 0, vq[:, tbb * 128:(tbb + 1) * 128], r=(vq, ident))
                ve(lambda eng, tbb=tbb, c=ocp * 128: eng.tensor_tensor(
                    out=hT2[:, tbb * 2048 + c: tbb * 2048 + c + 128], in0=pb[5][:, 0:128],
                    in1=hT2[:, tbb * 2048 + c: tbb * 2048 + c + 128], op=ALU.add),
                   r=(PB[5], hT2), w=(hT2,))

        for tbb in range(2):
            tbix = ttile * 2 + tbb
            ctx.dma("se", xres[tbix * 128:(tbix + 1) * 128, :],
                    hT2[:, tbb * 2048:(tbb + 1) * 2048], r=(hT2,), w=(xres,))

    ctx.barrier()
    if ph <= 6:
        return _finish([es4b])
    es4b.close()

    es5a = ExitStack()

    def SB5a(name, shape, dt=F32):
        return es5a.enter_context(nc.sbuf_tensor(name, list(shape), dt))

    x_tok5 = SB5a("x_tok5", [128, D])
    h_tok5 = SB5a("h_tok5", [128, D])
    hT5 = SB5a("hT5", [128, 16 * 1024])
    wkv_sb = SB5a("wkv_sb", [128, 16 * 512])
    cs_cos = SB5a("cs_cos", [128, 8 * 256])
    cs_sin = SB5a("cs_sin", [128, 8 * 256])
    ro_a = SB5a("ro_a", [128, 256])
    ro_b = SB5a("ro_b", [128, 256])
    ro_out = SB5a("ro_out", [128, 512])
    qT_stage = SB5a("qT_stage", [128, 128])

    for tb in range(8):
        ctx.dma("sy", cs_cos[:, tb * 256:(tb + 1) * 256], c_cos[tb * 128:(tb + 1) * 128, :],
                w=(cs_cos,))
        ctx.dma("sy", cs_sin[:, tb * 256:(tb + 1) * 256], c_sin[tb * 128:(tb + 1) * 128, :],
                w=(cs_sin,))

    for tb in range(8):
        ctx.dma("sy", x_tok5[:], xres[tb * 128:(tb + 1) * 128, :], r=(xres,), w=(x_tok5,))
        emit_ln(h_tok5, x_tok5)
        for ic in range(16):
            bi = ic % 2
            tr(bi, 0, h_tok5[:, ic * 128:(ic + 1) * 128], r=(h_tok5, ident))
            ve(lambda eng, b=pb[bi][:, 0:128], c=ic * 1024 + tb * 128:
               eng.tensor_copy(out=hT5[:, c:c + 128], in_=b), r=(PB[bi],), w=(hT5,))

    def emit_rope(bank, tb, out_buf):
        c0 = tb * 256
        ve(lambda eng, b=pb[bank]: eng.tensor_tensor(
            out=ro_a[:], in0=b[:, 0::2], in1=cs_cos[:, c0:c0 + 256], op=ALU.mult),
            r=(PB[bank], cs_cos), w=(ro_a,))
        ve(lambda eng, b=pb[bank]: eng.tensor_tensor(
            out=ro_b[:], in0=b[:, 1::2], in1=cs_sin[:, c0:c0 + 256], op=ALU.mult),
            r=(PB[bank], cs_sin), w=(ro_b,))
        ve(lambda eng: eng.tensor_tensor(
            out=out_buf[:, 0::2], in0=ro_a[:], in1=ro_b[:], op=ALU.subtract),
            r=(ro_a, ro_b), w=(out_buf,))
        ve(lambda eng, b=pb[bank]: eng.tensor_tensor(
            out=ro_a[:], in0=b[:, 0::2], in1=cs_sin[:, c0:c0 + 256], op=ALU.mult),
            r=(PB[bank], cs_sin), w=(ro_a,))
        ve(lambda eng, b=pb[bank]: eng.tensor_tensor(
            out=ro_b[:], in0=b[:, 1::2], in1=cs_cos[:, c0:c0 + 256], op=ALU.mult),
            r=(PB[bank], cs_cos), w=(ro_b,))
        return ve(lambda eng: eng.tensor_tensor(
            out=out_buf[:, 1::2], in0=ro_a[:], in1=ro_b[:], op=ALU.add),
            r=(ro_a, ro_b), w=(out_buf,))

    for ic in range(16):
        ctx.dma("sy", wkv_sb[:, ic * 512:(ic + 1) * 512], wk1[ic * 128:(ic + 1) * 128, :],
                w=(wkv_sb,))
    for tb in range(8):
        for ic in range(16):
            mmg(2, pb[2][:], hT5[:, ic * 1024 + tb * 128: ic * 1024 + tb * 128 + 128],
                wkv_sb[:, ic * 512:(ic + 1) * 512], start=(ic == 0), stop=(ic == 15),
                r=(hT5, wkv_sb))
        emit_rope(2, tb, ro_out)
        ctx.dma("se", kv_own[tb * 128:(tb + 1) * 128, 0:512], ro_out[:], r=(ro_out,),
                w=(kv_own,))
    for ic in range(16):
        ctx.dma("sy", wkv_sb[:, ic * 512:(ic + 1) * 512], wv1[ic * 128:(ic + 1) * 128, :],
                w=(wkv_sb,))
    for tb in range(8):
        for ic in range(16):
            mmg(2, pb[2][:], hT5[:, ic * 1024 + tb * 128: ic * 1024 + tb * 128 + 128],
                wkv_sb[:, ic * 512:(ic + 1) * 512], start=(ic == 0), stop=(ic == 15),
                r=(hT5, wkv_sb))
        ve(lambda eng: eng.tensor_copy(out=ro_out[:], in_=pb[2][:]), r=(PB[2],),
           w=(ro_out,))
        ctx.dma("se", kv_own[tb * 128:(tb + 1) * 128, 512:1024], ro_out[:], r=(ro_out,),
                w=(kv_own,))
    ge(lambda eng: eng.collective_compute(
        "AllGather", ALU.bypass, replica_groups=[[0, 1, 2, 3], [4, 5, 6, 7]],
        ins=[kv_own.opt()], outs=[kv_full.opt()]), r=(kv_own,), w=(kv_full,))

    for og in range(4):
        for ic in range(16):
            ctx.dma("sy", wkv_sb[:, ic * 512:(ic + 1) * 512],
                    wq1[ic * 128:(ic + 1) * 128, og * 512:(og + 1) * 512],
                    w=(wkv_sb,))
        for tb in range(8):
            for ic in range(16):
                mmg(2, pb[2][:], hT5[:, ic * 1024 + tb * 128: ic * 1024 + tb * 128 + 128],
                    wkv_sb[:, ic * 512:(ic + 1) * 512], start=(ic == 0), stop=(ic == 15),
                    r=(hT5, wkv_sb))
            emit_rope(2, tb, ro_out)
            for j in range(4):
                tr(3, 0, ro_out[:, j * 128:(j + 1) * 128], r=(ro_out, ident))
                ve(lambda eng: eng.tensor_copy(out=qT_stage[:], in_=pb[3][:, 0:128]),
                   r=(PB[3],), w=(qT_stage,))
                ctx.dma("se", qt1[og * 4 + j, tb], qT_stage[:], r=(qT_stage,),
                        w=(qt1,))

    ctx.barrier()
    if ph <= 7:
        return _finish([es5a])
    es5a.close()

    es5b = ExitStack()

    def SB5b(name, shape, dt=F32):
        return es5b.enter_context(nc.sbuf_tensor(name, list(shape), dt))

    kT_sb = SB5b("kT_sb", [128, 4 * 4096])
    v_sb = SB5b("v_sb", [128, 32 * 512])
    q_sb5 = SB5b("q_sb5", [128, 1024])
    s_a = SB5b("s_a", [128, 4096])
    s_b = SB5b("s_b", [128, 4096])
    p5_sb = SB5b("p5_sb", [128, 512])
    pT5 = SB5b("pT5", [128, 256])
    o_sb5 = SB5b("o_sb5", [128, 128])
    bc_sb5 = SB5b("bc_sb5", [128, 128])
    k_tile = SB5b("k_tile", [128, 512])

    kvr = kv_full.rearrange("(r u s) f -> u r s f", r=4, s=32)
    for u in range(32):
        ctx.dma("ge", k_tile[:], kvr[u, :, :, 0:512], r=(kv_full,), w=(k_tile,))
        ctx.dma("ge", v_sb[:, u * 512:(u + 1) * 512], kvr[u, :, :, 512:1024],
                r=(kv_full,), w=(v_sb,))
        for fc in range(4):
            bi = fc % 2
            tr(bi, 0, k_tile[:, fc * 128:(fc + 1) * 128], r=(k_tile, ident))
            ve(lambda eng, b=pb[bi][:, 0:128], c=fc * 4096 + u * 128:
               eng.tensor_copy(out=kT_sb[:, c:c + 128], in_=b), r=(PB[bi],),
               w=(kT_sb,))

    pairs = [h for a in range(4) for h in (8 * a, 8 * a + 1, 8 * a + 2, 8 * a + 3)]
    for h in pairs:
        j = h // 4
        fc = j // 2
        oc_a, sub_a = h // 2, 64 * (h % 2)
        hb = h + 4
        oc_b, sub_b = hb // 2, 64 * (hb % 2)
        for tb in range(8):
            ctx.dma("se", q_sb5[0:64, tb * 128:(tb + 1) * 128],
                    qt1[oc_a, tb, sub_a:sub_a + 64, :], r=(qt1,), w=(q_sb5,))
            ctx.dma("se", q_sb5[64:128, tb * 128:(tb + 1) * 128],
                    qt1[oc_b, tb, sub_b:sub_b + 64, :], r=(qt1,),
                    w=(q_sb5,))
        for qt in range(8):
            nch = qt + 1
            for u5 in range(nch):
                mmg(4, pb[4][:], q_sb5[0:64, qt * 128:(qt + 1) * 128],
                    kT_sb[0:64, fc * 4096 + u5 * 512: fc * 4096 + u5 * 512 + 512],
                    start=True, stop=True, r=(q_sb5, kT_sb))
                mmg(5, pb[5][:], q_sb5[64:128, qt * 128:(qt + 1) * 128],
                    kT_sb[64:128, fc * 4096 + u5 * 512: fc * 4096 + u5 * 512 + 512],
                    start=True, stop=True, r=(q_sb5, kT_sb))
                if u5 == qt:
                    ve(lambda eng, c=u5 * 512: eng.tensor_tensor(
                        out=s_a[:, c:c + 512], in0=pb[4][:], in1=maskt[:], op=ALU.add),
                        r=(PB[4], maskt), w=(s_a,))
                    ve(lambda eng, c=u5 * 512: eng.tensor_tensor(
                        out=s_b[:, c:c + 512], in0=pb[5][:], in1=maskt[:], op=ALU.add),
                        r=(PB[5], maskt), w=(s_b,))
                else:
                    se(lambda eng, c=u5 * 512: eng.copy(out=s_a[:, c:c + 512],
                                                          in_=pb[4][:]),
                       r=(PB[4],), w=(s_a,))
                    se(lambda eng, c=u5 * 512: eng.copy(out=s_b[:, c:c + 512],
                                                          in_=pb[5][:]),
                       r=(PB[5],), w=(s_b,))
            for s_x, trb, avb, vj, oc_x, sub_x in ((s_a, 2, 0, j, oc_a, sub_a),
                                                   (s_b, 3, 1, j + 1, oc_b, sub_b)):
                ve(lambda eng: eng.memset(mx_t[:], -1e30), w=(mx_t,))
                ve(lambda eng: eng.memset(acc_t[:], 0.0), w=(acc_t,))
                for u5 in range(nch):
                    ve(lambda eng, c=u5 * 512, k=u5, s_x=s_x: eng.reduce_max(
                        out=mx_t[:, k:k + 1], in_=s_x[:, c:c + 512], axis=AX.X),
                        r=(s_x,), w=(mx_t,))
                ve(lambda eng, n=nch: eng.reduce_max(out=st_c[:], in_=mx_t[:, 0:n],
                                                     axis=AX.X), r=(mx_t,), w=(st_c,))
                ve(lambda eng: eng.tensor_scalar(
                    out=bias_t[:], in0=st_c[:], scalar1=float(np.float32(-0.125)), scalar2=None,
                    op0=ALU.mult), r=(st_c,), w=(bias_t,))
                for u5 in range(nch):
                    se(lambda eng, c=u5 * 512, k=u5, s_x=s_x: eng.activation(
                        out=p5_sb[:], in_=s_x[:, c:c + 512], func=AF.Exp, bias=bias_t[:],
                        scale=float(np.float32(0.125)), accum_out=acc_t[:, k:k + 1]),
                        r=(s_x, bias_t), w=(p5_sb, acc_t))
                    for jj in range(4):
                        u = u5 * 4 + jj
                        off = (jj % 2) * 128
                        tr(trb, off, p5_sb[:, jj * 128:(jj + 1) * 128], r=(p5_sb, ident))
                        ve(lambda eng, o=off, trb=trb: eng.tensor_copy(
                            out=pT5[:, o:o + 128], in_=pb[trb][:, o:o + 128]),
                            r=(PB[trb],), w=(pT5,))
                        mmg(avb, pb[avb][sub_x:sub_x + 64, 0:128],
                            v_sb[:, u * 512 + 64 * vj: u * 512 + 64 * vj + 64],
                            pT5[:, off:off + 128], start=(u == 0), stop=(u == 4 * nch - 1),
                            r=(v_sb, pT5))
                ve(lambda eng, n=nch: eng.reduce_sum(out=st_c[:], in_=acc_t[:, 0:n],
                                                     axis=AX.X), r=(acc_t,), w=(st_c,))
                ve(lambda eng: eng.reciprocal(out=st_d[:], in_=st_c[:]), r=(st_c,),
                   w=(st_d,))
                tr(6, 0, st_d[:], r=(st_d, ident), rows=1)
                ve(lambda eng: eng.tensor_copy(out=rowt[:], in_=pb[6][:1, 0:128]),
                   r=(PB[6],), w=(rowt,))
                mmg(6, pb[6][:, 128:256], ones_r[:], rowt[:], start=True, stop=True,
                    r=(ones_r, rowt))
                ve(lambda eng: eng.tensor_copy(out=bc_sb5[:], in_=pb[6][:, 128:256]),
                   r=(PB[6],), w=(bc_sb5,))
                ve(lambda eng, avb=avb, sub_x=sub_x: eng.tensor_tensor(
                    out=o_sb5[sub_x:sub_x + 64, :], in0=pb[avb][sub_x:sub_x + 64, 0:128],
                    in1=bc_sb5[sub_x:sub_x + 64, :], op=ALU.mult),
                    r=(PB[avb], bc_sb5), w=(o_sb5,))
                ctx.dma("se", o1b[oc_x, qt, sub_x:sub_x + 64, :], o_sb5[sub_x:sub_x + 64, :],
                        r=(o_sb5,), w=(o1b,))

    ctx.barrier()
    if ph <= 8:
        return _finish([es5b])
    es5b.close()

    es5c = ExitStack()

    def SB5c(name, shape, dt=F32):
        return es5c.enter_context(nc.sbuf_tensor(name, list(shape), dt))

    wo_sb = SB5c("wo_sb", [128, 16 * 512])
    xr_sb5 = SB5c("xr_sb5", [128, 512])
    oT_buf5 = SB5c("oT_buf5", [128, 2048])
    wo_out5 = SB5c("wo_out5", [128, 512])

    for og in range(4):
        for ic in range(16):
            ctx.dma("sy", wo_sb[:, ic * 512:(ic + 1) * 512],
                    wo1[ic * 128:(ic + 1) * 128, og * 512:(og + 1) * 512],
                    w=(wo_sb,))
        for tb in range(8):
            for ic in range(16):
                ctx.dma("se", oT_buf5[:, ic * 128:(ic + 1) * 128], o1b[ic, tb],
                        r=(o1b,), w=(oT_buf5,))
            bi = 2 + (tb % 2)
            for ic in range(16):
                mmg(bi, pb[bi][:], oT_buf5[:, ic * 128:(ic + 1) * 128],
                    wo_sb[:, ic * 512:(ic + 1) * 512], start=(ic == 0), stop=(ic == 15),
                    r=(oT_buf5, wo_sb))
            ctx.dma("se", xr_sb5[:], xres[tb * 128:(tb + 1) * 128, og * 512:(og + 1) * 512],
                    r=(xres,), w=(xr_sb5,))
            ve(lambda eng, b=pb[bi]: eng.tensor_tensor(out=wo_out5[:], in0=b[:],
                                                       in1=xr_sb5[:], op=ALU.add),
               r=(PB[bi], xr_sb5), w=(wo_out5,))
            ctx.dma("se", xres[tb * 128:(tb + 1) * 128, og * 512:(og + 1) * 512],
                    wo_out5[:], r=(wo_out5,), w=(xres,))

    ctx.barrier()
    if ph <= 9:
        return _finish([es5c])
    es5c.close()

    es6 = ExitStack()

    def SB6(name, shape, dt=F32):
        return es6.enter_context(nc.sbuf_tensor(name, list(shape), dt))

    x_tok6 = SB6("x_tok6", [128, D])
    h_tok6 = SB6("h_tok6", [128, D])
    x3hi = SB6("x3hi", [128, 16 * 256], BF16)
    x3lo = SB6("x3lo", [128, 16 * 256], BF16)
    fhi = SB6("fhi", [128, 64 * 256], BF16)
    flo = SB6("flo", [128, 64 * 256], BF16)
    wgb = SB6("wgb", [128, 16 * 512], BF16)
    wub = SB6("wub", [128, 16 * 512], BF16)
    w8buf = [SB6(f"w8buf{i}", [128, 512], I8) for i in range(2)]
    wdb = [SB6(f"wdb{i}", [128, 512], BF16) for i in range(2)]
    sdb = SB6("sdb", [128, 2048])
    sd_row = SB6("sd_row", [1, 2048])
    sg_sb = SB6("sg_sb", [128, 64])
    su_sb = SB6("su_sb", [128, 64])
    vq6 = SB6("vq6", [128, 512])
    av6 = SB6("av6", [128, 512])
    rr6 = SB6("rr6", [128, 512])
    out6 = SB6("out6", [128, 512])

    ctx.dma("sy", sg_sb[:], sg1.rearrange("(o p) -> p o", p=128), w=(sg_sb,), slow=True)
    ctx.dma("sy", su_sb[:], su1.rearrange("(o p) -> p o", p=128), w=(su_sb,), slow=True)
    ctx.dma("sy", sd_row[:], sd1.rearrange("(a d) -> a d", a=1), w=(sd_row,))
    for jj in range(4):
        mmg(6, pb[6][:], ones_r[:], sd_row[:, jj * 512:(jj + 1) * 512],
            start=True, stop=True, r=(ones_r, sd_row))
        ve(lambda eng, c=jj * 512: eng.tensor_copy(out=sdb[:, c:c + 512], in_=pb[6][:]),
           r=(PB[6],), w=(sdb,))

    for q4 in range(4):
        for tbb in range(2):
            tbix = q4 * 2 + tbb
            ctx.dma("sy", x_tok6[:], xres[tbix * 128:(tbix + 1) * 128, :],
                    r=(xres,), w=(x_tok6,))
            emit_ln(h_tok6, x_tok6)
            for ic in range(16):
                bi = ic % 2
                tr(bi, 0, h_tok6[:, ic * 128:(ic + 1) * 128], r=(h_tok6, ident))
                c = ic * 256 + tbb * 128
                ve(lambda eng, b=pb[bi][:, 0:128], c=c:
                   eng.tensor_copy(out=x3hi[:, c:c + 128], in_=b), r=(PB[bi],), w=(x3hi,))
                ve(lambda eng, b=pb[bi][:, 0:128], c=c:
                   eng.tensor_tensor(out=x3lo[:, c:c + 128], in0=b, in1=x3hi[:, c:c + 128],
                                     op=ALU.subtract), r=(PB[bi], x3hi), w=(x3lo,))
        for og in range(16):
            for ic in range(16):
                bi8 = ic % 2
                ctx.dma("sy", w8buf[bi8][:],
                        wg1[ic * 128:(ic + 1) * 128, og * 512:(og + 1) * 512],
                        w=(w8buf[bi8],))
                ve(lambda eng, b=w8buf[bi8], c=ic * 512:
                   eng.tensor_copy(out=wgb[:, c:c + 512], in_=b[:]),
                   r=(w8buf[bi8],), w=(wgb,))
                ctx.dma("sy", w8buf[bi8][:],
                        wu1[ic * 128:(ic + 1) * 128, og * 512:(og + 1) * 512],
                        w=(w8buf[bi8],))
                ve(lambda eng, b=w8buf[bi8], c=ic * 512:
                   eng.tensor_copy(out=wub[:, c:c + 512], in_=b[:]),
                   r=(w8buf[bi8],), w=(wub,))
            for oc4 in range(4):
                oc = og * 4 + oc4
                for ic in range(16):
                    for hl, xsrc in ((0, x3hi), (1, x3lo)):
                        mmg(2, pb[2][:, 0:256],
                            wgb[:, ic * 512 + oc4 * 128: ic * 512 + oc4 * 128 + 128],
                            xsrc[:, ic * 256:(ic + 1) * 256],
                            start=(ic == 0 and hl == 0), stop=(ic == 15 and hl == 1),
                            r=(wgb, xsrc))
                        mmg(3, pb[3][:, 0:256],
                            wub[:, ic * 512 + oc4 * 128: ic * 512 + oc4 * 128 + 128],
                            xsrc[:, ic * 256:(ic + 1) * 256],
                            start=(ic == 0 and hl == 0), stop=(ic == 15 and hl == 1),
                            r=(wub, xsrc))
                ve(lambda eng, oc=oc: eng.tensor_scalar(
                    out=vq6[:, 0:256], in0=pb[2][:, 0:256], scalar1=sg_sb[:, oc:oc + 1],
                    scalar2=None, op0=ALU.mult), r=(PB[2], sg_sb), w=(vq6,))
                se(lambda eng: eng.activation(out=av6[:, 0:256], in_=vq6[:, 0:256],
                                              func=AF.Silu), r=(vq6,), w=(av6,))
                ve(lambda eng, oc=oc: eng.tensor_scalar(
                    out=rr6[:, 0:256], in0=pb[3][:, 0:256], scalar1=su_sb[:, oc:oc + 1],
                    scalar2=None, op0=ALU.mult), r=(PB[3], su_sb), w=(rr6,))
                ve(lambda eng: eng.tensor_tensor(out=vq6[:, 256:512], in0=av6[:, 0:256],
                                                 in1=rr6[:, 0:256], op=ALU.mult),
                   r=(av6, rr6), w=(vq6,))
                ve(lambda eng, c=oc * 256: eng.tensor_copy(out=fhi[:, c:c + 256],
                                                           in_=vq6[:, 256:512]),
                   r=(vq6,), w=(fhi,))
                ve(lambda eng, c=oc * 256: eng.tensor_tensor(
                    out=flo[:, c:c + 256], in0=vq6[:, 256:512], in1=fhi[:, c:c + 256],
                    op=ALU.subtract), r=(vq6, fhi), w=(flo,))
        for ogd in range(4):
            for icp in range(64):
                bi8 = icp % 2
                ctx.dma("sy", w8buf[bi8][:],
                        wd1[icp * 128:(icp + 1) * 128, ogd * 512:(ogd + 1) * 512],
                        w=(w8buf[bi8],))
                ve(lambda eng, b=w8buf[bi8], d=wdb[bi8]: eng.tensor_copy(out=d[:], in_=b[:]),
                   r=(w8buf[bi8],), w=(wdb[bi8],))
                for tb in range(2):
                    for hl, fsrc in ((0, fhi), (1, flo)):
                        mmg(4 + tb, pb[4 + tb][:],
                            fsrc[:, icp * 256 + tb * 128: icp * 256 + tb * 128 + 128],
                            wdb[bi8][:], start=(icp == 0 and hl == 0),
                            stop=(icp == 63 and hl == 1), r=(fsrc, wdb[bi8]))
            for tb in range(2):
                tbix = q4 * 2 + tb
                ve(lambda eng, b=pb[4 + tb], c=ogd * 512:
                   eng.tensor_tensor(out=vq6[:], in0=b[:], in1=sdb[:, c:c + 512],
                                     op=ALU.mult), r=(PB[4 + tb], sdb), w=(vq6,))
                ctx.dma("sy", av6[:], xres[tbix * 128:(tbix + 1) * 128,
                                           ogd * 512:(ogd + 1) * 512],
                        r=(xres,), w=(av6,))
                ve(lambda eng: eng.tensor_tensor(out=out6[:], in0=vq6[:], in1=av6[:],
                                                 op=ALU.add), r=(vq6, av6), w=(out6,))
                ctx.dma("se", y_out[tbix * 128:(tbix + 1) * 128,
                                    ogd * 512:(ogd + 1) * 512], out6[:], r=(out6,))

    for e in ENGS:
        for sname in ctx.val:
            ctx.wait(e, sname, ctx.val[sname])

    es6.close()
    with nc.Block() as block:
        ctx.realize(block)
    esg.close()
    return nc


_PROG_CACHE = {}


def _get_prog():
    import os
    ph = int(os.environ.get("KPHASE", "99"))
    key = ("nc", ph, os.environ.get("KN_HEADS"), os.environ.get("KN_QB"),
           os.environ.get("KN_PART"))
    if key not in _PROG_CACHE:
        _PROG_CACHE[key] = build_program(ph)
    return _PROG_CACHE[key]


def kernel(L0_attn_x, L0_attn_W_q, L0_attn_W_down_kv, L0_attn_W_up_k, L0_attn_W_up_v,
           L0_attn_W_o, L0_ffn_w_gate, L0_ffn_w_up, L0_ffn_w_down,
           L1_attn_wq, L1_attn_wk, L1_attn_wv, L1_attn_wo,
           L1_ffn_w_gate, L1_ffn_w_up, L1_ffn_w_down,
           L1_ffn_s_gate, L1_ffn_s_up, L1_ffn_s_down):
    import ml_dtypes
    nc = _get_prog()
    x = np.ascontiguousarray(np.asarray(L0_attn_x, dtype=np.float32))
    shared = {
        "W_q": np.ascontiguousarray(np.asarray(L0_attn_W_q, dtype=np.float32)),
        "W_dkv": np.ascontiguousarray(np.asarray(L0_attn_W_down_kv, dtype=np.float32)),
        "W_upk": np.ascontiguousarray(np.asarray(L0_attn_W_up_k, dtype=np.float32)),
        "W_upv": np.ascontiguousarray(np.asarray(L0_attn_W_up_v, dtype=np.float32)),
        "W_o": np.ascontiguousarray(np.asarray(L0_attn_W_o, dtype=np.float32)),
        "Wg0": np.ascontiguousarray(np.asarray(L0_ffn_w_gate, dtype=np.float32)),
        "Wu0": np.ascontiguousarray(np.asarray(L0_ffn_w_up, dtype=np.float32)),
        "Wd0": np.ascontiguousarray(np.asarray(L0_ffn_w_down, dtype=np.float32)),
        "wq1": np.ascontiguousarray(np.asarray(L1_attn_wq, dtype=np.float32)),
        "wk1": np.ascontiguousarray(np.asarray(L1_attn_wk, dtype=np.float32)),
        "wv1": np.ascontiguousarray(np.asarray(L1_attn_wv, dtype=np.float32)),
        "wo1": np.ascontiguousarray(np.asarray(L1_attn_wo, dtype=np.float32)),
        "wg1": np.ascontiguousarray(np.asarray(L1_ffn_w_gate, dtype=np.int8)),
        "wu1": np.ascontiguousarray(np.asarray(L1_ffn_w_up, dtype=np.int8)),
        "wd1": np.ascontiguousarray(np.asarray(L1_ffn_w_down, dtype=np.int8)),
        "sg1": np.ascontiguousarray(np.asarray(L1_ffn_s_gate, dtype=np.float32)),
        "su1": np.ascontiguousarray(np.asarray(L1_ffn_s_up, dtype=np.float32)),
        "sd1": np.ascontiguousarray(np.asarray(L1_ffn_s_down, dtype=np.float32)),
        "c_ident": np.eye(128, dtype=np.float32),
        "c_identb": np.eye(128, dtype=ml_dtypes.bfloat16),
        "c_h128": _host_hadamard(),
        "c_ones": np.ones((1, 128), dtype=np.float32),
    }
    in_maps = []
    for c in range(8):
        b, cp = c // 4, c % 4
        cos8, sin8 = _host_rope(c)
        m = dict(shared)
        m["x_own"] = np.ascontiguousarray(x[b, cp::4, :])
        m["c_mask"] = _host_mask(c)
        m["c_cos"] = cos8
        m["c_sin"] = sin8
        in_maps.append(m)
    res = run_bass_kernel_spmd(nc, in_maps, core_ids=list(range(8)))
    out = np.empty((B, S, D), dtype=np.float32)
    for c in range(8):
        b, cp = c // 4, c % 4
        out[b, cp::4, :] = res.results[c]["y_out"]
    return out


# version 09
# speedup vs baseline: 1.2040x; 1.0298x over previous
import numpy as np
from contextlib import ExitStack

import concourse.bass as bass
import concourse.mybir as mybir
from concourse.bass_utils import run_bass_kernel_spmd
from bass_rust import ReduceOp

F32 = mybir.dt.float32
BF16 = mybir.dt.bfloat16
I8 = mybir.dt.int8
AF = mybir.ActivationFunctionType
ALU = mybir.AluOpType
AX = mybir.AxisListType

B = 2
S = 4096
D = 2048
NTOK = 1024
DL0 = 512
FF = 8192
ROPE_THETA = 10000.0

C0_A = np.float32(1.0 / np.sqrt(np.float64(128.0)))
C0_B = np.float32((1.0 / np.sqrt(np.float64(128.0))) / np.float64(C0_A))
CH2048 = np.float32(1.0 / np.sqrt(np.float64(2048.0)))
CH8192 = np.float32(1.0 / np.sqrt(np.float64(8192.0)))
INV254 = np.float32(np.float64(1.0) / np.float64(254.0))
INV2_24 = np.float32(2.0 ** -24)

ENGS = ("te", "ve", "se", "ge", "sy")


class Ctx:

    def __init__(self, nc):
        self.nc = nc
        self.q = {e: [] for e in ENGS}
        self.sems = {}
        self.val = {}
        self.wm = {e: {} for e in ENGS}
        self.bufw = {}
        self.bufr = {}

    def add_sem(self, es, name):
        self.sems[name] = es.enter_context(self.nc.semaphore(name))
        self.val[name] = 0

    def wait(self, e, sname, v):
        if v is None or v <= 0:
            return
        if self.wm[e].get(sname, 0) >= v:
            return
        self.wm[e][sname] = v
        s = self.sems[sname]
        self.q[e].append(lambda eng, s=s, v=v: eng.wait_ge(s, v))

    def waits(self, e, tickets):
        for t in tickets:
            if t is not None:
                self.wait(e, t[0], t[1])

    def op(self, e, fn, r=(), w=(), wait=(), inc=None, amt=1, dr=True):
        if hasattr(self, "psum_ids") and self.psum_ids:
            rr = tuple(b for b in r if id(b) not in self.psum_ids)
            w = tuple(w) + tuple(b for b in r if id(b) in self.psum_ids)
            r = rr
        tickets = list(wait)
        for buf in r:
            tickets.append(self.bufw.get(id(buf)))
        for buf in w:
            tickets.append(self.bufw.get(id(buf)))
            for sname, v in self.bufr.get(id(buf), {}).items():
                tickets.append((sname, v))
        self.waits(e, tickets)
        tk = None
        if inc is not None:
            self.val[inc] += amt
            s = self.sems[inc]
            if e == "te" and dr:
                def run(eng, fn=fn, s=s, amt=amt):
                    fn(eng)
                    eng.maybe_drain_then_inc((s, amt), fusable=True)
                self.q[e].append(run)
            else:
                self.q[e].append(lambda eng, fn=fn, s=s, amt=amt: fn(eng).then_inc(s, amt))
            tk = (inc, self.val[inc])
        else:
            self.q[e].append(lambda eng, fn=fn: fn(eng))
        if tk is not None:
            for buf in r:
                d = self.bufr.setdefault(id(buf), {})
                if d.get(tk[0], 0) < tk[1]:
                    d[tk[0]] = tk[1]
            for buf in w:
                self.bufw[id(buf)] = tk
                self.bufr[id(buf)] = {}
        return tk

    def dma(self, qe, out_ap, in_ap, r=(), w=(), wait=(), accum=False, slow=False):
        kw = {"accum_op": ALU.add} if accum else {}
        if slow:
            kw["allow_slow_non_contiguous"] = True
        chs = self.dchan[qe]
        ch = chs[self.drr[qe] % len(chs)]
        self.drr[qe] += 1
        self.wait(qe, ch, self.val[ch])
        return self.op(
            qe,
            lambda eng, o=out_ap, i=in_ap, kw=kw: eng.dma_start(out=o, in_=i, **kw),
            r=r, w=w, wait=wait, inc=ch, amt=16,
        )

    def barrier(self):
        for e, sname in (("te", "te"), ("ve", "ve"), ("se", "se"), ("ge", "ge")):
            self.op(e, lambda eng: eng.drain(), inc=sname)
        targets = [(s, self.val[s]) for s in self.val]
        for e in ENGS:
            for s, v in targets:
                self.wait(e, s, v)

    def realize(self, block):
        def mk(ops):
            def run(eng):
                for f in ops:
                    f(eng)
            return run
        block.sync(mk(self.q["sy"]))
        block.tensor(mk(self.q["te"]))
        block.vector(mk(self.q["ve"]))
        block.scalar(mk(self.q["se"]))
        block.gpsimd(mk(self.q["ge"]))


def _host_rope(cid):
    cp = cid % 4
    pos = 4.0 * np.arange(NTOK, dtype=np.float64) + cp
    freq = np.float64(ROPE_THETA) ** (-(np.arange(32, dtype=np.float64) * (2.0 / 64.0)))
    ph = pos[:, None] * freq[None, :]
    cos8 = np.tile(np.cos(ph), (1, 8)).astype(np.float32)
    sin8 = np.tile(np.sin(ph), (1, 8)).astype(np.float32)
    return np.ascontiguousarray(cos8), np.ascontiguousarray(sin8)


def _host_mask(cid):
    cp = cid % 4
    i = np.arange(128)[:, None]
    f = np.arange(512)[None, :]
    u = f // 128
    p = f % 128
    kpos = 128 * u + 4 * (p % 32) + p // 32
    qpos = 4 * i + cp
    return np.ascontiguousarray(
        np.where(kpos <= qpos, np.float64(0.0), np.float64(-1e9)).astype(np.float32))


def _host_hadamard():
    h = np.array([[1.0]])
    while h.shape[0] < 128:
        h = np.block([[h, h], [h, -h]])
    return np.ascontiguousarray(h.astype(np.float32))


def _os_early():
    import os
    return os.environ.get("KDUMP")


def build_program(ph=99):
    import os
    kn_heads = int(os.environ.get("KN_HEADS", "16"))
    kn_qb = int(os.environ.get("KN_QB", "8"))
    kn_part = int(os.environ.get("KN_PART", "3"))
    nc = bass.Bass("TRN2", target_bir_lowering=False)

    def EI(name, shape, dt=F32):
        return nc.dram_tensor(name, list(shape), dt, kind="ExternalInput").ap()

    x_own = EI("x_own", [NTOK, D])
    W_q = EI("W_q", [D, D])
    W_dkv = EI("W_dkv", [DL0, D])
    W_upk = EI("W_upk", [16, 128, DL0])
    W_upv = EI("W_upv", [16, 128, DL0])
    W_o = EI("W_o", [D, D])
    Wg0 = EI("Wg0", [FF, D])
    Wu0 = EI("Wu0", [FF, D])
    Wd0 = EI("Wd0", [D, FF])
    wq1 = EI("wq1", [D, D])
    wk1 = EI("wk1", [D, DL0])
    wv1 = EI("wv1", [D, DL0])
    wo1 = EI("wo1", [D, D])
    wg1 = EI("wg1", [D, FF], I8)
    wu1 = EI("wu1", [D, FF], I8)
    wd1 = EI("wd1", [FF, D], I8)
    sg1 = EI("sg1", [FF])
    su1 = EI("su1", [FF])
    sd1 = EI("sd1", [D])
    c_ident = EI("c_ident", [128, 128])
    c_identb = EI("c_identb", [128, 128], BF16)
    c_h128 = EI("c_h128", [128, 128])
    c_mask = EI("c_mask", [128, 512])
    c_cos = EI("c_cos", [NTOK, 256])
    c_sin = EI("c_sin", [NTOK, 256])
    c_ones = EI("c_ones", [1, 128])

    y_out = nc.dram_tensor("y_out", [NTOK, D], F32, kind="ExternalOutput").ap()

    def DT(name, shape, dt=F32):
        return nc.dram_tensor(name, list(shape), dt).ap()

    xres = DT("xres", [NTOK, D])
    qt0 = DT("qt0", [16, 2, 128, 512])
    ot0 = DT("ot0", [16, 8, 128, 128])
    ckv_own = DT("ckv_own", [NTOK, DL0])
    ckv_full = DT("ckv_full", [S, DL0])
    kv_own = DT("kv_own", [NTOK, 2 * DL0])
    kv_full = DT("kv_full", [S, 2 * DL0])
    qt1 = DT("qt1", [16, 8, 128, 128])
    o1b = DT("o1b", [16, 8, 128, 128])
    wgt_b = DT("wgt_b", [64, 128, 16, 128], BF16)
    wut_b = DT("wut_b", [64, 128, 16, 128], BF16)
    wdt_b = DT("wdt_b", [16, 128, 64, 128], BF16)

    esg = ExitStack()
    ctx = Ctx(nc)
    for sname in ("te", "ve", "se", "ge"):
        ctx.add_sem(esg, sname)
    ctx.dchan = {"sy": [], "se": [], "ge": []}
    ctx.drr = {"sy": 0, "se": 0, "ge": 0}
    for qe, nch in (("sy", 4), ("se", 4), ("ge", 2)):
        for i in range(nch):
            nm = f"d{qe}{i}"
            ctx.add_sem(esg, nm)
            ctx.dchan[qe].append(nm)

    def SBg(name, shape, dt=F32):
        return esg.enter_context(nc.sbuf_tensor(name, list(shape), dt))

    ident = SBg("ident", [128, 128])
    maskt = SBg("maskt", [128, 512])
    ones_r = SBg("ones_r", [1, 128])
    eps_t = SBg("eps_t", [128, 1])
    thr_p = [SBg(f"thrp{i}", [128, 1]) for i in range(3)]
    thr_n = [SBg(f"thrn{i}", [128, 1]) for i in range(3)]
    scw = SBg("scw", [128, 4])
    ln_s6 = SBg("ln_s6", [128, 24])
    ln_mv = SBg("ln_mv", [128, 2])
    ln_std = SBg("ln_std", [128, 1])
    ln_rstd = SBg("ln_rstd", [128, 1])
    st_a = SBg("st_a", [128, 4])
    st_b = SBg("st_b", [128, 4])
    st_c = SBg("st_c", [128, 1])
    st_d = SBg("st_d", [128, 1])
    mx_t = SBg("mx_t", [128, 32])
    acc_t = SBg("acc_t", [128, 32])
    bias_t = SBg("bias_t", [128, 1])
    rowt = SBg("rowt", [1, 128])
    rowt2 = SBg("rowt2", [1, 256])
    wsum = SBg("wsum", [128, 224])
    wstat = [SBg(f"wstat{i}", [128, D]) for i in range(2)]

    pb = [esg.enter_context(nc.psum_tensor(f"pb{i}", [128, 512], F32)) for i in range(7)]
    pbT = esg.enter_context(nc.psum_tensor("pbT", [128, 1024], BF16))
    PB = list(pb) + [pbT]
    ctx.psum_ids = {id(b) for b in PB}

    def mmg(bi, out_ap, lhsT_ap, rhs_ap, start, stop, r=(), wait=()):
        return ctx.op("te", lambda eng, o=out_ap, l=lhsT_ap, r_=rhs_ap, st=start, sp=stop:
                      eng.matmul(out=o, lhsT=l, rhs=r_, start=st, stop=sp),
                      r=tuple(r), w=(PB[bi],), wait=wait, inc="te", dr=stop)

    def tr(bi, off, in_ap, r=(), wait=(), idn=None, rows=128):
        ap = pbT[:, off:off + 128] if bi == 7 else pb[bi][:rows, off:off + 128]
        return ctx.op("te", lambda eng, o=ap, i=in_ap, d=(idn if idn is not None else ident):
                      eng.transpose(out=o, in_=i, identity=d[:]),
                      r=tuple(r), w=(PB[bi],), wait=wait, inc="te")

    def ve(fn, r=(), w=(), wait=()):
        return ctx.op("ve", fn, r=r, w=w, wait=wait, inc="ve")

    def se(fn, r=(), w=(), wait=()):
        return ctx.op("se", fn, r=r, w=w, wait=wait, inc="se")

    def ge(fn, r=(), w=(), wait=()):
        return ctx.op("ge", fn, r=r, w=w, wait=wait, inc="ge")

    def emit_ln(htok, xtok, extra_wait=()):
        for qq in range(4):
            ve(lambda eng, qq=qq: eng.bn_stats(out=ln_s6[:, qq * 6:(qq + 1) * 6],
                                               in_=xtok[:, qq * 512:(qq + 1) * 512]),
               r=(xtok,), w=(ln_s6,), wait=extra_wait)
        ve(lambda eng: eng.bn_aggr(out=ln_mv[:], in_=ln_s6[:]), r=(ln_s6,), w=(ln_mv,))
        se(lambda eng: eng.activation(out=ln_std[:], in_=ln_mv[:, 1:2], func=AF.Sqrt,
                                      bias=eps_t[:]), r=(ln_mv, eps_t), w=(ln_std,))
        ve(lambda eng: eng.reciprocal(out=ln_rstd[:], in_=ln_std[:]), r=(ln_std,),
           w=(ln_rstd,))
        return ve(lambda eng: eng.tensor_scalar(out=htok[:], in0=xtok[:],
                                                scalar1=ln_mv[:, 0:1], scalar2=ln_rstd[:],
                                                op0=ALU.subtract, op1=ALU.mult),
                  r=(xtok, ln_mv, ln_rstd), w=(htok,))


    def _finish(stacks, dump=None):
        if dump is not None:
            dv = dump.rearrange("o n p t -> (o n) p t")
            nb_, wid = dv.shape[0], dv.shape[2]
            per = 2048 // wid
            for i in range(nb_):
                ctx.dma("se", y_out[128 * (i // per):128 * (i // per) + 128,
                                    wid * (i % per):wid * (i % per) + wid], dv[i],
                        r=(dump,))
        else:
            for i in range(4):
                ctx.dma("se", y_out[i * 256:(i + 1) * 256, :],
                        xres[i * 256:(i + 1) * 256, :], r=(xres,))
        for e in ENGS:
            for sname in ctx.val:
                ctx.wait(e, sname, ctx.val[sname])
        for st in stacks:
            st.close()
        with nc.Block() as block:
            ctx.realize(block)
        esg.close()
        return nc

    ctx.dma("sy", ident[:], c_ident, w=(ident,))
    ctx.dma("sy", maskt[:], c_mask, w=(maskt,))
    ctx.dma("sy", ones_r[:], c_ones, w=(ones_r,))
    ve(lambda eng: eng.memset(eps_t[:], 1e-5), w=(eps_t,))
    ve(lambda eng: eng.memset(mx_t[:], -1e30), w=(mx_t,))
    ve(lambda eng: eng.memset(acc_t[:], 0.0), w=(acc_t,))
    for i in range(4):
        ctx.dma("se", xres[i * 256:(i + 1) * 256, :], x_own[i * 256:(i + 1) * 256, :],
                w=(xres,))

    es1 = ExitStack()

    def SB1(name, shape, dt=F32):
        return es1.enter_context(nc.sbuf_tensor(name, list(shape), dt))

    x_tok = SB1("x_tok1", [128, D])
    h_tok = SB1("h_tok1", [128, D])
    hT = SB1("hT1", [128, 16 * 1024])
    wq_buf = [SB1(f"wq_buf{i}", [128, D]) for i in range(2)]
    wqT = SB1("wqT", [128, D])
    q_sb = SB1("q_sb1", [128, 512])
    wdT = SB1("wdT", [128, 16 * 512])
    ckv_sb = SB1("ckv_sb", [128, 512])

    for tb in range(8):
        ctx.dma("sy", x_tok[:], x_own[tb * 128:(tb + 1) * 128, :], w=(x_tok,))
        emit_ln(h_tok, x_tok)
        for ic in range(16):
            bi = ic % 2
            tr(bi, 0, h_tok[:, ic * 128:(ic + 1) * 128], r=(h_tok, ident))
            c = ic * 1024 + tb * 128
            ve(lambda eng, b=pb[bi][:, 0:128], c=c:
               eng.tensor_copy(out=hT[:, c:c + 128], in_=b), r=(PB[bi],), w=(hT,))

    col = 0
    wst_cols = {}
    for wi, Wsrc in enumerate((Wg0, Wu0, Wd0)):
        rows, cols = (FF, D) if wi < 2 else (D, FF)
        wst_cols[wi] = (col, col + (rows // 128) * (cols // D))
        for rb in range(rows // 128):
            for cs in range(cols // D):
                bi = col % 2
                ctx.dma("se", wstat[bi][:],
                        Wsrc[rb * 128:(rb + 1) * 128, cs * D:(cs + 1) * D], w=(wstat[bi],))
                ve(lambda eng, b=wstat[bi], c=col:
                   eng.tensor_reduce(out=wsum[:, c:c + 1], in_=b[:], axis=AX.X, op=ALU.add,
                                     apply_absolute_value=True), r=(wstat[bi],), w=(wsum,))
                col += 1
    for wi in range(3):
        c0, c1 = wst_cols[wi]
        ve(lambda eng, c0=c0, c1=c1, wi=wi:
           eng.tensor_reduce(out=st_a[:, wi:wi + 1], in_=wsum[:, c0:c1], axis=AX.X,
                             op=ALU.add), r=(wsum,), w=(st_a,))
        tr(6, 0, st_a[:, wi:wi + 1], r=(st_a, ident), rows=1)
        ve(lambda eng: eng.tensor_copy(out=rowt[:], in_=pb[6][:1, 0:128]), r=(PB[6],),
           w=(rowt,))
        ve(lambda eng: eng.reduce_sum(out=st_c[:1, :], in_=rowt[:], axis=AX.X),
           r=(rowt,), w=(st_c,))
        mmg(6, pb[6][:, 0:1], ones_r[:], st_c[:1, :], start=True, stop=True,
            r=(ones_r, st_c))
        ve(lambda eng, wi=wi: eng.tensor_copy(out=st_b[:, wi:wi + 1], in_=pb[6][:, 0:1]),
           r=(PB[6],), w=(st_b,))
        ve(lambda eng, wi=wi: eng.tensor_scalar(
            out=scw[:, wi:wi + 1], in0=st_b[:, wi:wi + 1], scalar1=float(INV2_24),
            scalar2=float(np.float32(1e-5)), op0=ALU.mult, op1=ALU.max), r=(st_b,), w=(scw,))
        ve(lambda eng, wi=wi: eng.tensor_scalar(
            out=thr_p[wi][:], in0=scw[:, wi:wi + 1], scalar1=float(np.float32(0.5)), scalar2=None,
            op0=ALU.mult), r=(scw,), w=(thr_p[wi],))
        ve(lambda eng, wi=wi: eng.tensor_scalar(
            out=thr_n[wi][:], in0=scw[:, wi:wi + 1], scalar1=float(np.float32(-0.5)), scalar2=None,
            op0=ALU.mult), r=(scw,), w=(thr_n[wi],))

    for oc in range(16):
        wb = wq_buf[oc % 2]
        ctx.dma("sy", wb[:], W_q[oc * 128:(oc + 1) * 128, :], w=(wb,))
        for ic in range(16):
            bi = ic % 2
            tr(bi, 0, wb[:, ic * 128:(ic + 1) * 128], r=(wb, ident))
            ve(lambda eng, b=pb[bi][:, 0:128], c=ic * 128:
               eng.tensor_copy(out=wqT[:, c:c + 128], in_=b), r=(PB[bi],), w=(wqT,))
        for nb in range(2):
            for ic in range(16):
                mmg(2, pb[2][:], wqT[:, ic * 128:(ic + 1) * 128],
                    hT[:, ic * 1024 + nb * 512: ic * 1024 + nb * 512 + 512],
                    start=(ic == 0), stop=(ic == 15), r=(wqT, hT))
            ve(lambda eng: eng.tensor_scalar(out=q_sb[:], in0=pb[2][:], scalar1=float(C0_A),
                                             scalar2=None, op0=ALU.mult),
               r=(PB[2],), w=(q_sb,))
            ctx.dma("se", qt0[oc, nb], q_sb[:], r=(q_sb,), w=(qt0,))

    for lc in range(4):
        ctx.dma("sy", wq_buf[0][:], W_dkv[lc * 128:(lc + 1) * 128, :], w=(wq_buf[0],))
        for ic in range(16):
            bi = ic % 2
            tr(bi, 0, wq_buf[0][:, ic * 128:(ic + 1) * 128], r=(wq_buf[0], ident))
            ve(lambda eng, b=pb[bi][:, 0:128], c=ic * 512 + lc * 128:
               eng.tensor_copy(out=wdT[:, c:c + 128], in_=b), r=(PB[bi],), w=(wdT,))
    for tb in range(8):
        for ic in range(16):
            mmg(3, pb[3][:], hT[:, ic * 1024 + tb * 128: ic * 1024 + tb * 128 + 128],
                wdT[:, ic * 512:(ic + 1) * 512], start=(ic == 0), stop=(ic == 15),
                r=(hT, wdT))
        ve(lambda eng: eng.tensor_copy(out=ckv_sb[:], in_=pb[3][:]), r=(PB[3],),
           w=(ckv_sb,))
        ctx.dma("se", ckv_own[tb * 128:(tb + 1) * 128, :], ckv_sb[:], r=(ckv_sb,),
                w=(ckv_own,))

    ctx.barrier()
    if ph <= 1:
        import os
        return _finish([es1], dump=qt0 if os.environ.get("KDUMP") == "qt0" else None)
    es1.close()

    es3 = ExitStack()

    def SB3(name, shape, dt=F32):
        return es3.enter_context(nc.sbuf_tensor(name, list(shape), dt))

    ckvT = SB3("ckvT", [128, 4 * 4096])
    kT_h = SB3("kT_h", [128, 4096])
    v_h = SB3("v_h", [128, 4096])
    qh = SB3("qh", [128, 1024])
    s_sb = SB3("s_sb", [128, 4096])
    p_sb = SB3("p_sb", [128, 4096])
    pT_sb = SB3("pT_sb", [128, 4096])
    o_sb = SB3("o_sb", [128, 128])
    wup_buf = SB3("wup_buf", [128, 2 * DL0])
    bc_sb = SB3("bc_sb", [128, 128])
    wupkT = SB3("wupkT", [128, 512])
    wupvT = SB3("wupvT", [128, 512])
    ckv_tile = SB3("ckv_tile", [128, 512])

    ge(lambda eng: eng.collective_compute(
        "AllGather", ALU.bypass, replica_groups=[[0, 1, 2, 3], [4, 5, 6, 7]],
        ins=[ckv_own.opt()], outs=[ckv_full.opt()]), r=(ckv_own,), w=(ckv_full,))

    for u in range(32):
        ctx.dma("ge", ckv_tile[:], ckv_full[u * 128:(u + 1) * 128, :],
                r=(ckv_full,), w=(ckv_tile,))
        for lc in range(4):
            bi = lc % 2
            tr(bi, 0, ckv_tile[:, lc * 128:(lc + 1) * 128], r=(ckv_tile, ident))
            ve(lambda eng, b=pb[bi][:, 0:128], c=lc * 4096 + u * 128:
               eng.tensor_copy(out=ckvT[:, c:c + 128], in_=b), r=(PB[bi],), w=(ckvT,))

    if ph <= 2:
        ctx.barrier()
        import os
        if os.environ.get("KDUMP") == "ckv":
            dv = ckv_full.rearrange("(a p) f -> a p f", p=128)
            for i in range(32):
                ctx.dma("se", y_out[128 * (i // 4):128 * (i // 4) + 128,
                                    512 * (i % 4):512 * (i % 4) + 512], dv[i],
                        r=(ckv_full,))
            for e in ENGS:
                for sname in ctx.val:
                    ctx.wait(e, sname, ctx.val[sname])
            es3.close()
            with nc.Block() as block:
                ctx.realize(block)
            esg.close()
            return nc
        return _finish([es3])

    if _os_early() == "ckvT2":
        ctx.barrier()
        for i in range(8):
            ctx.dma("se", y_out[i * 128:(i + 1) * 128, :],
                    ckvT[:, i * 2048:(i + 1) * 2048], r=(ckvT,))
        for e in ENGS:
            for sname in ctx.val:
                ctx.wait(e, sname, ctx.val[sname])
        es3.close()
        with nc.Block() as block:
            ctx.realize(block)
        esg.close()
        return nc

    for h in range(kn_heads):
        ctx.dma("sy", wup_buf[:, 0:512], W_upk[h], w=(wup_buf,))
        ctx.dma("sy", wup_buf[:, 512:1024], W_upv[h], w=(wup_buf,))
        for lc in range(4):
            tr(0, 0, wup_buf[:, lc * 128:(lc + 1) * 128], r=(wup_buf, ident))
            ve(lambda eng, b=pb[0][:, 0:128], c=lc * 128:
               eng.tensor_copy(out=wupkT[:, c:c + 128], in_=b), r=(PB[0],), w=(wupkT,))
            tr(1, 0, wup_buf[:, 512 + lc * 128: 512 + (lc + 1) * 128], r=(wup_buf, ident))
            ve(lambda eng, b=pb[1][:, 0:128], c=lc * 128:
               eng.tensor_copy(out=wupvT[:, c:c + 128], in_=b), r=(PB[1],), w=(wupvT,))
        for kb in range(8):
            for lc in range(4):
                mmg(2, pb[2][:], wupkT[:, lc * 128:(lc + 1) * 128],
                    ckvT[:, lc * 4096 + kb * 512: lc * 4096 + kb * 512 + 512],
                    start=(lc == 0), stop=(lc == 3), r=(wupkT, ckvT))
            ve(lambda eng, c=kb * 512: eng.tensor_copy(out=kT_h[:, c:c + 512], in_=pb[2][:]),
               r=(PB[2],), w=(kT_h,))
        for u in range(32):
            for lc in range(4):
                mmg(3, pb[3][:, 0:128], ckvT[:, lc * 4096 + u * 128: lc * 4096 + u * 128 + 128],
                    wupvT[:, lc * 128:(lc + 1) * 128], start=(lc == 0), stop=(lc == 3),
                    r=(ckvT, wupvT))
            ve(lambda eng, c=u * 128: eng.tensor_copy(out=v_h[:, c:c + 128],
                                                      in_=pb[3][:, 0:128]),
               r=(PB[3],), w=(v_h,))
        ctx.dma("se", qh[:, 0:512], qt0[h, 0], r=(qt0,), w=(qh,))
        ctx.dma("se", qh[:, 512:1024], qt0[h, 1], r=(qt0,), w=(qh,))

        for qb in range(kn_qb):
            for kb in range(8):
                bi = 4 + (kb % 2)
                mmg(bi, pb[bi][:], qh[:, qb * 128:(qb + 1) * 128],
                    kT_h[:, kb * 512:(kb + 1) * 512], start=True, stop=True, r=(qh, kT_h))
                ve(lambda eng, b=pb[bi], c=kb * 512:
                   eng.tensor_copy(out=s_sb[:, c:c + 512], in_=b[:]),
                   r=(PB[bi],), w=(s_sb,))
                ve(lambda eng, c=kb * 512, k=kb:
                   eng.reduce_max(out=mx_t[:, k:k + 1], in_=s_sb[:, c:c + 512], axis=AX.X),
                   r=(s_sb,), w=(mx_t,))
            ve(lambda eng: eng.reduce_max(out=st_c[:], in_=mx_t[:, 0:32], axis=AX.X),
               r=(mx_t,), w=(st_c,))
            ve(lambda eng: eng.tensor_scalar(out=bias_t[:], in0=st_c[:],
                                             scalar1=float(-C0_B), scalar2=None,
                                             op0=ALU.mult), r=(st_c,), w=(bias_t,))
            for kb in range(8):
                se(lambda eng, c=kb * 512, k=kb: eng.activation(
                    out=p_sb[:, c:c + 512], in_=s_sb[:, c:c + 512], func=AF.Exp,
                    bias=bias_t[:], scale=float(C0_B), accum_out=acc_t[:, k:k + 1]),
                    r=(s_sb, bias_t), w=(p_sb, acc_t))
            for u in range(32):
                off = (u % 4) * 128
                tr(2, off, p_sb[:, u * 128:(u + 1) * 128], r=(p_sb, ident))
                ve(lambda eng, o=off, c=u * 128: eng.tensor_copy(
                    out=pT_sb[:, c:c + 128], in_=pb[2][:, o:o + 128]),
                   r=(PB[2],), w=(pT_sb,))
            for u in range(32):
                mmg(6, pb[6][:, 0:128], v_h[:, u * 128:(u + 1) * 128],
                    pT_sb[:, u * 128:(u + 1) * 128], start=(u == 0), stop=(u == 31),
                    r=(v_h, pT_sb))
            ve(lambda eng: eng.reduce_sum(out=st_c[:], in_=acc_t[:, 0:32], axis=AX.X),
               r=(acc_t,), w=(st_c,))
            ve(lambda eng: eng.reciprocal(out=st_d[:], in_=st_c[:]), r=(st_c,), w=(st_d,))
            tr(3, 0, st_d[:], r=(st_d, ident), rows=1)
            ve(lambda eng: eng.tensor_copy(out=rowt[:], in_=pb[3][:1, 0:128]),
               r=(PB[3],), w=(rowt,))
            mmg(3, pb[3][:, 128:256], ones_r[:], rowt[:], start=True, stop=True,
                r=(ones_r, rowt))
            ve(lambda eng: eng.tensor_copy(out=bc_sb[:], in_=pb[3][:, 128:256]),
               r=(PB[3],), w=(bc_sb,))
            ve(lambda eng: eng.tensor_tensor(out=o_sb[:], in0=pb[6][:, 0:128],
                                             in1=bc_sb[:], op=ALU.mult),
               r=(PB[6], bc_sb), w=(o_sb,))
            ctx.dma("se", ot0[h, qb], o_sb[:], r=(o_sb,), w=(ot0,))

    import os as _os
    if _os.environ.get("KDUMP") == "ckvT":
        ctx.barrier()
        for i in range(8):
            ctx.dma("se", y_out[i * 128:(i + 1) * 128, :],
                    ckvT[:, i * 2048:(i + 1) * 2048], r=(ckvT,))
        for e in ENGS:
            for sname in ctx.val:
                ctx.wait(e, sname, ctx.val[sname])
        es3.close()
        with nc.Block() as block:
            ctx.realize(block)
        esg.close()
        return nc
    if _os.environ.get("KDUMP") == "kv":
        for a_ in range(8):
            ctx.dma("se", ckv_own[a_ * 128:(a_ + 1) * 128, :],
                    kT_h[:, a_ * 512:(a_ + 1) * 512], r=(kT_h,), w=(ckv_own,))
        for a_ in range(4):
            ctx.dma("se", kv_own[a_ * 128:(a_ + 1) * 128, :],
                    v_h[:, a_ * 1024:(a_ + 1) * 1024], r=(v_h,), w=(kv_own,))
        ctx.barrier()
        ctx.dma("se", y_out[0:256, :], ckv_own.rearrange("(a b) f -> a (b f)", b=4),
                r=(ckv_own,))
        ctx.dma("se", y_out[256:512, :], kv_own[0:512, :].rearrange("(a b) f -> a (b f)", b=2),
                r=(kv_own,))
        for e in ENGS:
            for sname in ctx.val:
                ctx.wait(e, sname, ctx.val[sname])
        es3.close()
        with nc.Block() as block:
            ctx.realize(block)
        esg.close()
        return nc
    ctx.barrier()
    if ph <= 3:
        import os
        return _finish([es3], dump=ot0 if os.environ.get("KDUMP") == "ot0" else None)
    es3.close()

    es3b = ExitStack()

    def SB3b(name, shape, dt=F32):
        return es3b.enter_context(nc.sbuf_tensor(name, list(shape), dt))

    woT = SB3b("woT", [128, 16 * 512])
    xr_sb = SB3b("xr_sb", [128, 512])
    oT_buf = SB3b("oT_buf", [128, 2048])
    wo_out = SB3b("wo_out", [128, 512])
    wo_buf = [SB3b(f"wo_buf{i}", [128, D]) for i in range(2)]

    for og in range(4):
        for ocr in range(4):
            wb = wo_buf[ocr % 2]
            ctx.dma("sy", wb[:], W_o[og * 512 + ocr * 128: og * 512 + (ocr + 1) * 128, :],
                    w=(wb,))
            for ic in range(16):
                bi = ic % 2
                tr(bi, 0, wb[:, ic * 128:(ic + 1) * 128], r=(wb, ident))
                ve(lambda eng, b=pb[bi][:, 0:128], c=ic * 512 + ocr * 128:
                   eng.tensor_copy(out=woT[:, c:c + 128], in_=b), r=(PB[bi],), w=(woT,))
        for tb in range(8):
            for ic in range(16):
                ctx.dma("se", oT_buf[:, ic * 128:(ic + 1) * 128], ot0[ic, tb],
                        r=(ot0,), w=(oT_buf,))
            bi = 2 + (tb % 2)
            for ic in range(16):
                mmg(bi, pb[bi][:], oT_buf[:, ic * 128:(ic + 1) * 128],
                    woT[:, ic * 512:(ic + 1) * 512], start=(ic == 0), stop=(ic == 15),
                    r=(oT_buf, woT))
            ctx.dma("se", xr_sb[:], xres[tb * 128:(tb + 1) * 128, og * 512:(og + 1) * 512],
                    r=(xres,), w=(xr_sb,))
            ve(lambda eng, b=pb[bi]: eng.tensor_tensor(out=wo_out[:], in0=b[:],
                                                       in1=xr_sb[:], op=ALU.add),
               r=(PB[bi], xr_sb), w=(wo_out,))
            ctx.dma("se", xres[tb * 128:(tb + 1) * 128, og * 512:(og + 1) * 512],
                    wo_out[:], r=(wo_out,), w=(xres,))

    ctx.barrier()
    if ph <= 4:
        return _finish([es3b])
    es3b.close()

    es4a = ExitStack()

    def SB4a(name, shape, dt=F32):
        return es4a.enter_context(nc.sbuf_tensor(name, list(shape), dt))

    identb = SB4a("identb", [128, 128], BF16)
    m_buf = SB4a("m_buf", [128, D], BF16)
    sg_a = SB4a("sg_a", [128, D])
    sg_b = SB4a("sg_b", [128, D])
    mT_sb = SB4a("mT_sb", [128, D], BF16)
    ctx.dma("sy", identb[:], c_identb, w=(identb,))

    for wi, (Wsrc, Wdst, rows, cols) in enumerate(
            ((Wg0, wgt_b, FF, D), (Wu0, wut_b, FF, D), (Wd0, wdt_b, D, FF))):
        for rb in range(rows // 128):
            for cs in range(cols // D):
                bi = (rb * (cols // D) + cs) % 2
                ctx.dma("sy", wstat[bi][:],
                        Wsrc[rb * 128:(rb + 1) * 128, cs * D:(cs + 1) * D], w=(wstat[bi],))
                se(lambda eng, b=wstat[bi], w=wi: eng.activation(
                    out=sg_a[:], in_=b[:], func=AF.Sign, bias=thr_n[w][:]),
                    r=(wstat[bi], thr_n[wi]), w=(sg_a,))
                se(lambda eng, b=wstat[bi], w=wi: eng.activation(
                    out=sg_b[:], in_=b[:], func=AF.Sign, bias=thr_p[w][:]),
                    r=(wstat[bi], thr_p[wi]), w=(sg_b,))
                ve(lambda eng: eng.tensor_tensor(out=m_buf[:], in0=sg_a[:], in1=sg_b[:],
                                                 op=ALU.add), r=(sg_a, sg_b), w=(m_buf,))
                for ic in range(16):
                    off = (ic % 8) * 128
                    tr(7, off, m_buf[:, ic * 128:(ic + 1) * 128], r=(m_buf, identb),
                       idn=identb)
                    ve(lambda eng, o=off, c=ic * 128:
                       eng.tensor_copy(out=mT_sb[:, c:c + 128], in_=pbT[:, o:o + 128]),
                       r=(PB[7],), w=(mT_sb,))
                    ctx.dma("se", Wdst[rb, :, cs * 16 + ic, :],
                            mT_sb[:, ic * 128:(ic + 1) * 128], r=(mT_sb,),
                            w=(Wdst,))

    ctx.barrier()
    if ph <= 5:
        return _finish([es4a])
    es4a.close()

    es4b = ExitStack()

    def SB4b(name, shape, dt=F32):
        return es4b.enter_context(nc.sbuf_tensor(name, list(shape), dt))

    h128 = SB4b("h128", [128, 128])
    x_tok4 = SB4b("x_tok4", [128, D])
    h_tok4 = SB4b("h_tok4", [128, D])
    hT2 = SB4b("hT2", [128, 16 * 256])
    k1 = SB4b("k1", [128, 16 * 256], BF16)
    t_sb = SB4b("t_sb", [128, 64 * 256])
    k2 = SB4b("k2", [128, 64 * 256], BF16)
    cmx = SB4b("cmx", [128, 256])
    cmb = SB4b("cmb", [128, 256])
    sbc = SB4b("sbc", [128, 256])
    comb = SB4b("comb", [128, 256])
    vq = SB4b("vq", [128, 256])
    avx = SB4b("avx", [128, 256])
    rrx = SB4b("rrx", [128, 256])
    sgn = SB4b("sgn", [128, 256])
    ggx = SB4b("ggx", [128, 256])
    wg_sb = [SB4b(f"wg_sb{i}", [128, 16 * 128], BF16) for i in range(2)]
    wu_sb = [SB4b(f"wu_sb{i}", [128, 16 * 128], BF16) for i in range(2)]
    wd_sb = [SB4b(f"wd_sb{i}", [128, 32 * 128], BF16) for i in range(2)]
    ot_stage = SB4b("ot_stage", [128, 128])
    ctx.dma("sy", h128[:], c_h128, w=(h128,))

    def emit_quant(src_sb, nchunk, ch_norm, k_dst):
        ve(lambda eng: eng.scalar_tensor_tensor(out=cmx[:], in0=src_sb[:, 0:256],
                                                scalar=-1.0, in1=src_sb[:, 0:256],
                                                op0=ALU.mult, op1=ALU.max),
           r=(src_sb,), w=(cmx,))
        for icx in range(1, nchunk):
            c = icx * 256
            ve(lambda eng, c=c: eng.scalar_tensor_tensor(
                out=avx[:], in0=src_sb[:, c:c + 256], scalar=-1.0,
                in1=src_sb[:, c:c + 256], op0=ALU.mult, op1=ALU.max),
                r=(src_sb,), w=(avx,))
            ve(lambda eng: eng.tensor_tensor(out=cmx[:], in0=cmx[:], in1=avx[:],
                                             op=ALU.max), r=(cmx, avx), w=(cmx,))
        for half in range(2):
            tr(6, 0, cmx[:, half * 128:(half + 1) * 128], r=(cmx, ident))
            ve(lambda eng: eng.reduce_max(out=st_c[:], in_=pb[6][:, 0:128], axis=AX.X),
               r=(PB[6],), w=(st_c,))
            tr(6, 128, st_c[:], r=(st_c, ident), rows=1)
            ve(lambda eng, half=half: eng.tensor_copy(out=rowt2[:, half * 128:(half + 1) * 128],
                                                      in_=pb[6][:1, 128:256]),
               r=(PB[6],), w=(rowt2,))
        mmg(6, pb[6][:, 0:256], ones_r[:], rowt2[:], start=True, stop=True,
            r=(ones_r, rowt2))
        ve(lambda eng: eng.tensor_copy(out=cmb[:], in_=pb[6][:, 0:256]), r=(PB[6],),
           w=(cmb,))
        ve(lambda eng: eng.tensor_scalar(out=cmx[:], in0=cmb[:], scalar1=float(ch_norm),
                                         scalar2=float(np.float32(1e-5)), op0=ALU.mult,
                                         op1=ALU.max), r=(cmb,), w=(cmx,))
        ve(lambda eng: eng.reciprocal(out=cmb[:], in_=cmx[:]), r=(cmx,), w=(cmb,))
        ve(lambda eng: eng.tensor_scalar(out=sbc[:], in0=cmb[:], scalar1=float(ch_norm),
                                         scalar2=float(np.float32(127.0)), op0=ALU.mult,
                                         op1=ALU.mult), r=(cmb,), w=(sbc,))
        ve(lambda eng: eng.tensor_scalar(out=comb[:], in0=cmx[:], scalar1=float(INV254),
                                         scalar2=None, op0=ALU.mult), r=(cmx,), w=(comb,))
        for icx in range(nchunk):
            c = icx * 256
            ve(lambda eng, c=c: eng.tensor_tensor(out=vq[:], in0=src_sb[:, c:c + 256],
                                                  in1=sbc[:], op=ALU.mult),
               r=(src_sb, sbc), w=(vq,))
            se(lambda eng: eng.activation(out=sgn[:], in_=vq[:], func=AF.Sign),
               r=(vq,), w=(sgn,))
            ve(lambda eng: eng.scalar_tensor_tensor(out=avx[:], in0=vq[:], scalar=-1.0,
                                                    in1=vq[:], op0=ALU.mult, op1=ALU.max),
               r=(vq,), w=(avx,))
            ve(lambda eng: eng.tensor_scalar(out=avx[:], in0=avx[:], scalar1=0.5,
                                             scalar2=None, op0=ALU.add), r=(avx,), w=(avx,))
            ve(lambda eng: eng.tensor_copy(out=rrx[:], in_=avx[:]), r=(avx,), w=(rrx,))
            for bb in (64.0, 32.0, 16.0, 8.0, 4.0, 2.0, 1.0):
                ve(lambda eng, bb=bb: eng.tensor_scalar(out=ggx[:], in0=rrx[:], scalar1=bb,
                                                        scalar2=None, op0=ALU.is_ge),
                   r=(rrx,), w=(ggx,))
                ve(lambda eng, bb=bb: eng.scalar_tensor_tensor(out=rrx[:], in0=ggx[:],
                                                               scalar=-bb, in1=rrx[:],
                                                               op0=ALU.mult, op1=ALU.add),
                   r=(ggx, rrx), w=(rrx,))
            ve(lambda eng: eng.tensor_tensor(out=avx[:], in0=avx[:], in1=rrx[:],
                                             op=ALU.subtract), r=(avx, rrx), w=(avx,))
            ve(lambda eng, c=c: eng.tensor_tensor(out=k_dst[:, c:c + 256], in0=avx[:],
                                                  in1=sgn[:], op=ALU.mult),
               r=(avx, sgn), w=(k_dst,))

    for ttile in range(4):
        for tbb in range(2):
            tbix = ttile * 2 + tbb
            ctx.dma("sy", x_tok4[:], xres[tbix * 128:(tbix + 1) * 128, :],
                    r=(xres,), w=(x_tok4,))
            emit_ln(h_tok4, x_tok4)
            for ic in range(16):
                bi = ic % 2
                tr(bi, 0, h_tok4[:, ic * 128:(ic + 1) * 128], r=(h_tok4, ident))
                ve(lambda eng, b=pb[bi][:, 0:128], c=ic * 256 + tbb * 128:
                   eng.tensor_copy(out=hT2[:, c:c + 128], in_=b), r=(PB[bi],), w=(hT2,))
        for stg in (1, 2, 4, 8):
            for icx in range(16):
                if icx & stg:
                    continue
                a = icx * 256
                b_ = (icx ^ stg) * 256
                ve(lambda eng, a=a, b_=b_: eng.tensor_tensor(
                    out=hT2[:, a:a + 256], in0=hT2[:, a:a + 256], in1=hT2[:, b_:b_ + 256],
                    op=ALU.add), r=(hT2,), w=(hT2,))
                ve(lambda eng, a=a, b_=b_: eng.scalar_tensor_tensor(
                    out=hT2[:, b_:b_ + 256], in0=hT2[:, b_:b_ + 256],
                    scalar=float(np.float32(-2.0)), in1=hT2[:, a:a + 256],
                    op0=ALU.mult, op1=ALU.add), r=(hT2,), w=(hT2,))
        for icx in range(16):
            bi = icx % 2
            mmg(bi, pb[bi][:, 0:256], h128[:], hT2[:, icx * 256:(icx + 1) * 256],
                start=True, stop=True, r=(h128, hT2))
            ve(lambda eng, b=pb[bi], c=icx * 256:
               eng.tensor_copy(out=hT2[:, c:c + 256], in_=b[:, 0:256]),
               r=(PB[bi],), w=(hT2,))
        emit_quant(hT2, 16, CH2048, k1)
        for oc in range(64):
            wgb_ = wg_sb[oc % 2]
            wub_ = wu_sb[oc % 2]
            ctx.dma("sy", wgb_[:], wgt_b[oc], w=(wgb_,))
            ctx.dma("sy", wub_[:], wut_b[oc], w=(wub_,))
            for ic in range(16):
                mmg(2, pb[2][:, 0:256], wgb_[:, ic * 128:(ic + 1) * 128],
                    k1[:, ic * 256:(ic + 1) * 256], start=(ic == 0), stop=(ic == 15),
                    r=(wgb_, k1))
                mmg(3, pb[3][:, 0:256], wub_[:, ic * 128:(ic + 1) * 128],
                    k1[:, ic * 256:(ic + 1) * 256], start=(ic == 0), stop=(ic == 15),
                    r=(wub_, k1))
            ve(lambda eng: eng.scalar_tensor_tensor(
                out=vq[:], in0=pb[2][:, 0:256], scalar=scw[:, 0:1], in1=comb[:],
                op0=ALU.mult, op1=ALU.mult), r=(PB[2], scw, comb), w=(vq,))
            se(lambda eng: eng.activation(out=avx[:], in_=vq[:], func=AF.Silu),
               r=(vq,), w=(avx,))
            ve(lambda eng: eng.scalar_tensor_tensor(
                out=rrx[:], in0=pb[3][:, 0:256], scalar=scw[:, 1:2], in1=comb[:],
                op0=ALU.mult, op1=ALU.mult), r=(PB[3], scw, comb), w=(rrx,))
            ve(lambda eng, c=oc * 256: eng.tensor_tensor(
                out=t_sb[:, c:c + 256], in0=avx[:], in1=rrx[:], op=ALU.mult),
                r=(avx, rrx), w=(t_sb,))
        for stg in (1, 2, 4, 8, 16, 32):
            for icx in range(64):
                if icx & stg:
                    continue
                a = icx * 256
                b_ = (icx ^ stg) * 256
                ve(lambda eng, a=a, b_=b_: eng.tensor_tensor(
                    out=t_sb[:, a:a + 256], in0=t_sb[:, a:a + 256], in1=t_sb[:, b_:b_ + 256],
                    op=ALU.add), r=(t_sb,), w=(t_sb,))
                ve(lambda eng, a=a, b_=b_: eng.scalar_tensor_tensor(
                    out=t_sb[:, b_:b_ + 256], in0=t_sb[:, b_:b_ + 256],
                    scalar=float(np.float32(-2.0)), in1=t_sb[:, a:a + 256],
                    op0=ALU.mult, op1=ALU.add), r=(t_sb,), w=(t_sb,))
        for icx in range(64):
            bi = icx % 2
            mmg(bi, pb[bi][:, 0:256], h128[:], t_sb[:, icx * 256:(icx + 1) * 256],
                start=True, stop=True, r=(h128, t_sb))
            ve(lambda eng, b=pb[bi], c=icx * 256:
               eng.tensor_copy(out=t_sb[:, c:c + 256], in_=b[:, 0:256]),
               r=(PB[bi],), w=(t_sb,))
        emit_quant(t_sb, 64, CH8192, k2)
        for tbb in range(2):
            tbix = ttile * 2 + tbb
            ctx.dma("sy", hT2[:, tbb * 2048:(tbb + 1) * 2048],
                    xres[tbix * 128:(tbix + 1) * 128, :], r=(xres,), w=(hT2,))
        for ocp in range(16):
            for half in range(2):
                wdb_ = wd_sb[(ocp * 2 + half) % 2]
                ctx.dma("sy", wdb_[:], wdt_b[ocp, :, half * 32:(half + 1) * 32, :],
                        w=(wdb_,))
                for ii in range(32):
                    icp = half * 32 + ii
                    mmg(4, pb[4][:, 0:256], wdb_[:, ii * 128:(ii + 1) * 128],
                        k2[:, icp * 256:(icp + 1) * 256],
                        start=(icp == 0), stop=(icp == 63), r=(wdb_, k2))
            ve(lambda eng: eng.scalar_tensor_tensor(
                out=vq[:], in0=pb[4][:, 0:256], scalar=scw[:, 2:3], in1=comb[:],
                op0=ALU.mult, op1=ALU.mult), r=(PB[4], scw, comb), w=(vq,))
            for tbb in range(2):
                tr(5, 0, vq[:, tbb * 128:(tbb + 1) * 128], r=(vq, ident))
                ve(lambda eng, tbb=tbb, c=ocp * 128: eng.tensor_tensor(
                    out=hT2[:, tbb * 2048 + c: tbb * 2048 + c + 128], in0=pb[5][:, 0:128],
                    in1=hT2[:, tbb * 2048 + c: tbb * 2048 + c + 128], op=ALU.add),
                   r=(PB[5], hT2), w=(hT2,))

        for tbb in range(2):
            tbix = ttile * 2 + tbb
            ctx.dma("se", xres[tbix * 128:(tbix + 1) * 128, :],
                    hT2[:, tbb * 2048:(tbb + 1) * 2048], r=(hT2,), w=(xres,))

    ctx.barrier()
    if ph <= 6:
        return _finish([es4b])
    es4b.close()

    es5a = ExitStack()

    def SB5a(name, shape, dt=F32):
        return es5a.enter_context(nc.sbuf_tensor(name, list(shape), dt))

    x_tok5 = SB5a("x_tok5", [128, D])
    h_tok5 = SB5a("h_tok5", [128, D])
    hT5 = SB5a("hT5", [128, 16 * 1024])
    wkv_sb = SB5a("wkv_sb", [128, 16 * 512])
    cs_cos = SB5a("cs_cos", [128, 8 * 256])
    cs_sin = SB5a("cs_sin", [128, 8 * 256])
    ro_a = SB5a("ro_a", [128, 256])
    ro_b = SB5a("ro_b", [128, 256])
    ro_out = SB5a("ro_out", [128, 512])
    qT_stage = SB5a("qT_stage", [128, 128])

    for tb in range(8):
        ctx.dma("sy", cs_cos[:, tb * 256:(tb + 1) * 256], c_cos[tb * 128:(tb + 1) * 128, :],
                w=(cs_cos,))
        ctx.dma("sy", cs_sin[:, tb * 256:(tb + 1) * 256], c_sin[tb * 128:(tb + 1) * 128, :],
                w=(cs_sin,))

    for tb in range(8):
        ctx.dma("sy", x_tok5[:], xres[tb * 128:(tb + 1) * 128, :], r=(xres,), w=(x_tok5,))
        emit_ln(h_tok5, x_tok5)
        for ic in range(16):
            bi = ic % 2
            tr(bi, 0, h_tok5[:, ic * 128:(ic + 1) * 128], r=(h_tok5, ident))
            ve(lambda eng, b=pb[bi][:, 0:128], c=ic * 1024 + tb * 128:
               eng.tensor_copy(out=hT5[:, c:c + 128], in_=b), r=(PB[bi],), w=(hT5,))

    def emit_rope(bank, tb, out_buf):
        c0 = tb * 256
        ve(lambda eng, b=pb[bank]: eng.tensor_tensor(
            out=ro_a[:], in0=b[:, 0::2], in1=cs_cos[:, c0:c0 + 256], op=ALU.mult),
            r=(PB[bank], cs_cos), w=(ro_a,))
        ve(lambda eng, b=pb[bank]: eng.tensor_tensor(
            out=ro_b[:], in0=b[:, 1::2], in1=cs_sin[:, c0:c0 + 256], op=ALU.mult),
            r=(PB[bank], cs_sin), w=(ro_b,))
        ve(lambda eng: eng.tensor_tensor(
            out=out_buf[:, 0::2], in0=ro_a[:], in1=ro_b[:], op=ALU.subtract),
            r=(ro_a, ro_b), w=(out_buf,))
        ve(lambda eng, b=pb[bank]: eng.tensor_tensor(
            out=ro_a[:], in0=b[:, 0::2], in1=cs_sin[:, c0:c0 + 256], op=ALU.mult),
            r=(PB[bank], cs_sin), w=(ro_a,))
        ve(lambda eng, b=pb[bank]: eng.tensor_tensor(
            out=ro_b[:], in0=b[:, 1::2], in1=cs_cos[:, c0:c0 + 256], op=ALU.mult),
            r=(PB[bank], cs_cos), w=(ro_b,))
        return ve(lambda eng: eng.tensor_tensor(
            out=out_buf[:, 1::2], in0=ro_a[:], in1=ro_b[:], op=ALU.add),
            r=(ro_a, ro_b), w=(out_buf,))

    for ic in range(16):
        ctx.dma("sy", wkv_sb[:, ic * 512:(ic + 1) * 512], wk1[ic * 128:(ic + 1) * 128, :],
                w=(wkv_sb,))
    for tb in range(8):
        for ic in range(16):
            mmg(2, pb[2][:], hT5[:, ic * 1024 + tb * 128: ic * 1024 + tb * 128 + 128],
                wkv_sb[:, ic * 512:(ic + 1) * 512], start=(ic == 0), stop=(ic == 15),
                r=(hT5, wkv_sb))
        emit_rope(2, tb, ro_out)
        ctx.dma("se", kv_own[tb * 128:(tb + 1) * 128, 0:512], ro_out[:], r=(ro_out,),
                w=(kv_own,))
    for ic in range(16):
        ctx.dma("sy", wkv_sb[:, ic * 512:(ic + 1) * 512], wv1[ic * 128:(ic + 1) * 128, :],
                w=(wkv_sb,))
    for tb in range(8):
        for ic in range(16):
            mmg(2, pb[2][:], hT5[:, ic * 1024 + tb * 128: ic * 1024 + tb * 128 + 128],
                wkv_sb[:, ic * 512:(ic + 1) * 512], start=(ic == 0), stop=(ic == 15),
                r=(hT5, wkv_sb))
        ve(lambda eng: eng.tensor_copy(out=ro_out[:], in_=pb[2][:]), r=(PB[2],),
           w=(ro_out,))
        ctx.dma("se", kv_own[tb * 128:(tb + 1) * 128, 512:1024], ro_out[:], r=(ro_out,),
                w=(kv_own,))
    ge(lambda eng: eng.collective_compute(
        "AllGather", ALU.bypass, replica_groups=[[0, 1, 2, 3], [4, 5, 6, 7]],
        ins=[kv_own.opt()], outs=[kv_full.opt()]), r=(kv_own,), w=(kv_full,))

    for og in range(4):
        for ic in range(16):
            ctx.dma("sy", wkv_sb[:, ic * 512:(ic + 1) * 512],
                    wq1[ic * 128:(ic + 1) * 128, og * 512:(og + 1) * 512],
                    w=(wkv_sb,))
        for tb in range(8):
            for ic in range(16):
                mmg(2, pb[2][:], hT5[:, ic * 1024 + tb * 128: ic * 1024 + tb * 128 + 128],
                    wkv_sb[:, ic * 512:(ic + 1) * 512], start=(ic == 0), stop=(ic == 15),
                    r=(hT5, wkv_sb))
            emit_rope(2, tb, ro_out)
            for j in range(4):
                tr(3, 0, ro_out[:, j * 128:(j + 1) * 128], r=(ro_out, ident))
                ve(lambda eng: eng.tensor_copy(out=qT_stage[:], in_=pb[3][:, 0:128]),
                   r=(PB[3],), w=(qT_stage,))
                ctx.dma("se", qt1[og * 4 + j, tb], qT_stage[:], r=(qT_stage,),
                        w=(qt1,))

    ctx.barrier()
    if ph <= 7:
        return _finish([es5a])
    es5a.close()

    es5b = ExitStack()

    def SB5b(name, shape, dt=F32):
        return es5b.enter_context(nc.sbuf_tensor(name, list(shape), dt))

    kT_sb = SB5b("kT_sb", [128, 4 * 4096])
    v_sb = SB5b("v_sb", [128, 32 * 512])
    q_sb5 = SB5b("q_sb5", [128, 1024])
    s_a = SB5b("s_a", [128, 4096])
    s_b = SB5b("s_b", [128, 4096])
    p5_sb = SB5b("p5_sb", [128, 512])
    pT5 = SB5b("pT5", [128, 256])
    o_sb5 = SB5b("o_sb5", [128, 128])
    bc_sb5 = SB5b("bc_sb5", [128, 128])
    k_tile = SB5b("k_tile", [128, 512])

    kvr = kv_full.rearrange("(r u s) f -> u r s f", r=4, s=32)
    for u in range(32):
        ctx.dma("ge", k_tile[:], kvr[u, :, :, 0:512], r=(kv_full,), w=(k_tile,))
        ctx.dma("ge", v_sb[:, u * 512:(u + 1) * 512], kvr[u, :, :, 512:1024],
                r=(kv_full,), w=(v_sb,))
        for fc in range(4):
            bi = fc % 2
            tr(bi, 0, k_tile[:, fc * 128:(fc + 1) * 128], r=(k_tile, ident))
            ve(lambda eng, b=pb[bi][:, 0:128], c=fc * 4096 + u * 128:
               eng.tensor_copy(out=kT_sb[:, c:c + 128], in_=b), r=(PB[bi],),
               w=(kT_sb,))

    pairs = [h for a in range(4) for h in (8 * a, 8 * a + 1, 8 * a + 2, 8 * a + 3)]
    for h in pairs:
        j = h // 4
        fc = j // 2
        oc_a, sub_a = h // 2, 64 * (h % 2)
        hb = h + 4
        oc_b, sub_b = hb // 2, 64 * (hb % 2)
        for tb in range(8):
            ctx.dma("se", q_sb5[0:64, tb * 128:(tb + 1) * 128],
                    qt1[oc_a, tb, sub_a:sub_a + 64, :], r=(qt1,), w=(q_sb5,))
            ctx.dma("se", q_sb5[64:128, tb * 128:(tb + 1) * 128],
                    qt1[oc_b, tb, sub_b:sub_b + 64, :], r=(qt1,),
                    w=(q_sb5,))
        for qt in range(8):
            nch = qt + 1
            for u5 in range(nch):
                mmg(4, pb[4][:], q_sb5[0:64, qt * 128:(qt + 1) * 128],
                    kT_sb[0:64, fc * 4096 + u5 * 512: fc * 4096 + u5 * 512 + 512],
                    start=True, stop=True, r=(q_sb5, kT_sb))
                mmg(5, pb[5][:], q_sb5[64:128, qt * 128:(qt + 1) * 128],
                    kT_sb[64:128, fc * 4096 + u5 * 512: fc * 4096 + u5 * 512 + 512],
                    start=True, stop=True, r=(q_sb5, kT_sb))
                if u5 == qt:
                    ve(lambda eng, c=u5 * 512: eng.tensor_tensor(
                        out=s_a[:, c:c + 512], in0=pb[4][:], in1=maskt[:], op=ALU.add),
                        r=(PB[4], maskt), w=(s_a,))
                    ve(lambda eng, c=u5 * 512: eng.tensor_tensor(
                        out=s_b[:, c:c + 512], in0=pb[5][:], in1=maskt[:], op=ALU.add),
                        r=(PB[5], maskt), w=(s_b,))
                else:
                    se(lambda eng, c=u5 * 512: eng.copy(out=s_a[:, c:c + 512],
                                                          in_=pb[4][:]),
                       r=(PB[4],), w=(s_a,))
                    se(lambda eng, c=u5 * 512: eng.copy(out=s_b[:, c:c + 512],
                                                          in_=pb[5][:]),
                       r=(PB[5],), w=(s_b,))
            for s_x, trb, avb, vj, oc_x, sub_x in ((s_a, 2, 0, j, oc_a, sub_a),
                                                   (s_b, 3, 1, j + 1, oc_b, sub_b)):
                ve(lambda eng: eng.memset(mx_t[:], -1e30), w=(mx_t,))
                ve(lambda eng: eng.memset(acc_t[:], 0.0), w=(acc_t,))
                for u5 in range(nch):
                    ve(lambda eng, c=u5 * 512, k=u5, s_x=s_x: eng.reduce_max(
                        out=mx_t[:, k:k + 1], in_=s_x[:, c:c + 512], axis=AX.X),
                        r=(s_x,), w=(mx_t,))
                ve(lambda eng, n=nch: eng.reduce_max(out=st_c[:], in_=mx_t[:, 0:n],
                                                     axis=AX.X), r=(mx_t,), w=(st_c,))
                ve(lambda eng: eng.tensor_scalar(
                    out=bias_t[:], in0=st_c[:], scalar1=float(np.float32(-0.125)), scalar2=None,
                    op0=ALU.mult), r=(st_c,), w=(bias_t,))
                for u5 in range(nch):
                    se(lambda eng, c=u5 * 512, k=u5, s_x=s_x: eng.activation(
                        out=p5_sb[:], in_=s_x[:, c:c + 512], func=AF.Exp, bias=bias_t[:],
                        scale=float(np.float32(0.125)), accum_out=acc_t[:, k:k + 1]),
                        r=(s_x, bias_t), w=(p5_sb, acc_t))
                    for jj in range(4):
                        u = u5 * 4 + jj
                        off = (jj % 2) * 128
                        tr(trb, off, p5_sb[:, jj * 128:(jj + 1) * 128], r=(p5_sb, ident))
                        ve(lambda eng, o=off, trb=trb: eng.tensor_copy(
                            out=pT5[:, o:o + 128], in_=pb[trb][:, o:o + 128]),
                            r=(PB[trb],), w=(pT5,))
                        mmg(avb, pb[avb][sub_x:sub_x + 64, 0:128],
                            v_sb[:, u * 512 + 64 * vj: u * 512 + 64 * vj + 64],
                            pT5[:, off:off + 128], start=(u == 0), stop=(u == 4 * nch - 1),
                            r=(v_sb, pT5))
                ve(lambda eng, n=nch: eng.reduce_sum(out=st_c[:], in_=acc_t[:, 0:n],
                                                     axis=AX.X), r=(acc_t,), w=(st_c,))
                ve(lambda eng: eng.reciprocal(out=st_d[:], in_=st_c[:]), r=(st_c,),
                   w=(st_d,))
                tr(6, 0, st_d[:], r=(st_d, ident), rows=1)
                ve(lambda eng: eng.tensor_copy(out=rowt[:], in_=pb[6][:1, 0:128]),
                   r=(PB[6],), w=(rowt,))
                mmg(6, pb[6][:, 128:256], ones_r[:], rowt[:], start=True, stop=True,
                    r=(ones_r, rowt))
                ve(lambda eng: eng.tensor_copy(out=bc_sb5[:], in_=pb[6][:, 128:256]),
                   r=(PB[6],), w=(bc_sb5,))
                ve(lambda eng, avb=avb, sub_x=sub_x: eng.tensor_tensor(
                    out=o_sb5[sub_x:sub_x + 64, :], in0=pb[avb][sub_x:sub_x + 64, 0:128],
                    in1=bc_sb5[sub_x:sub_x + 64, :], op=ALU.mult),
                    r=(PB[avb], bc_sb5), w=(o_sb5,))
                ctx.dma("se", o1b[oc_x, qt, sub_x:sub_x + 64, :], o_sb5[sub_x:sub_x + 64, :],
                        r=(o_sb5,), w=(o1b,))

    ctx.barrier()
    if ph <= 8:
        return _finish([es5b])
    es5b.close()

    es5c = ExitStack()

    def SB5c(name, shape, dt=F32):
        return es5c.enter_context(nc.sbuf_tensor(name, list(shape), dt))

    wo_sb = SB5c("wo_sb", [128, 16 * 512])
    xr_sb5 = SB5c("xr_sb5", [128, 512])
    oT_buf5 = SB5c("oT_buf5", [128, 2048])
    wo_out5 = SB5c("wo_out5", [128, 512])

    for og in range(4):
        for ic in range(16):
            ctx.dma("sy", wo_sb[:, ic * 512:(ic + 1) * 512],
                    wo1[ic * 128:(ic + 1) * 128, og * 512:(og + 1) * 512],
                    w=(wo_sb,))
        for tb in range(8):
            for ic in range(16):
                ctx.dma("se", oT_buf5[:, ic * 128:(ic + 1) * 128], o1b[ic, tb],
                        r=(o1b,), w=(oT_buf5,))
            bi = 2 + (tb % 2)
            for ic in range(16):
                mmg(bi, pb[bi][:], oT_buf5[:, ic * 128:(ic + 1) * 128],
                    wo_sb[:, ic * 512:(ic + 1) * 512], start=(ic == 0), stop=(ic == 15),
                    r=(oT_buf5, wo_sb))
            ctx.dma("se", xr_sb5[:], xres[tb * 128:(tb + 1) * 128, og * 512:(og + 1) * 512],
                    r=(xres,), w=(xr_sb5,))
            ve(lambda eng, b=pb[bi]: eng.tensor_tensor(out=wo_out5[:], in0=b[:],
                                                       in1=xr_sb5[:], op=ALU.add),
               r=(PB[bi], xr_sb5), w=(wo_out5,))
            ctx.dma("se", xres[tb * 128:(tb + 1) * 128, og * 512:(og + 1) * 512],
                    wo_out5[:], r=(wo_out5,), w=(xres,))

    ctx.barrier()
    if ph <= 9:
        return _finish([es5c])
    es5c.close()

    es6 = ExitStack()

    def SB6(name, shape, dt=F32):
        return es6.enter_context(nc.sbuf_tensor(name, list(shape), dt))

    x_tok6 = SB6("x_tok6", [128, D])
    h_tok6 = SB6("h_tok6", [128, D])
    x3hi = SB6("x3hi", [128, 16 * 256], BF16)
    x3lo = SB6("x3lo", [128, 16 * 256], BF16)
    fhi = SB6("fhi", [128, 64 * 256], BF16)
    flo = SB6("flo", [128, 64 * 256], BF16)
    wgb = [SB6(f"wgb{i}", [128, 16 * 512], BF16) for i in range(2)]
    wub = [SB6(f"wub{i}", [128, 16 * 512], BF16) for i in range(2)]
    w8buf = [SB6(f"w8buf{i}", [128, 512], I8) for i in range(2)]
    wdb = [SB6(f"wdb{i}", [128, 512], BF16) for i in range(2)]
    sdb = SB6("sdb", [128, 2048])
    sd_row = SB6("sd_row", [1, 512])
    sg_sb = SB6("sg_sb", [128, 64])
    su_sb = SB6("su_sb", [128, 64])
    vq6 = SB6("vq6", [128, 512])
    av6 = SB6("av6", [128, 512])
    rr6 = SB6("rr6", [128, 512])
    out6 = SB6("out6", [128, 512])

    ctx.dma("sy", sg_sb[:], sg1.rearrange("(o p) -> p o", p=128), w=(sg_sb,), slow=True)
    ctx.dma("sy", su_sb[:], su1.rearrange("(o p) -> p o", p=128), w=(su_sb,), slow=True)
    for jj in range(4):
        ctx.dma("sy", sd_row[:],
                sd1[jj * 512:(jj + 1) * 512].rearrange("(a d) -> a d", a=1), w=(sd_row,))
        mmg(6, pb[6][:], ones_r[:], sd_row[:], start=True, stop=True, r=(ones_r, sd_row))
        ve(lambda eng, c=jj * 512: eng.tensor_copy(out=sdb[:, c:c + 512], in_=pb[6][:]),
           r=(PB[6],), w=(sdb,))

    for q4 in range(4):
        for tbb in range(2):
            tbix = q4 * 2 + tbb
            ctx.dma("sy", x_tok6[:], xres[tbix * 128:(tbix + 1) * 128, :],
                    r=(xres,), w=(x_tok6,))
            emit_ln(h_tok6, x_tok6)
            for ic in range(16):
                bi = ic % 2
                tr(bi, 0, h_tok6[:, ic * 128:(ic + 1) * 128], r=(h_tok6, ident))
                c = ic * 256 + tbb * 128
                ve(lambda eng, b=pb[bi][:, 0:128], c=c:
                   eng.tensor_copy(out=x3hi[:, c:c + 128], in_=b), r=(PB[bi],), w=(x3hi,))
                ve(lambda eng, b=pb[bi][:, 0:128], c=c:
                   eng.tensor_tensor(out=x3lo[:, c:c + 128], in0=b, in1=x3hi[:, c:c + 128],
                                     op=ALU.subtract), r=(PB[bi], x3hi), w=(x3lo,))
        for og in range(16):
            wgb_ = wgb[og % 2]
            wub_ = wub[og % 2]
            for ic in range(16):
                bi8 = ic % 2
                ctx.dma("sy", w8buf[bi8][:],
                        wg1[ic * 128:(ic + 1) * 128, og * 512:(og + 1) * 512],
                        w=(w8buf[bi8],))
                ve(lambda eng, b=w8buf[bi8], c=ic * 512, d=wgb_:
                   eng.tensor_copy(out=d[:, c:c + 512], in_=b[:]),
                   r=(w8buf[bi8],), w=(wgb_,))
                ctx.dma("sy", w8buf[bi8][:],
                        wu1[ic * 128:(ic + 1) * 128, og * 512:(og + 1) * 512],
                        w=(w8buf[bi8],))
                ve(lambda eng, b=w8buf[bi8], c=ic * 512, d=wub_:
                   eng.tensor_copy(out=d[:, c:c + 512], in_=b[:]),
                   r=(w8buf[bi8],), w=(wub_,))
            for oc4 in range(4):
                oc = og * 4 + oc4
                for ic in range(16):
                    for hl, xsrc in ((0, x3hi), (1, x3lo)):
                        mmg(2, pb[2][:, 0:256],
                            wgb_[:, ic * 512 + oc4 * 128: ic * 512 + oc4 * 128 + 128],
                            xsrc[:, ic * 256:(ic + 1) * 256],
                            start=(ic == 0 and hl == 0), stop=(ic == 15 and hl == 1),
                            r=(wgb_, xsrc))
                        mmg(3, pb[3][:, 0:256],
                            wub_[:, ic * 512 + oc4 * 128: ic * 512 + oc4 * 128 + 128],
                            xsrc[:, ic * 256:(ic + 1) * 256],
                            start=(ic == 0 and hl == 0), stop=(ic == 15 and hl == 1),
                            r=(wub_, xsrc))
                ve(lambda eng, oc=oc: eng.tensor_scalar(
                    out=vq6[:, 0:256], in0=pb[2][:, 0:256], scalar1=sg_sb[:, oc:oc + 1],
                    scalar2=None, op0=ALU.mult), r=(PB[2], sg_sb), w=(vq6,))
                se(lambda eng: eng.activation(out=av6[:, 0:256], in_=vq6[:, 0:256],
                                              func=AF.Silu), r=(vq6,), w=(av6,))
                ve(lambda eng, oc=oc: eng.tensor_scalar(
                    out=rr6[:, 0:256], in0=pb[3][:, 0:256], scalar1=su_sb[:, oc:oc + 1],
                    scalar2=None, op0=ALU.mult), r=(PB[3], su_sb), w=(rr6,))
                ve(lambda eng: eng.tensor_tensor(out=vq6[:, 256:512], in0=av6[:, 0:256],
                                                 in1=rr6[:, 0:256], op=ALU.mult),
                   r=(av6, rr6), w=(vq6,))
                ve(lambda eng, c=oc * 256: eng.tensor_copy(out=fhi[:, c:c + 256],
                                                           in_=vq6[:, 256:512]),
                   r=(vq6,), w=(fhi,))
                ve(lambda eng, c=oc * 256: eng.tensor_tensor(
                    out=flo[:, c:c + 256], in0=vq6[:, 256:512], in1=fhi[:, c:c + 256],
                    op=ALU.subtract), r=(vq6, fhi), w=(flo,))
        for ogd in range(4):
            for icp in range(64):
                bi8 = icp % 2
                ctx.dma("sy", w8buf[bi8][:],
                        wd1[icp * 128:(icp + 1) * 128, ogd * 512:(ogd + 1) * 512],
                        w=(w8buf[bi8],))
                ve(lambda eng, b=w8buf[bi8], d=wdb[bi8]: eng.tensor_copy(out=d[:], in_=b[:]),
                   r=(w8buf[bi8],), w=(wdb[bi8],))
                for tb in range(2):
                    for hl, fsrc in ((0, fhi), (1, flo)):
                        mmg(4 + tb, pb[4 + tb][:],
                            fsrc[:, icp * 256 + tb * 128: icp * 256 + tb * 128 + 128],
                            wdb[bi8][:], start=(icp == 0 and hl == 0),
                            stop=(icp == 63 and hl == 1), r=(fsrc, wdb[bi8]))
            for tb in range(2):
                tbix = q4 * 2 + tb
                ve(lambda eng, b=pb[4 + tb], c=ogd * 512:
                   eng.tensor_tensor(out=vq6[:], in0=b[:], in1=sdb[:, c:c + 512],
                                     op=ALU.mult), r=(PB[4 + tb], sdb), w=(vq6,))
                ctx.dma("sy", av6[:], xres[tbix * 128:(tbix + 1) * 128,
                                           ogd * 512:(ogd + 1) * 512],
                        r=(xres,), w=(av6,))
                ve(lambda eng: eng.tensor_tensor(out=out6[:], in0=vq6[:], in1=av6[:],
                                                 op=ALU.add), r=(vq6, av6), w=(out6,))
                ctx.dma("se", y_out[tbix * 128:(tbix + 1) * 128,
                                    ogd * 512:(ogd + 1) * 512], out6[:], r=(out6,))

    for e in ENGS:
        for sname in ctx.val:
            ctx.wait(e, sname, ctx.val[sname])

    es6.close()
    with nc.Block() as block:
        ctx.realize(block)
    esg.close()
    return nc


_PROG_CACHE = {}


def _get_prog():
    import os
    ph = int(os.environ.get("KPHASE", "99"))
    key = ("nc", ph, os.environ.get("KN_HEADS"), os.environ.get("KN_QB"),
           os.environ.get("KN_PART"))
    if key not in _PROG_CACHE:
        _PROG_CACHE[key] = build_program(ph)
    return _PROG_CACHE[key]


def kernel(L0_attn_x, L0_attn_W_q, L0_attn_W_down_kv, L0_attn_W_up_k, L0_attn_W_up_v,
           L0_attn_W_o, L0_ffn_w_gate, L0_ffn_w_up, L0_ffn_w_down,
           L1_attn_wq, L1_attn_wk, L1_attn_wv, L1_attn_wo,
           L1_ffn_w_gate, L1_ffn_w_up, L1_ffn_w_down,
           L1_ffn_s_gate, L1_ffn_s_up, L1_ffn_s_down):
    import ml_dtypes
    nc = _get_prog()
    x = np.ascontiguousarray(np.asarray(L0_attn_x, dtype=np.float32))
    shared = {
        "W_q": np.ascontiguousarray(np.asarray(L0_attn_W_q, dtype=np.float32)),
        "W_dkv": np.ascontiguousarray(np.asarray(L0_attn_W_down_kv, dtype=np.float32)),
        "W_upk": np.ascontiguousarray(np.asarray(L0_attn_W_up_k, dtype=np.float32)),
        "W_upv": np.ascontiguousarray(np.asarray(L0_attn_W_up_v, dtype=np.float32)),
        "W_o": np.ascontiguousarray(np.asarray(L0_attn_W_o, dtype=np.float32)),
        "Wg0": np.ascontiguousarray(np.asarray(L0_ffn_w_gate, dtype=np.float32)),
        "Wu0": np.ascontiguousarray(np.asarray(L0_ffn_w_up, dtype=np.float32)),
        "Wd0": np.ascontiguousarray(np.asarray(L0_ffn_w_down, dtype=np.float32)),
        "wq1": np.ascontiguousarray(np.asarray(L1_attn_wq, dtype=np.float32)),
        "wk1": np.ascontiguousarray(np.asarray(L1_attn_wk, dtype=np.float32)),
        "wv1": np.ascontiguousarray(np.asarray(L1_attn_wv, dtype=np.float32)),
        "wo1": np.ascontiguousarray(np.asarray(L1_attn_wo, dtype=np.float32)),
        "wg1": np.ascontiguousarray(np.asarray(L1_ffn_w_gate, dtype=np.int8)),
        "wu1": np.ascontiguousarray(np.asarray(L1_ffn_w_up, dtype=np.int8)),
        "wd1": np.ascontiguousarray(np.asarray(L1_ffn_w_down, dtype=np.int8)),
        "sg1": np.ascontiguousarray(np.asarray(L1_ffn_s_gate, dtype=np.float32)),
        "su1": np.ascontiguousarray(np.asarray(L1_ffn_s_up, dtype=np.float32)),
        "sd1": np.ascontiguousarray(np.asarray(L1_ffn_s_down, dtype=np.float32)),
        "c_ident": np.eye(128, dtype=np.float32),
        "c_identb": np.eye(128, dtype=ml_dtypes.bfloat16),
        "c_h128": _host_hadamard(),
        "c_ones": np.ones((1, 128), dtype=np.float32),
    }
    in_maps = []
    for c in range(8):
        b, cp = c // 4, c % 4
        cos8, sin8 = _host_rope(c)
        m = dict(shared)
        m["x_own"] = np.ascontiguousarray(x[b, cp::4, :])
        m["c_mask"] = _host_mask(c)
        m["c_cos"] = cos8
        m["c_sin"] = sin8
        in_maps.append(m)
    res = run_bass_kernel_spmd(nc, in_maps, core_ids=list(range(8)))
    out = np.empty((B, S, D), dtype=np.float32)
    for c in range(8):
        b, cp = c // 4, c % 4
        out[b, cp::4, :] = res.results[c]["y_out"]
    return out
